# version 04
# baseline (speedup 1.0000x reference)
import numpy as np
import ml_dtypes
from contextlib import ExitStack

import concourse.bass as bass
import concourse.mybir as mybir
from concourse.bass_utils import run_bass_kernel_spmd

F32 = mybir.dt.float32
BF16 = mybir.dt.bfloat16
AF = mybir.ActivationFunctionType
ALU = mybir.AluOpType
AX = mybir.AxisListType
BF = ml_dtypes.bfloat16

D = 1024
S = 2048
BATCH = 16
NCORE = 8
SPC = BATCH // NCORE
KC = D // 128
NEG_BIG = -1e30

E0 = 64; INTER0 = 512; IC0 = INTER0 // 128
ISH = 1024
E1 = 8
HEADS = 16; DH = 64; NHP = HEADS // 2
TOK = 512
HALF = 1024
NT = HALF // TOK
NTC = HALF // 128

ENGS = ("tensor", "vector", "scalar", "gpsimd", "sync")



def _c(a):
    return np.ascontiguousarray(a)


def prep_host(inp):
    h = {}
    h["sgT"] = _c(inp["L0_ffn_sg_w"].astype(np.float32).T)
    h["suT"] = _c(inp["L0_ffn_su_w"].astype(np.float32).T)
    h["sdT"] = _c(inp["L0_ffn_sd_w"].astype(np.float32).T)
    h["gateT"] = _c(inp["L0_ffn_gate_w"].astype(np.float32).T)
    h["gbias"] = _c(inp["L0_ffn_gate_bias"].astype(np.float32).reshape(E0, 1))
    h["egT"] = _c(inp["L0_ffn_eg_w"].astype(np.float32).transpose(0, 2, 1))
    h["euT"] = _c(inp["L0_ffn_eu_w"].astype(np.float32).transpose(0, 2, 1))
    h["edT"] = _c(inp["L0_ffn_ed_w"].astype(np.float32).transpose(0, 2, 1))
    h["gamma"] = _c(inp["L0_attn_gamma"].astype(np.float32).reshape(KC, 128).T)
    wq = inp["L0_attn_w_qkv"].astype(np.float32).reshape(8, 128, 3, NHP, 128)
    h["wqkv0"] = _c(wq.transpose(3, 0, 1, 2, 4))
    h["wo0"] = _c(inp["L0_attn_w_out"].astype(np.float32))
    WT = {}
    for n in ["q", "k", "v", "o"]:
        WT[n] = _c(inp["L1_attn_W" + n].astype(np.float32).T)
        h["A%sT" % n] = _c(inp["L1_attn_A" + n].astype(np.float32).T)
        h["B%sTr" % n] = _c(inp["L1_attn_B" + n].astype(np.float32).T)
    h["WoT"] = WT["o"]
    wq1 = np.stack([WT["q"], WT["k"], WT["v"]], 0).reshape(3, 8, 128, NHP, 128)
    h["wqkv1"] = _c(wq1.transpose(3, 1, 2, 0, 4))
    aq1 = np.stack([h["AqT"], h["AkT"], h["AvT"]], 0)
    apad = np.zeros((D, 96), dtype=np.float32)
    for p in range(3):
        apad[:, p * 32:p * 32 + 8] = aq1[p]
    h["aqkv1"] = _c(apad.reshape(8, 128, 96))
    bq1 = np.stack([h["BqTr"], h["BkTr"], h["BvTr"]], 0).reshape(3, 8, NHP, 128)
    h["bqkv1"] = _c(bq1.transpose(2, 1, 0, 3))
    bb = np.stack([inp["L1_attn_bq"], inp["L1_attn_bk"], inp["L1_attn_bv"]], 0)
    h["biasqkv1"] = _c(bb.astype(np.float32).reshape(3, 8, 128).transpose(1, 2, 0))
    h["bo1"] = _c(inp["L1_attn_bo"].astype(np.float32).reshape(KC, 128).T)
    h["rz"] = _c(np.broadcast_to(
        inp["L1_attn_rezero"].astype(np.float32).reshape(1, 1), (128, 1)))
    h["routerT"] = _c(inp["L1_ffn_router_w"].astype(np.float32).T)
    h["expW"] = _c(inp["L1_ffn_expert_weights"].astype(np.float32))
    h["ident"] = np.eye(128, dtype=np.float32)
    h["identb"] = np.eye(128, dtype=BF)
    sel64 = np.zeros((E0, E0 * 128), dtype=BF)
    for e in range(E0):
        sel64[e, e * 128:(e + 1) * 128] = 1
    h["sel64"] = sel64
    sel8 = np.zeros((E1, E1 * 128), dtype=BF)
    for e in range(E1):
        sel8[e, e * 128:(e + 1) * 128] = 1
    h["sel8"] = sel8
    h["onerow_f"] = np.ones((1, 128), dtype=np.float32)
    o33 = np.zeros((33, 128), dtype=np.float32)
    o33[0] = 1.0; o33[32] = 1.0
    h["ones33"] = o33
    h["ones_f"] = np.ones((128, 128), dtype=np.float32)
    h["ones_b"] = np.ones((128, 128), dtype=BF)
    h["onerow_b"] = np.ones((1, 128), dtype=BF)
    slopes = 0.5 ** np.arange(HEADS, dtype=np.float32)
    jj = np.arange(S, dtype=np.float32)
    h["alibiB"] = _c(np.broadcast_to(
        (slopes[:, None] * jj[None, :])[:, None, :], (HEADS, 128, S)))
    pcol = np.arange(128, dtype=np.float32)
    ali_c = np.zeros((128, HEADS, S // 128), dtype=np.float32)
    for hh in range(HEADS):
        for jc in range(S // 128):
            ali_c[:, hh, jc] = slopes[hh] * (jc * 128 + pcol)
    h["alibiC"] = ali_c
    x = inp["L0_attn_x"].astype(np.float32)
    percore = []
    for cid in range(NCORE):
        xs = x[cid * SPC:(cid + 1) * SPC]
        xt = _c(xs.transpose(0, 2, 1)).reshape(SPC, KC, 128, S)
        percore.append({"xT": xt})
    return h, percore



class Ctx:
    def __init__(self, nc, es):
        self.nc = nc
        self.es = es
        self.prog = {e: [] for e in ENGS}
        self.sems = {}
        self.counts = {}
        self.waited = {e: {} for e in ENGS}
        self.ready = {}
        self.freed = {}
        self.bufs = {}
        self.dram = {}
        self.const_cols = {}

    def sem(self, name):
        if name not in self.sems:
            self.sems[name] = self.es.enter_context(self.nc.semaphore(name))
            self.counts[name] = 0
        return self.sems[name]

    def inc(self, name, n=1):
        self.sem(name)
        self.counts[name] += n
        return self.counts[name]

    def emit(self, engine, fn):
        self.prog[engine].append(fn)

    def wait(self, engine, sem_name, val):
        if val is None or val <= 0:
            return
        sm = self.sem(sem_name)
        w = self.waited[engine]
        if w.get(sem_name, 0) >= val:
            return
        w[sem_name] = val
        self.prog[engine].append(lambda eng, s=sm, v=val: eng.wait_ge(s, v))

    def wait_dep(self, engine, dep):
        if dep is not None:
            self.wait(engine, dep[0], dep[1])

    def require(self, engine, res):
        self.wait_dep(engine, self.ready.get(res))

    def acquire(self, engine, res):
        deps = self.freed.get(res)
        if deps:
            for d in deps:
                self.wait_dep(engine, d)

    def produce(self, res, sem_name, val):
        self.ready[res] = (sem_name, val)

    def release(self, res, sem_name, val):
        lst = self.freed.setdefault(res, [])
        for i, (s, v) in enumerate(lst):
            if s == sem_name:
                lst[i] = (sem_name, max(v, val))
                break
        else:
            lst.append((sem_name, val))

    release_add = release

    def barrier(self, engines=("tensor", "vector", "scalar", "gpsimd")):
        bar = self.sem("bar")
        for snm, v in list(self.counts.items()):
            if snm.startswith("wb_") and v:
                self.wait("sync", snm, v)
        tgt = self.counts["bar"] + len(engines) + 1
        self.counts["bar"] = tgt
        dummy = self.bufs["dummy"]
        self.prog["sync"].append(lambda eng, s=bar: eng.nop().then_inc(s, 1))
        for e in engines:
            if e == "tensor":
                self.prog[e].append(lambda eng, s=bar: eng.nop().then_inc(s, 1))
            elif e == "vector":
                self.prog[e].append(lambda eng, s=bar: eng.engine_nop().then_inc(s, 1))
            elif e == "scalar":
                self.prog[e].append(lambda eng, s=bar, d=dummy: eng.memzero(d[0:1, 0:1]).then_inc(s, 1))
            elif e == "gpsimd":
                self.prog[e].append(lambda eng, s=bar, d=dummy: eng.memset(d[0:1, 1:2], 0.0).then_inc(s, 1))
        for e in engines:
            self.wait(e, "bar", tgt)
        self.wait("sync", "bar", tgt)
        for e in ENGS:
            w = self.waited[e]
            for s in ("pe", "dv", "ac", "gp"):
                if s in self.counts:
                    w[s] = max(w.get(s, 0), self.counts[s])
        return ("bar", tgt)


_NAME_CTR = [0]


def _uname(name):
    _NAME_CTR[0] += 1
    return "k%d_%s" % (_NAME_CTR[0], name)


def asb(c, name, shape, dtype, es=None):
    return (es or c.es).enter_context(
        c.nc.sbuf_tensor(_uname(name), list(shape), dtype))


def aps(c, name, shape, dtype=F32, es=None):
    return (es or c.es).enter_context(
        c.nc.psum_tensor(_uname(name), list(shape), dtype))


def t_mm(c, out_ap, lhsT_ap, rhs_ap, start, stop, inc=False):
    if inc:
        c.emit("tensor", lambda eng, o=out_ap, l=lhsT_ap, r=rhs_ap, a=start, b=stop,
               s=c.sem("pe"): eng.matmul(o, l, r, start=a, stop=b).then_inc(s, 1))
        return c.inc("pe")
    c.emit("tensor", lambda eng, o=out_ap, l=lhsT_ap, r=rhs_ap, a=start, b=stop:
           eng.matmul(o, l, r, start=a, stop=b))
    return None


def t_transpose(c, out_ap, in_ap, ident_ap):
    c.emit("tensor", lambda eng, o=out_ap, i=in_ap, d=ident_ap,
           s=c.sem("pe"): eng.transpose(o, i, d).then_inc(s, 1))
    return c.inc("pe")


def v_op(c, fn, inc=True):
    if inc:
        c.emit("vector", lambda eng, f=fn, s=c.sem("dv"): f(eng).then_inc(s, 1))
        return c.inc("dv")
    c.emit("vector", fn)
    return None


def s_op(c, fn, inc=True):
    if inc:
        c.emit("scalar", lambda eng, f=fn, s=c.sem("ac"): f(eng).then_inc(s, 1))
        return c.inc("ac")
    c.emit("scalar", fn)
    return None


def g_op(c, fn, inc=True):
    if inc:
        c.emit("gpsimd", lambda eng, f=fn, s=c.sem("gp"): f(eng).then_inc(s, 1))
        return c.inc("gp")
    c.emit("gpsimd", fn)
    return None


def wb_cache(c, fam, item, src_ap, cast_dep, frees=()):
    c.wait_dep("sync", cast_dep)
    sem = "wb_" + (frees[0] if frees else fam)
    wb = dma(c, "sync", c.wcache[fam][item], src_ap, sem)
    c.wc_written[fam][item] = (sem, wb)
    for r in frees:
        c.release_add(r, sem, wb)
    return wb


def dma(c, queue, out_ap, in_ap, sem_name):
    c.emit(queue, lambda eng, o=out_ap, i=in_ap, s=c.sem(sem_name):
           eng.dma_start(out=o, in_=i).then_inc(s, 16))
    return c.inc(sem_name, 16)



def emit_ln(c, xT, xn, tok0, ntok, sc):
    ones_f = c.bufs["ones_f"]
    p_sum, p_sq = sc["p0"], sc["p1"]
    statB, statB2, sq = sc["statB"], sc["statB2"], sc["sq"]
    ntile = ntok // TOK
    sq_mm = {}
    stats_read = None
    prev_apply = None
    for it in range(ntile):
        t0 = tok0 + it * TOK
        c.require("tensor", "xT")
        c.require("vector", "xT")
        if stats_read is not None:
            c.wait("tensor", "dv", stats_read)
        if prev_apply is not None:
            c.wait("vector", "dv", prev_apply)
        for kc in range(KC):
            par = kc % 2
            gi = it * KC + kc
            if gi >= 2:
                c.wait("vector", "pe", sq_mm[gi - 2])
            vq = v_op(c, lambda eng, o=sq[:, par], i=xT[:, kc, t0:t0 + TOK]:
                      eng.tensor_tensor(o, i, i, op=ALU.mult))
            c.wait("tensor", "dv", vq)
            t_mm(c, p_sum[:], ones_f[:], xT[:, kc, t0:t0 + TOK],
                 start=(kc == 0), stop=(kc == KC - 1))
            sq_mm[gi] = t_mm(c, p_sq[:], ones_f[:], sq[:, par],
                             start=(kc == 0), stop=(kc == KC - 1), inc=True)
        pe_done = c.counts["pe"]
        sB = statB[:, it % 2]
        sB2 = statB2[:, it % 2]
        c.wait("vector", "pe", pe_done)
        if prev_apply is not None:
            c.wait("scalar", "dv", prev_apply)
        v_op(c, lambda eng, o=sB2, i=p_sum[:]:
             eng.tensor_single_scalar(o, i, 1.0 / D, op=ALU.mult))
        v1 = v_op(c, lambda eng, o=sB, i=p_sq[:]:
                  eng.tensor_single_scalar(o, i, 1.0 / D, op=ALU.mult))
        stats_read = v1
        c.wait("scalar", "dv", v1)
        a1 = s_op(c, lambda eng, o=sq[:, 0], i=sB2:
                  eng.activation(out=o, in_=i, func=AF.Square))
        c.wait("vector", "ac", a1)
        v2 = v_op(c, lambda eng, o=sB, a=sB, b=sq[:, 0]: eng.tensor_sub(o, a, b))
        c.wait("scalar", "dv", v2)
        a2 = s_op(c, lambda eng, o=sB, i=sB, b=c.const_cols["eps_ln"]:
                  eng.activation(out=o, in_=i, func=AF.Sqrt, bias=b))
        c.wait("vector", "ac", a2)
        v3 = v_op(c, lambda eng, o=sB, i=sB: eng.reciprocal(o, i))
        c.wait("vector", "dv", v3)
        v4 = v_op(c, lambda eng, o=sB2, i=sB2, r=sB:
                  eng.scalar_tensor_tensor(o, i, -1.0, r, op0=ALU.mult, op1=ALU.mult))
        c.wait("vector", "dv", v4)
        sq_w = {}
        for kc in range(KC):
            par = kc % 2
            if kc >= 2:
                c.wait("vector", "dv", sq_w[kc - 2])
            vv = v_op(c, lambda eng, o=sq[:, par], x=xT[:, kc, t0:t0 + TOK], r=sB:
                      eng.tensor_mul(o, x, r))
            c.wait("vector", "dv", vv)
            sq_w[kc] = v_op(c, lambda eng, o=xn[:, kc, it * TOK:(it + 1) * TOK],
                            a=sq[:, par], b=sB2: eng.tensor_add(o, a, b))
        prev_apply = c.counts["dv"]
    c.produce("xn", "dv", c.counts["dv"])



DBG = {"on": False, "maps": []}


def dbg_dump(c, tag, ap, shape, dtype):
    if not DBG["on"] or (DBG.get("only") and not any(t in tag for t in DBG["only"])):
        return
    nm = "dbg_%s" % tag
    d = c.nc.dram_tensor(nm, list(shape), dtype, kind="ExternalOutput").ap()
    bt = c.barrier()
    c.wait_dep("sync", bt)
    ld = dma(c, "sync", d, ap, "dbgs")
    c.wait("sync", "dbgs", ld)
    for e in ("tensor", "vector", "scalar", "gpsimd"):
        c.wait(e, "dbgs", ld)
    DBG["maps"].append(nm)


def emit_ffn0(c, xT, half):
    tok0 = half * HALF
    with ExitStack() as es:
        xn = asb(c, "f0xn", [128, KC, HALF], BF16, es)[:]
        acc = asb(c, "f0acc", [128, KC, HALF], F32, es)[:]
        combT = asb(c, "f0cT", [E0, HALF], BF16, es)[:]
        sel_sb = asb(c, "f0sel", [E0, 2, 128], BF16, es)[:]
        gps = [aps(c, "f0g%d" % i, [128, TOK], F32, es)[:] for i in range(2)]
        ups = [aps(c, "f0u%d" % i, [128, TOK], F32, es)[:] for i in range(2)]
        p_ed = aps(c, "f0ed", [128, 3, TOK], F32, es)[:]
        p_cb = aps(c, "f0pcb", [128, TOK], F32, es)[:]

        with ExitStack() as bs:
            gate_f = asb(c, "f0gtf", [128, KC, E0], F32, bs)[:]
            gate_b = asb(c, "f0gtb", [128, KC, E0], BF16, bs)[:]
            statB = asb(c, "f0statB", [128, 2, TOK], F32, bs)[:]
            statB2 = asb(c, "f0statB2", [128, 2, TOK], F32, bs)[:]
            sq = asb(c, "f0sq", [128, 2, TOK], F32, bs)[:]
            scoresT = asb(c, "f0scT", [E0, HALF], F32, bs)[:]
            scoresT2 = asb(c, "f0scT2", [E0, HALF], F32, bs)[:]
            stok = asb(c, "f0stok", [128, NTC, E0], F32, bs)[:]
            r8 = asb(c, "f0r8", [128, 4, 8], F32, bs)[:]
            r64 = asb(c, "f0r64", [128, 2, E0], F32, bs)[:]
            r1 = asb(c, "f0r1", [128, 4], F32, bs)[:]
            ctok = asb(c, "f0ctok", [128, E0], F32, bs)[:]

            ldg = dma(c, "sync", gate_f[:], c.dram["gateT"].rearrange(
                "(kc kp) e -> kp kc e", kp=128), "ldw_gate")
            c.wait("scalar", "ldw_gate", ldg)
            a_g = s_op(c, lambda eng, o=gate_b[:], i=gate_f[:]:
                       eng.activation(out=o, in_=i, func=AF.Copy))

            ln_sc = {"p0": gps[0], "p1": gps[1], "statB": statB,
                     "statB2": statB2, "sq": sq}
            emit_ln(c, xT, xn, tok0, HALF, ln_sc)
            dbg_dump(c, "xn_h%d" % half, xn, [128, KC, HALF], BF16)

            c.require("tensor", "xn")
            c.wait("tensor", "ac", a_g)
            lg_done = []
            for it in range(NT):
                last = None
                if lg_done:
                    c.wait("tensor", "ac", lg_done[-1])
                for kc in range(KC):
                    last = t_mm(c, p_cb[0:E0, :], gate_b[:, kc],
                                xn[:, kc, it * TOK:(it + 1) * TOK],
                                start=(kc == 0), stop=(kc == KC - 1),
                                inc=(kc == KC - 1))
                c.wait("scalar", "pe", last)
                a1 = s_op(c, lambda eng, o=scoresT[:, it * TOK:(it + 1) * TOK],
                          i=p_cb[0:E0, :]: eng.activation(out=o, in_=i, func=AF.Sigmoid))
                lg_done.append(a1)
            c.wait("vector", "ac", lg_done[-1])
            gb = c.bufs["gbias"]
            vsc = v_op(c, lambda eng, o=scoresT2[:], i=scoresT[:], s=gb[:]:
                       eng.tensor_scalar(o, i, s, None, op0=ALU.add))
            ident = c.bufs["ident"]
            c.wait("tensor", "dv", vsc)
            tr_done = {}
            for tc in range(NTC):
                if tc >= 2:
                    c.wait("tensor", "ac", tr_done[tc - 2])
                pe1 = t_transpose(c, p_ed[:, tc % 2, 0:E0],
                                  scoresT2[:, tc * 128:(tc + 1) * 128],
                                  ident[0:E0, 0:E0])
                c.wait("scalar", "pe", pe1)
                tr_done[tc] = s_op(c, lambda eng, o=stok[:, tc], i=p_ed[:, tc % 2, 0:E0]:
                                   eng.activation(out=o, in_=i, func=AF.Copy))
            last_cT = None
            last_pe2 = None
            for tc in range(NTC):
                c.wait("vector", "ac", tr_done[tc])
                st_ap = stok[:, tc]
                v1 = v_op(c, lambda eng, o=r8[:, 0],
                          i=st_ap.rearrange("p (g j) -> p g j", g=8):
                          eng.reduce_max(o, i, axis=AX.X))
                c.wait("vector", "dv", v1)
                v2 = v_op(c, lambda eng, o=r8[:, 1], i=r8[:, 0]: eng.max(out=o, in_=i))
                c.wait("vector", "dv", v2)
                v3 = v_op(c, lambda eng, o=r8[:, 2], i=r8[:, 0], s=r8[:, 1, 3:4]:
                          eng.tensor_scalar(o, i, s, None, op0=ALU.is_ge))
                c.wait("vector", "dv", v3)
                for g in range(8):
                    v_op(c, lambda eng, o=r64[:, 0, g * 8:(g + 1) * 8],
                         i=st_ap[:, g * 8:(g + 1) * 8], s=r8[:, 2, g:g + 1]:
                         eng.tensor_scalar(o, i, s, None, op0=ALU.mult),
                         inc=(g == 7))
                c.wait("vector", "dv", c.counts["dv"])
                v5 = v_op(c, lambda eng, o=r8[:, 3], i=r64[:, 0]: eng.max(out=o, in_=i))
                c.wait("vector", "dv", v5)
                v6 = v_op(c, lambda eng, o=r64[:, 1], i=r64[:, 0], s=r8[:, 3, 5:6]:
                          eng.tensor_scalar(o, i, s, None, op0=ALU.is_ge))
                c.wait("vector", "dv", v6)
                v7 = v_op(c, lambda eng, o=r64[:, 1], a=r64[:, 1], b=r64[:, 0]:
                          eng.tensor_mul(o, a, b))
                c.wait("vector", "dv", v7)
                v8 = v_op(c, lambda eng, o=r1[:, 0:1], i=r64[:, 1]:
                          eng.reduce_sum(o, i, axis=AX.X))
                c.wait("vector", "dv", v8)
                v9 = v_op(c, lambda eng, o=r1[:, 1:2], i=r1[:, 0:1]:
                          eng.tensor_single_scalar(o, i, 1e-8, op=ALU.add))
                c.wait("vector", "dv", v9)
                v10 = v_op(c, lambda eng, o=r1[:, 2:3], i=r1[:, 1:2]:
                           eng.reciprocal(o, i))
                c.wait("vector", "dv", v10)
                if last_pe2 is not None:
                    c.wait("vector", "pe", last_pe2)
                v11 = v_op(c, lambda eng, o=ctok[:], i=r64[:, 1], s=r1[:, 2:3]:
                           eng.tensor_scalar(o, i, s, None, op0=ALU.mult))
                c.wait("tensor", "dv", v11)
                if last_cT is not None:
                    c.wait("tensor", "ac", last_cT)
                pe2 = t_transpose(c, p_ed[0:E0, 2, 0:128], ctok, ident[:])
                last_pe2 = pe2
                c.wait("scalar", "pe", pe2)
                last_cT = s_op(c, lambda eng, o=combT[:, tc * 128:(tc + 1) * 128],
                               i=p_ed[0:E0, 2, 0:128]:
                               eng.activation(out=o, in_=i, func=AF.Copy))
            c.produce("combT", "ac", last_cT)
            dbg_dump(c, "scoresT_h%d" % half, scoresT2, [E0, HALF], F32)
            dbg_dump(c, "combT_h%d" % half, combT, [E0, HALF], BF16)
            rt_dv = c.counts["dv"]; rt_ac = c.counts["ac"]; rt_pe = c.counts["pe"]
            for r in ("wgu0", "wgu1", "wd0", "wd1", "stage0", "stage1"):
                c.release_add(r, "dv", rt_dv)
                c.release_add(r, "ac", rt_ac)
                c.release_add(r, "pe", rt_pe)
            for b in range(3):
                c.release_add("ped%d" % b, "ac", rt_ac)
                c.release_add("ped%d" % b, "dv", rt_dv)
            c.release_add("pcb", "ac", rt_ac)
            for p in range(2):
                c.release_add("gps%d" % p, "dv", rt_dv)
                c.release_add("ups%d" % p, "dv", rt_dv)

        wg = asb(c, "f0wg", [128, 2, KC * 512], BF16, es)[:]
        wu = asb(c, "f0wu", [128, 2, KC * 512], BF16, es)[:]
        wd = asb(c, "f0wd", [128, 2, IC0 * 1024], BF16, es)[:]
        stage = asb(c, "f0st", [128, 2, 1024], F32, es)[:]
        hst = asb(c, "f0h", [128, 8, TOK], BF16, es)[:]
        sg = asb(c, "f0sg", [128, 2, TOK], F32, es)[:]
        ht = asb(c, "f0ht", [128, 2, TOK], F32, es)[:]
        cbs = asb(c, "f0cb", [128, 2, TOK], BF16, es)[:]

        stream_ctr = [0]

        def stream_chunk(dst_ap, src_ap, key=None):
            sslot = stream_ctr[0] % 2
            stream_ctr[0] += 1
            c.acquire("sync", "stage%d" % sslot)
            dst_stage = stage[:, sslot, :]
            if len(src_ap.shape) == 3:
                dst_stage = dst_stage.rearrange("p (q m) -> p q m",
                                                q=src_ap.shape[1])
            ld = dma(c, "sync", dst_stage, src_ap, "ldw_st%d" % sslot)
            c.wait("gpsimd", "ldw_st%d" % sslot, ld)
            if key:
                c.acquire("gpsimd", key)
            av = g_op(c, lambda eng, o=dst_ap, i=stage[:, sslot, :]:
                      eng.tensor_copy(o, i))
            c.release("stage%d" % sslot, "gp", av)
            return av

        wg_f = wg.rearrange("p s m -> p (s m)")
        wu_f = wu.rearrange("p s m -> p (s m)")
        wd_f = wd.rearrange("p s m -> p (s m)")
        ld_sh = None
        for i, (dst, drn) in enumerate([(wg_f, "sgT"), (wu_f, "suT"),
                                        (wd_f, "sdT")]):
            for r in ("wgu0", "wgu1", "wd0", "wd1"):
                c.acquire("sync", r)
            if i in c.wc_written["sh"]:
                c.wait_dep("sync", c.wc_written["sh"][i])
                ldv = dma(c, "sync", dst, c.wcache["sh"][i], "ldshc")
                ld_sh = ("ldshc", ldv)
            else:
                last = None
                for ch in range(8):
                    last = stream_chunk(dst[:, ch * 1024:(ch + 1) * 1024],
                                        c.dram[drn][ch * 128:(ch + 1) * 128])
                wb_cache(c, "sh", i, dst, ("gp", last), frees=("wgu0", "wgu1", "wd0", "wd1"))
                ld_sh = ("gp", last)
        c.wait_dep("tensor", ld_sh)
        acc_last = {}
        h_done = {}
        for it in range(NT):
            ts = slice(it * TOK, (it + 1) * TOK)
            for ic in range(8):
                par = ic % 2
                c.acquire("tensor", "gps%d" % par)
                c.acquire("tensor", "ups%d" % par)
                for kc in range(KC):
                    t_mm(c, gps[par][:], wg_f[:, kc * 1024 + ic * 128: kc * 1024 + ic * 128 + 128],
                         xn[:, kc, ts], start=(kc == 0), stop=(kc == KC - 1))
                last_u = None
                for kc in range(KC):
                    last_u = t_mm(c, ups[par][:], wu_f[:, kc * 1024 + ic * 128: kc * 1024 + ic * 128 + 128],
                                  xn[:, kc, ts], start=(kc == 0), stop=(kc == KC - 1),
                                  inc=(kc == KC - 1))
                c.wait("scalar", "pe", last_u)
                c.acquire("scalar", "sg%d" % par)
                a_s = s_op(c, lambda eng, o=sg[:, par], i=gps[par][:]:
                           eng.activation(out=o, in_=i, func=AF.Silu))
                c.release("gps%d" % par, "ac", a_s)
                c.wait("vector", "ac", a_s)
                c.acquire("vector", "hst%d" % ic)
                v_h = v_op(c, lambda eng, o=hst[:, ic], a=sg[:, par], b=ups[par][:]:
                           eng.tensor_mul(o, a, b))
                c.release("ups%d" % par, "dv", v_h)
                c.release("sg%d" % par, "dv", v_h)
                h_done[ic] = v_h
            c.wait("tensor", "dv", h_done[7])
            for hc in range(KC):
                bank = hc % 3
                c.acquire("tensor", "ped%d" % bank)
                last = None
                for ic in range(8):
                    last = t_mm(c, p_ed[:, bank, :],
                                wd_f[:, ic * 1024 + hc * 128: ic * 1024 + hc * 128 + 128],
                                hst[:, ic], start=(ic == 0), stop=(ic == 7),
                                inc=(ic == 7))
                c.wait("scalar", "pe", last)
                a_i = s_op(c, lambda eng, o=acc[:, hc, ts], i=p_ed[:, bank, :]:
                           eng.activation(out=o, in_=i, func=AF.Copy))
                c.release("ped%d" % bank, "ac", a_i)
                acc_last[(hc, it)] = ("ac", a_i)
            pe_now = c.counts["pe"]
            for ic in range(8):
                c.release("hst%d" % ic, "pe", pe_now)

        dbg_dump(c, "accsh_h%d" % half, acc, [128, KC, HALF], F32)

        def stream_expert_gu(e):
            slot = e % 2
            key = "wgu%d" % slot
            if e in c.wc_written["eg"]:
                c.wait_dep("sync", c.wc_written["eg"][e])
                c.wait_dep("sync", c.wc_written["eu"][e])
                c.acquire("sync", key)
                dma(c, "sync", wg[:, slot], c.wcache["eg"][e], "ldgu%d" % slot)
                ldv = dma(c, "sync", wu[:, slot], c.wcache["eu"][e],
                          "ldgu%d" % slot)
                last = ("ldgu%d" % slot, ldv)
            else:
                lv = None
                for ch in range(IC0):
                    lv = stream_chunk(wg[:, slot, ch * 1024:(ch + 1) * 1024],
                                      c.dram["egT"][e, ch * 256:(ch + 1) * 256].rearrange(
                                          "(q kp) m -> kp q m", kp=128), key)
                wb_cache(c, "eg", e, wg[:, slot], ("gp", lv), frees=("wgu%d" % slot,))
                for ch in range(IC0):
                    lv = stream_chunk(wu[:, slot, ch * 1024:(ch + 1) * 1024],
                                      c.dram["euT"][e, ch * 256:(ch + 1) * 256].rearrange(
                                          "(q kp) m -> kp q m", kp=128), key)
                wb_cache(c, "eu", e, wu[:, slot], ("gp", lv), frees=("wgu%d" % slot,))
                last = ("gp", lv)
            c.acquire("sync", "sel%d" % slot)
            lds = dma(c, "sync", sel_sb[:, slot, :],
                      c.dram["sel64"][:, e * 128:(e + 1) * 128],
                      "ldw_sel%d" % slot)
            return last, lds

        def stream_expert_ed(e):
            slot = e % 2
            key = "wd%d" % slot
            if e in c.wc_written["ed"]:
                c.wait_dep("sync", c.wc_written["ed"][e])
                c.acquire("sync", key)
                ldv = dma(c, "sync", wd[:, slot], c.wcache["ed"][e],
                          "lded%d" % slot)
                return ("lded%d" % slot, ldv)
            lv = None
            for ch in range(IC0):
                lv = stream_chunk(wd[:, slot, ch * 1024:(ch + 1) * 1024],
                                  c.dram["edT"][e, ch * 128:(ch + 1) * 128], key)
            wb_cache(c, "ed", e, wd[:, slot], ("gp", lv), frees=("wd%d" % slot,))
            return ("gp", lv)

        pe_sh = c.counts["pe"]
        for s in (0, 1):
            c.release("wgu%d" % s, "pe", pe_sh)
            c.release("wd%d" % s, "pe", pe_sh)
        gu_done = {}
        ed_cast = {}
        sel_done = {}
        gu_done[0], sel_done[0] = stream_expert_gu(0)
        ed_cast[0] = stream_expert_ed(0)
        gu_done[1], sel_done[1] = stream_expert_gu(1)
        ed_cast[1] = stream_expert_ed(1)
        hprime = {}

        def emit_ed(e, it):
            ts = slice(it * TOK, (it + 1) * TOK)
            slot = e % 2
            jslot = (e * NT + it) % 2
            c.wait_dep("tensor", ed_cast[e])
            c.wait("tensor", "dv", hprime[(e, it)])
            for hc in range(KC):
                bank = hc % 3
                c.acquire("tensor", "ped%d" % bank)
                last = None
                for ic in range(IC0):
                    last = t_mm(c, p_ed[:, bank, :],
                                wd[:, slot, ic * 1024 + hc * 128: ic * 1024 + hc * 128 + 128],
                                hst[:, jslot * 4 + ic],
                                start=(ic == 0), stop=(ic == IC0 - 1),
                                inc=(ic == IC0 - 1))
                c.wait("vector", "pe", last)
                c.wait_dep("vector", acc_last.get((hc, it)))
                vadd = v_op(c, lambda eng, o=acc[:, hc, ts], a=acc[:, hc, ts],
                            b=p_ed[:, bank, :]: eng.tensor_add(o, a, b))
                c.release("ped%d" % bank, "dv", vadd)
                acc_last[(hc, it)] = ("dv", vadd)
            pe_now = c.counts["pe"]
            for ic in range(IC0):
                c.release("hst%d" % (jslot * 4 + ic), "pe", pe_now)
            if it == NT - 1:
                c.release("wd%d" % slot, "pe", pe_now)
                c.release("sel%d" % slot, "pe", pe_now)

        jobs = [(e, it) for e in range(E0) for it in range(NT)]
        for ji, (e, it) in enumerate(jobs):
            slot = e % 2
            jslot = ji % 2
            ts = slice(it * TOK, (it + 1) * TOK)
            if it == 0:
                c.wait_dep("tensor", gu_done[e])
                c.wait("tensor", "ldw_sel%d" % slot, sel_done[e])
            c.acquire("tensor", "pcb")
            c.require("tensor", "combT")
            cbm = t_mm(c, p_cb[:], sel_sb[:, slot, :], combT[:, ts],
                       start=True, stop=True, inc=True)
            c.wait("scalar", "pe", cbm)
            c.acquire("scalar", "cbs%d" % jslot)
            cbe = s_op(c, lambda eng, o=cbs[:, jslot], i=p_cb[:]:
                       eng.activation(out=o, in_=i, func=AF.Copy))
            c.release("pcb", "ac", cbe)
            for ic in range(IC0):
                par = ic % 2
                c.acquire("tensor", "gps%d" % par)
                c.acquire("tensor", "ups%d" % par)
                for kc in range(KC):
                    t_mm(c, gps[par][:], wg[:, slot, kc * 512 + ic * 128: kc * 512 + ic * 128 + 128],
                         xn[:, kc, ts], start=(kc == 0), stop=(kc == KC - 1))
                last_u = None
                for kc in range(KC):
                    last_u = t_mm(c, ups[par][:], wu[:, slot, kc * 512 + ic * 128: kc * 512 + ic * 128 + 128],
                                  xn[:, kc, ts], start=(kc == 0), stop=(kc == KC - 1),
                                  inc=(kc == KC - 1))
                c.wait("scalar", "pe", last_u)
                c.acquire("scalar", "sg%d" % par)
                a_s = s_op(c, lambda eng, o=sg[:, par], i=gps[par][:]:
                           eng.activation(out=o, in_=i, func=AF.Silu))
                c.release("gps%d" % par, "ac", a_s)
                c.wait("vector", "ac", a_s)
                c.acquire("vector", "ht%d" % par)
                v_h = v_op(c, lambda eng, o=ht[:, par], a=sg[:, par], b=ups[par][:]:
                           eng.tensor_mul(o, a, b))
                c.release("ups%d" % par, "dv", v_h)
                c.release("sg%d" % par, "dv", v_h)
                c.wait("vector", "dv", v_h)
                c.wait("vector", "ac", cbe)
                c.acquire("vector", "hst%d" % (jslot * 4 + ic))
                g_h = v_op(c, lambda eng, o=hst[:, jslot * 4 + ic], a=ht[:, par],
                           b=cbs[:, jslot]: eng.tensor_tensor(o, a, b, op=ALU.mult))
                c.release("ht%d" % par, "dv", g_h)
                if ic == IC0 - 1:
                    c.release("cbs%d" % jslot, "dv", g_h)
                    hprime[(e, it)] = g_h
            if it == NT - 1:
                pe_now = c.counts["pe"]
                c.release("wgu%d" % slot, "pe", pe_now)
                c.release("sel%d" % slot, "pe", pe_now)
                if e + 2 < E0:
                    gu_done[e + 2], sel_done[e + 2] = stream_expert_gu(e + 2)
            if ji >= 1:
                pe_, pit_ = jobs[ji - 1]
                emit_ed(pe_, pit_)
                if pit_ == NT - 1 and pe_ + 2 < E0:
                    ed_cast[pe_ + 2] = stream_expert_ed(pe_ + 2)
        emit_ed(*jobs[-1])

        c.wait("vector", "dv", c.counts["dv"])
        for kc in range(KC):
            v_op(c, lambda eng, o=xT[:, kc, tok0:tok0 + HALF],
                 a=acc[:, kc, :], x2=xT[:, kc, tok0:tok0 + HALF]:
                 eng.scalar_tensor_tensor(o, a, 0.5, x2, op0=ALU.mult, op1=ALU.add))
        dbg_dump(c, "acc_h%d" % half, acc, [128, KC, HALF], F32)
        c.produce("xT", "dv", c.counts["dv"])
        c.barrier()


NQC = S // 128
NJC = S // 128
QT4 = S // TOK
AVT = 128
NAV = S // AVT


def emit_attn(c, xT, layer):
    ident = c.bufs["ident"]
    identb = c.bufs["identb"]
    with ExitStack() as es:
        xn = asb(c, "atxn", [128, KC, S], BF16, es)[:]
        attnout = asb(c, "atao", [128, NHP, S], BF16, es)[:]
        qt = asb(c, "atqkv", [128, 3, S], BF16, es)[:]
        vtm = asb(c, "atvtm", [128, NJC, 128], BF16, es)[:]
        wqf = asb(c, "atwqf", [128, KC, 128], F32, es)[:]
        wqb = asb(c, "atwqb", [128, KC, 384], BF16, es)[:]
        Ss = asb(c, "atSs", [128, S], F32, es)[:]
        mlsA = asb(c, "atmlsA", [128, NQC], F32, es)[:]
        mlsR = asb(c, "atmlsR", [16, 128], F32, es)[:]
        mlsBq = asb(c, "atmlsBq", [128, 2, AVT], F32, es)[:]
        atile = asb(c, "atat", [128, NJC, AVT], BF16, es)[:]
        esd = atile.rearrange("p a b -> p (a b)")
        tsub = asb(c, "attsub", [128, 2, AVT], F32, es)[:]
        nm1 = asb(c, "atnm", [128, 4], F32, es)[:]
        if layer == 1:
            alb = asb(c, "atalb", [128, S], F32, es)[:]
            xab = asb(c, "atxab", [96, S], BF16, es)[:]
            aqb = asb(c, "ataqb", [128, KC, 96], BF16, es)[:]
            bqf = asb(c, "atbqf", [72, 128], F32, es)[:]
            bqb = asb(c, "atbqb", [72, 128], BF16, es)[:]
            bias_sb = asb(c, "atbias", [128, 3], F32, es)[:]
            aob = asb(c, "ataob", [128, KC, 8], BF16, es)[:]
            xaO = xab[0:8, :]
        psA = aps(c, "atpsA", [128, TOK], F32, es)[:]
        psB = aps(c, "atpsB", [128, TOK], F32, es)[:]
        psC = aps(c, "atpsC", [128, AVT], F32, es)[:]
        psD = aps(c, "atpsD", [128, AVT], F32, es)[:]
        psE = aps(c, "atpsE", [128, TOK], F32, es)[:]
        psF = aps(c, "atpsF", [128, 128], BF16, es)[:]
        psG = aps(c, "atpsG", [128, 128], BF16, es)[:]
        psH = aps(c, "atpsH", [128, AVT], F32, es)[:]
        mls_row = asb(c, "atmlsrow", [33, S // 2], F32, es)[:]

        with ExitStack() as bs:
            statB = asb(c, "atstatB", [128, 2, TOK], F32, bs)[:]
            statB2 = asb(c, "atstatB2", [128, 2, TOK], F32, bs)[:]
            sq = asb(c, "atsq", [128, 2, TOK], F32, bs)[:]
            ln_sc = {"p0": psA, "p1": psB, "statB": statB, "statB2": statB2,
                     "sq": sq}
            emit_ln(c, xT, xn, 0, S, ln_sc)

            if layer == 0:
                c.require("vector", "xn")
                gp1 = c.bufs["gp1"]
                sqr_mm = {}
                stats_read = None
                prev_apply = None
                for it in range(QT4):
                    t0 = it * TOK
                    if stats_read is not None:
                        c.wait("tensor", "dv", stats_read)
                    if prev_apply is not None:
                        c.wait("vector", "dv", prev_apply)
                    for kc in range(KC):
                        par = kc % 2
                        gi = it * KC + kc
                        if gi >= 2:
                            c.wait("vector", "pe", sqr_mm[gi - 2])
                        vq = v_op(c, lambda eng, o=sq[:, par], i=xn[:, kc, t0:t0 + TOK]:
                                  eng.tensor_tensor(o, i, i, op=ALU.mult))
                        c.wait("tensor", "dv", vq)
                        sqr_mm[gi] = t_mm(c, psA[:], c.bufs["ones_f"][:], sq[:, par],
                                          start=(kc == 0), stop=(kc == KC - 1),
                                          inc=True)
                    pe_done = c.counts["pe"]
                    sB = statB[:, it % 2]
                    c.wait("scalar", "pe", pe_done)
                    if prev_apply is not None:
                        c.wait("scalar", "dv", prev_apply)
                    a1 = s_op(c, lambda eng, o=sB, i=psA[:]:
                              eng.activation(out=o, in_=i, func=AF.Sqrt))
                    c.wait("vector", "ac", a1)
                    v1 = v_op(c, lambda eng, o=sB, i=sB:
                              eng.tensor_single_scalar(o, i, 1e-12, op=ALU.max))
                    c.wait("vector", "dv", v1)
                    v2 = v_op(c, lambda eng, o=sB, i=sB: eng.reciprocal(o, i))
                    stats_read = v2
                    c.wait("vector", "dv", v2)
                    for kc in range(KC):
                        par = kc % 2
                        vv = v_op(c, lambda eng, o=sq[:, par], x=xn[:, kc, t0:t0 + TOK],
                                  r=sB: eng.tensor_mul(o, x, r))
                        c.wait("vector", "dv", vv)
                        v_op(c, lambda eng, o=xn[:, kc, t0:t0 + TOK], i=sq[:, par],
                             s=gp1[:, kc:kc + 1]:
                             eng.tensor_scalar(o, i, s, None, op0=ALU.mult))
                    prev_apply = c.counts["dv"]
                c.produce("xn", "dv", c.counts["dv"])
            c.barrier()

        xn_dep = c.ready["xn"]
        wname = "wqkv0" if layer == 0 else "wqkv1"
        if layer == 1:
            aqf_v = wqf.rearrange("p a b -> p (a b)")[:, 0:96 * KC].rearrange(
                "p (a b) -> p a b", a=KC)
            lda = dma(c, "sync", aqf_v, c.dram["aqkv1"].rearrange(
                "kc kp m -> kp kc m"), "ldw_aq")
            c.wait("gpsimd", "ldw_aq", lda)
            ga = g_op(c, lambda eng, o=aqb[:], i=aqf_v: eng.tensor_copy(o, i))
            aof_v = wqf.rearrange("p a b -> p (a b)")[:, 0:8 * KC].rearrange(
                "p (a b) -> p a b", a=KC)
            c.wait("sync", "gp", ga)
            ldao = dma(c, "sync", aof_v, c.dram["AoT"].rearrange(
                "(kc kp) r -> kp kc r", kp=128), "ldw_ao")
            c.wait("gpsimd", "ldw_ao", ldao)
            ga2a = g_op(c, lambda eng, o=aob[:], i=aof_v: eng.tensor_copy(o, i))
            c.release("atwqf", "gp", ga2a)

        for hp in range(NHP):
            qfam = "qkv%d" % layer
            if hp in c.wc_written[qfam]:
                c.wait_dep("sync", c.wc_written[qfam][hp])
                c.acquire("sync", "atwqb")
                ldq = dma(c, "sync", wqb.rearrange("p a b -> p (a b)"),
                          c.wcache[qfam][hp], "ldqkvc")
                gw_dep = ("ldqkvc", ldq)
            else:
                c.acquire("gpsimd", "atwqb")
                gw = None
                for p in range(3):
                    c.acquire("sync", "atwqf")
                    ldw = dma(c, "sync", wqf[:], c.dram[wname][hp, :, :, p].rearrange(
                        "kc kp m -> kp kc m"), "ldw_qkv")
                    c.wait("gpsimd", "ldw_qkv", ldw)
                    gw = g_op(c, lambda eng, o=wqb[:, :, p * 128:(p + 1) * 128], i=wqf[:]:
                              eng.tensor_copy(o, i))
                    c.release("atwqf", "gp", gw)
                wb_cache(c, qfam, hp, wqb.rearrange("p a b -> p (a b)"),
                         ("gp", gw), frees=("atwqb",))
                gw_dep = ("gp", gw)
            if layer == 1:
                c.acquire("sync", "atbq")
                for p in range(3):
                    dma(c, "sync", bqf[32 * p:32 * p + 8, :],
                        c.dram["bqkv1"][hp, :, p, :], "ldw_bq")
                ldbias = dma(c, "sync", bias_sb[:], c.dram["biasqkv1"][hp], "ldw_bq")
                c.wait("gpsimd", "ldw_bq", ldbias)
                gb = None
                for p in range(3):
                    gb = g_op(c, lambda eng, o=bqb[32 * p:32 * p + 8, :],
                              i=bqf[32 * p:32 * p + 8, :]: eng.tensor_copy(o, i))
            c.wait_dep("tensor", xn_dep)
            c.wait_dep("tensor", gw_dep)
            if layer == 1:
                c.wait("tensor", "gp", gb)
                c.wait("tensor", "gp", ga)
                c.wait("scalar", "ldw_bq", ldbias)
                for it in range(QT4):
                    ts = slice(it * TOK, (it + 1) * TOK)
                    c.acquire("tensor", "psE")
                    last = None
                    for kc in range(KC):
                        last = t_mm(c, psE[0:96, :], aqb[:, kc, :], xn[:, kc, ts],
                                    start=(kc == 0), stop=(kc == KC - 1),
                                    inc=(kc == KC - 1))
                    c.wait("scalar", "pe", last)
                    ax = s_op(c, lambda eng, o=xab[:, ts], i=psE[0:96, :]:
                              eng.activation(out=o, in_=i, func=AF.Copy, scale=0.125))
                    c.release("psE", "ac", ax)
                xab_ready = c.counts["ac"]
                c.wait("tensor", "ac", xab_ready)
            for p in range(3):
                for it in range(QT4):
                    ts = slice(it * TOK, (it + 1) * TOK)
                    bank, bname = (psA, "psA") if (p * QT4 + it) % 2 == 0 else (psB, "psB")
                    c.acquire("tensor", bname)
                    nk = KC if layer == 1 else KC - 1
                    for kc in range(nk):
                        t_mm(c, bank[:], wqb[:, kc, p * 128:(p + 1) * 128],
                             xn[:, kc, ts], start=(kc == 0), stop=False)
                    if layer == 1:
                        last = t_mm(c, bank[:], bqb[32 * p:32 * p + 8, :],
                                    xab[32 * p:32 * p + 8, ts],
                                    start=False, stop=True, inc=True)
                    else:
                        last = t_mm(c, bank[:], wqb[:, KC - 1, p * 128:(p + 1) * 128],
                                    xn[:, KC - 1, ts], start=False, stop=True, inc=True)
                    if layer == 0:
                        c.wait("scalar", "pe", last)
                        scale = 0.125 if p == 0 else 1.0
                        ev = s_op(c, lambda eng, o=qt[:, p, ts], i=bank[:], sc=scale:
                                  eng.activation(out=o, in_=i, func=AF.Copy, scale=sc))
                    else:
                        c.wait("vector", "pe", last)
                        c.wait("vector", "ldw_bq", ldbias)
                        ev = v_op(c, lambda eng, o=qt[:, p, ts], i=bank[:],
                                  b=bias_sb[:, p:p + 1]:
                                  eng.tensor_scalar(o, i, b, None, op0=ALU.add))
                        c.release(bname, "dv", ev)
                        continue
                    c.release(bname, "ac", ev)
            c.release("atwqb", "pe", c.counts["pe"])
            if layer == 1:
                c.release("atbq", "ac", c.counts["ac"])
            c.wait("tensor", "ac", c.counts["ac"])
            c.wait("tensor", "dv", c.counts["dv"])
            for jc in range(NJC):
                bank, bname = (psF, "psF") if jc % 2 == 0 else (psG, "psG")
                c.acquire("tensor", bname)
                pv = t_transpose(c, bank[:], qt[:, 2, jc * 128:(jc + 1) * 128],
                                 identb[:])
                c.wait("scalar", "pe", pv)
                ev = s_op(c, lambda eng, o=vtm[:, jc, :], i=bank[:]:
                          eng.activation(out=o, in_=i, func=AF.Copy))
                c.release(bname, "ac", ev)
            vtm_ready = c.counts["ac"]
            for hh in range(2):
                h = hp * 2 + hh
                r0 = 64 * hh
                if layer == 1:
                    c.acquire("sync", "atalb")
                    ldal = dma(c, "sync", alb[:], c.dram["alibiB"][h], "ldw_alb")
                    c.wait("vector", "ldw_alb", ldal)
                c.acquire("vector", "atmlsA")
                prev_a1 = None
                last_ev = None
                for qc in range(NQC):
                    qs = slice(qc * 128, (qc + 1) * 128)
                    for kt in range(QT4):
                        ks = slice(kt * TOK, (kt + 1) * TOK)
                        bank, bname = (psA, "psA") if kt % 2 == 0 else (psB, "psB")
                        c.acquire("tensor", bname)
                        last = t_mm(c, bank[:], qt[r0:r0 + 64, 0, qs],
                                    qt[r0:r0 + 64, 1, ks], start=True, stop=True,
                                    inc=True)
                        c.wait("vector", "pe", last)
                        if prev_a1 is not None:
                            c.wait("vector", "ac", prev_a1)
                        if layer == 1:
                            ev = v_op(c, lambda eng, o=Ss[:, ks], a=bank[:],
                                      b=alb[:, ks]: eng.tensor_add(o, a, b))
                        else:
                            ev = v_op(c, lambda eng, o=Ss[:, ks], a=bank[:]:
                                      eng.tensor_copy(o, a))
                        c.release(bname, "dv", ev)
                        last_ev = ev
                    c.wait("vector", "dv", last_ev)
                    v1 = v_op(c, lambda eng, o=nm1[:, 0:1], i=Ss[:]:
                              eng.reduce_max(o, i, axis=AX.X, negate=True))
                    c.wait("scalar", "dv", v1)
                    a1 = s_op(c, lambda eng, o=esd[:, 0:S], i=Ss[:], b=nm1[:, 0:1],
                              acc=nm1[:, 1:2]:
                              eng.activation(out=o, in_=i, func=AF.Exp, bias=b,
                                             accum_out=acc))
                    prev_a1 = a1
                    c.wait("scalar", "ac", a1)
                    a2 = s_op(c, lambda eng, o=nm1[:, 2:3], i=nm1[:, 1:2]:
                              eng.activation(out=o, in_=i, func=AF.Ln))
                    c.wait("vector", "ac", a2)
                    vm = v_op(c, lambda eng, o=mlsA[:, qc:qc + 1], a=nm1[:, 2:3],
                              b=nm1[:, 0:1]: eng.tensor_sub(o, a, b))
                if layer == 1:
                    c.release("atalb", "dv", c.counts["dv"])
                c.wait("tensor", "dv", c.counts["dv"])
                c.acquire("tensor", "atmlsR")
                pt = t_transpose(c, psH[0:16, 0:128], mlsA[:], ident[:])
                c.release("atmlsA", "pe", pt)
                c.wait("scalar", "pe", pt)
                am = s_op(c, lambda eng, o=mlsR[:], i=psH[0:16, 0:128]:
                          eng.activation(out=o, in_=i, func=AF.Copy))
                c.wait("sync", "ac", am)
                c.acquire("sync", "mlsrow")
                b1 = dma(c, "sync", c.dram["mls_bounce"][:], mlsR[:], "ldw_mls")
                c.wait("sync", "ldw_mls", b1)
                flat = c.dram["mls_bounce"].rearrange("a b -> (a b)")
                dma(c, "sync", mls_row[0:1, :], flat[0:S // 2].unsqueeze(0),
                    "ldw_mls")
                b2 = dma(c, "sync", mls_row[32:33, :],
                         flat[S // 2:S].unsqueeze(0), "ldw_mls")
                c.release("atmlsR", "ldw_mls", b2)
                c.wait("tensor", "ldw_mls", b2)
                onerow_f = c.bufs["onerow_f"]
                for qa in range(NAV):
                    qs = slice(qa * AVT, (qa + 1) * AVT)
                    mpar = qa % 2
                    c.acquire("tensor", "psH")
                    mr_r = (qa // (NAV // 2)) * 32
                    mr_c = (qa % (NAV // 2)) * AVT
                    ones33 = c.bufs["ones33"]
                    pb = t_mm(c, psH[:, 0:AVT], ones33[mr_r:mr_r + 1, 0:AVT],
                              mls_row[mr_r:mr_r + 1, mr_c:mr_c + AVT],
                              start=True, stop=True, inc=True)
                    c.wait("scalar", "pe", pb)
                    c.acquire("scalar", "mlsBq%d" % mpar)
                    mb_ev = s_op(c, lambda eng, o=mlsBq[:, mpar], i=psH[:, 0:AVT]:
                                 eng.activation(out=o, in_=i, func=AF.Copy))
                    c.release("psH", "ac", mb_ev)
                    c.wait("vector", "ac", mb_ev)
                    for jc in range(NJC):
                        js = slice(jc * 128, (jc + 1) * 128)
                        bank, bname = (psC, "psC") if jc % 2 == 0 else (psD, "psD")
                        c.acquire("tensor", bname)
                        last = t_mm(c, bank[:], qt[r0:r0 + 64, 1, js],
                                    qt[r0:r0 + 64, 0, qs], start=True, stop=True,
                                    inc=True)
                        c.wait("vector", "pe", last)
                        par = jc % 2
                        c.acquire("vector", "attsub%d" % par)
                        vs = v_op(c, lambda eng, o=tsub[:, par], a=bank[:],
                                  b=mlsBq[:, mpar]: eng.tensor_sub(o, a, b))
                        c.release(bname, "dv", vs)
                        c.wait("scalar", "dv", vs)
                        c.acquire("scalar", "atat%d" % jc)
                        if layer == 1:
                            alC = c.bufs["alibiC"]
                            ae = s_op(c, lambda eng, o=atile[:, jc, :], i=tsub[:, par],
                                      b=alC[:, h, jc:jc + 1]:
                                      eng.activation(out=o, in_=i, func=AF.Exp, bias=b))
                        else:
                            ae = s_op(c, lambda eng, o=atile[:, jc, :], i=tsub[:, par]:
                                      eng.activation(out=o, in_=i, func=AF.Exp))
                        c.release("attsub%d" % par, "ac", ae)
                    c.wait("tensor", "ac", c.counts["ac"])
                    c.wait("tensor", "ac", vtm_ready)
                    c.acquire("tensor", "psE")
                    last = None
                    for jc in range(NJC):
                        last = t_mm(c, psE[0:64, 0:AVT], vtm[:, jc, r0:r0 + 64],
                                    atile[:, jc, :], start=(jc == 0),
                                    stop=(jc == NJC - 1), inc=(jc == NJC - 1))
                    pe_av = last
                    for jc in range(NJC):
                        c.release("atat%d" % jc, "pe", pe_av)
                    c.wait("scalar", "pe", pe_av)
                    av_ev = s_op(c, lambda eng, o=attnout[r0:r0 + 64, hp, qs],
                                 i=psE[0:64, 0:AVT]:
                                 eng.activation(out=o, in_=i, func=AF.Copy))
                    c.release("psE", "ac", av_ev)
                    c.release("mlsBq%d" % mpar, "dv", c.counts["dv"])
        ao_ready = c.counts["ac"]

        if layer == 1:
            c.wait("tensor", "ac", ao_ready)
            for it in range(QT4):
                ts = slice(it * TOK, (it + 1) * TOK)
                c.acquire("tensor", "psE")
                last = None
                for kc in range(KC):
                    last = t_mm(c, psE[0:8, :], aob[:, kc, :], attnout[:, kc, ts],
                                start=(kc == 0), stop=(kc == KC - 1),
                                inc=(kc == KC - 1))
                c.wait("scalar", "pe", last)
                axo = s_op(c, lambda eng, o=xaO[:, ts], i=psE[0:8, :]:
                           eng.activation(out=o, in_=i, func=AF.Copy, scale=0.125))
                c.release("psE", "ac", axo)
            xaO_ready = c.counts["ac"]
            c.require("vector", "xT")
            c.require("vector", "xn")
            for kc in range(KC):
                v_op(c, lambda eng, o=xT[:, kc, :], a=xT[:, kc, :], b=xn[:, kc, :]:
                     eng.tensor_add(o, a, b))
            xplus = c.counts["dv"]
        c.wait("tensor", "ac", ao_ready)
        woname = "wo0" if layer == 0 else "WoT"
        bo1 = c.bufs["bo1"]
        wqb_flat = wqb.rearrange("p a b -> p (a b)")
        wo_b = wqb_flat[:, 0:1024].rearrange("p (a b) -> p a b", a=KC)
        if layer == 1:
            bob = wqb_flat[0:8, 2048:3072].rearrange("p (a b) -> p a b", a=KC)
            bof_v = wqf[0:8, :, :]
            c.acquire("sync", "atwqf")
            ldbo = dma(c, "sync", bof_v, c.dram["BoTr"].rearrange(
                "r (kc m) -> r kc m", kc=KC), "ldw_bo")
            c.wait("gpsimd", "ldw_bo", ldbo)
            c.acquire("gpsimd", "atwqb")
            ga2 = g_op(c, lambda eng, o=bob, i=bof_v: eng.tensor_copy(o, i))
            c.release("atwqf", "gp", ga2)
            c.wait("tensor", "gp", ga2)
        wofam = "wo%d" % layer
        for mc in range(KC):
            if mc in c.wc_written[wofam]:
                c.wait_dep("sync", c.wc_written[wofam][mc])
                c.acquire("sync", "atwqb")
                ldq = dma(c, "sync", wo_b,
                          c.wcache[wofam][mc].rearrange("p (a b) -> p a b", a=KC),
                          "ldqkvc")
                gw_dep = ("ldqkvc", ldq)
            else:
                c.acquire("sync", "atwqf")
                ldw = dma(c, "sync", wqf[:, 0:KC, 0:128],
                          c.dram[woname][:, mc * 128:(mc + 1) * 128].rearrange(
                              "(kc kp) m -> kp kc m", kp=128), "ldw_qkv")
                c.wait("gpsimd", "ldw_qkv", ldw)
                c.acquire("gpsimd", "atwqb")
                gw = g_op(c, lambda eng, o=wo_b, i=wqf[:, 0:KC, 0:128]:
                          eng.tensor_copy(o, i))
                c.release("atwqf", "gp", gw)
                wb_cache(c, wofam, mc, wo_b, ("gp", gw), frees=("atwqb",))
                gw_dep = ("gp", gw)
            c.wait_dep("tensor", gw_dep)
            if layer == 1:
                c.wait("tensor", "ac", xaO_ready)
            for it in range(QT4):
                ts = slice(it * TOK, (it + 1) * TOK)
                bank, bname = (psA, "psA") if (mc * QT4 + it) % 2 == 0 else (psB, "psB")
                c.acquire("tensor", bname)
                nk2 = KC if layer == 1 else KC - 1
                for kc in range(nk2):
                    t_mm(c, bank[:], wo_b[:, kc, :],
                         attnout[:, kc, ts], start=(kc == 0), stop=False)
                if layer == 1:
                    last = t_mm(c, bank[:], bob[0:8, mc, :],
                                xaO[0:8, ts], start=False, stop=True, inc=True)
                    c.wait("vector", "pe", last)
                    c.acquire("vector", "atSs_wo")
                    evs = v_op(c, lambda eng, o=Ss[:, ts], i=bank[:],
                               b=bo1[:, mc:mc + 1]:
                               eng.tensor_scalar(o, i, b, None, op0=ALU.add))
                    c.release(bname, "dv", evs)
                    c.wait("vector", "dv", evs)
                    c.wait("vector", "dv", xplus)
                    rz = c.bufs["rz"]
                    rv = v_op(c, lambda eng, o=xT[:, mc, ts], a=Ss[:, ts],
                              x2=xT[:, mc, ts], s=rz[:]:
                              eng.scalar_tensor_tensor(o, a, s, x2, op0=ALU.mult,
                                                       op1=ALU.add))
                    c.release("atSs_wo", "dv", rv)
                else:
                    last = t_mm(c, bank[:], wo_b[:, KC - 1, :],
                                attnout[:, KC - 1, ts], start=False, stop=True,
                                inc=True)
                    c.wait("vector", "pe", last)
                    rv = v_op(c, lambda eng, o=xT[:, mc, ts], a=xT[:, mc, ts],
                              b=bank[:]: eng.tensor_add(o, a, b))
                    c.release(bname, "dv", rv)
            c.release("atwqb", "pe", c.counts["pe"])
        c.produce("xT", "dv", c.counts["dv"])
        c.barrier()


def emit_ffn1(c, xT, half):
    tok0 = half * HALF
    ident = c.bufs["ident"]
    with ExitStack() as es:
        xn = asb(c, "f1xn", [128, KC, HALF], BF16, es)[:]
        acc = asb(c, "f1acc", [128, KC, HALF], F32, es)[:]
        cT = asb(c, "f1cT", [E1, HALF], BF16, es)[:]
        beta_row = asb(c, "f1brow", [1, HALF], F32, es)[:]
        web = asb(c, "f1web", [128, 2, KC, D], BF16, es)[:]
        stage = asb(c, "f1st", [128, 2, 1024], F32, es)[:]
        xnp = asb(c, "f1xnp", [128, 2, KC, TOK], BF16, es)[:]
        cbs = asb(c, "f1cb", [128, 2, TOK], BF16, es)[:]
        sel_sb = asb(c, "f1sel", [E1, 2, 128], BF16, es)[:]
        rt_f = asb(c, "f1rtf", [128, KC, E1], F32, es)[:]
        rt_b = asb(c, "f1rtb", [128, KC, E1], BF16, es)[:]
        p_ed = aps(c, "f1ed", [128, 3, TOK], F32, es)[:]
        p_cb = aps(c, "f1pcb", [128, TOK], F32, es)[:]
        p_a = aps(c, "f1pa", [128, TOK], F32, es)[:]
        p_b = aps(c, "f1pb", [128, TOK], F32, es)[:]

        with ExitStack() as bs:
            statB = asb(c, "f1statB", [128, 2, TOK], F32, bs)[:]
            statB2 = asb(c, "f1statB2", [128, 2, TOK], F32, bs)[:]
            sq = asb(c, "f1sq", [128, 2, TOK], F32, bs)[:]
            lgT = asb(c, "f1lgT", [E1, HALF], F32, bs)[:]
            ltok = asb(c, "f1ltok", [128, NTC, E1], F32, bs)[:]
            r8 = asb(c, "f1r8", [128, 4, E1], F32, bs)[:]
            r1 = asb(c, "f1r1", [128, 6], F32, bs)[:]
            ctokb = asb(c, "f1ctok", [128, E1], F32, bs)[:]
            bcol = asb(c, "f1bcol", [128, NTC], F32, bs)[:]
            brt = asb(c, "f1brt", [16, 128], F32, bs)[:]

            ldr = dma(c, "sync", rt_f[:], c.dram["routerT"].rearrange(
                "(kc kp) e -> kp kc e", kp=128), "ldw_gate")
            c.wait("scalar", "ldw_gate", ldr)
            a_r = s_op(c, lambda eng, o=rt_b[:], i=rt_f[:]:
                       eng.activation(out=o, in_=i, func=AF.Copy))

            ln_sc = {"p0": p_a, "p1": p_b, "statB": statB, "statB2": statB2,
                     "sq": sq}
            emit_ln(c, xT, xn, tok0, HALF, ln_sc)

            c.require("tensor", "xn")
            c.wait("tensor", "ac", a_r)
            lg_done = []
            for it in range(NT):
                if lg_done:
                    c.wait("tensor", "ac", lg_done[-1])
                last = None
                for kc in range(KC):
                    last = t_mm(c, p_cb[0:E1, :], rt_b[:, kc],
                                xn[:, kc, it * TOK:(it + 1) * TOK],
                                start=(kc == 0), stop=(kc == KC - 1),
                                inc=(kc == KC - 1))
                c.wait("scalar", "pe", last)
                a1 = s_op(c, lambda eng, o=lgT[:, it * TOK:(it + 1) * TOK],
                          i=p_cb[0:E1, :]: eng.activation(out=o, in_=i, func=AF.Copy))
                lg_done.append(a1)
            c.wait("tensor", "ac", lg_done[-1])
            tr_done = {}
            for tc in range(NTC):
                if tc >= 2:
                    c.wait("tensor", "ac", tr_done[tc - 2])
                pe1 = t_transpose(c, p_ed[:, tc % 2, 0:E1],
                                  lgT[:, tc * 128:(tc + 1) * 128],
                                  ident[0:E1, 0:E1])
                c.wait("scalar", "pe", pe1)
                tr_done[tc] = s_op(c, lambda eng, o=ltok[:, tc], i=p_ed[:, tc % 2, 0:E1]:
                                   eng.activation(out=o, in_=i, func=AF.Copy))
            last_cT = None
            last_pe2 = None
            for tc in range(NTC):
                c.wait("vector", "ac", tr_done[tc])
                lt = ltok[:, tc]
                v1 = v_op(c, lambda eng, o=r1[:, 0:1], i=lt:
                          eng.reduce_max(o, i, axis=AX.X, negate=True))
                c.wait("scalar", "dv", v1)
                a1 = s_op(c, lambda eng, o=r8[:, 0], i=lt, b=r1[:, 0:1],
                          acc=r1[:, 1:2]:
                          eng.activation(out=o, in_=i, func=AF.Exp, bias=b,
                                         accum_out=acc))
                c.wait("vector", "ac", a1)
                v2 = v_op(c, lambda eng, o=r1[:, 2:3], i=r1[:, 1:2]:
                          eng.reciprocal(o, i))
                c.wait("vector", "dv", v2)
                v3 = v_op(c, lambda eng, o=r8[:, 1], i=r8[:, 0], s=r1[:, 2:3]:
                          eng.tensor_scalar(o, i, s, None, op0=ALU.mult))
                c.wait("vector", "dv", v3)
                v4 = v_op(c, lambda eng, o=r8[:, 2], i=r8[:, 1]: eng.max(out=o, in_=i))
                c.wait("vector", "dv", v4)
                v5 = v_op(c, lambda eng, o=r1[:, 3:4], a=r8[:, 2, 0:1], b=r8[:, 2, 1:2]:
                          eng.tensor_add(o, a, b))
                c.wait("vector", "dv", v5)
                v6 = v_op(c, lambda eng, o=r1[:, 3:4], i=r1[:, 3:4]:
                          eng.tensor_single_scalar(o, i, 1e-6, op=ALU.add))
                c.wait("vector", "dv", v6)
                v7 = v_op(c, lambda eng, o=r1[:, 4:5], i=r1[:, 3:4]:
                          eng.reciprocal(o, i))
                c.wait("vector", "dv", v7)
                v8 = v_op(c, lambda eng, o=r8[:, 3], i=r8[:, 1], s=r8[:, 2, 1:2]:
                          eng.tensor_scalar(o, i, s, None, op0=ALU.is_ge))
                c.wait("vector", "dv", v8)
                v9 = v_op(c, lambda eng, o=r8[:, 3], a=r8[:, 3], b=r8[:, 1]:
                          eng.tensor_mul(o, a, b))
                c.wait("vector", "dv", v9)
                if last_pe2 is not None:
                    c.wait("vector", "pe", last_pe2)
                v10 = v_op(c, lambda eng, o=ctokb[:], i=r8[:, 3], s=r1[:, 4:5]:
                           eng.tensor_scalar(o, i, s, None, op0=ALU.mult))
                c.wait("vector", "dv", v10)
                v11 = v_op(c, lambda eng, o=r8[:, 3], a=ctokb[:], b=r8[:, 1]:
                           eng.tensor_mul(o, a, b))
                c.wait("vector", "dv", v11)
                v12 = v_op(c, lambda eng, o=bcol[:, tc:tc + 1], i=r8[:, 3]:
                           eng.reduce_sum(o, i, axis=AX.X))
                c.wait("tensor", "dv", v12)
                if last_cT is not None:
                    c.wait("tensor", "ac", last_cT)
                pe2 = t_transpose(c, p_ed[0:E1, 2, 0:128], ctokb, ident[:])
                last_pe2 = pe2
                c.wait("scalar", "pe", pe2)
                last_cT = s_op(c, lambda eng, o=cT[:, tc * 128:(tc + 1) * 128],
                               i=p_ed[0:E1, 2, 0:128]:
                               eng.activation(out=o, in_=i, func=AF.Copy))
            c.wait("tensor", "dv", c.counts["dv"])
            c.wait("tensor", "ac", last_cT)
            pe3 = t_transpose(c, p_ed[0:NTC, 2, 0:128], bcol, ident[:])
            c.wait("scalar", "pe", pe3)
            ab = s_op(c, lambda eng, o=brt[0:NTC, :], i=p_ed[0:NTC, 2, 0:128]:
                      eng.activation(out=o, in_=i, func=AF.Copy))
            c.wait("sync", "ac", ab)
            b1 = dma(c, "sync", c.dram["beta_bounce"][:],
                     brt[0:NTC, :], "ldw_mls")
            c.wait("sync", "ldw_mls", b1)
            b2 = dma(c, "sync", beta_row[:],
                     c.dram["beta_bounce"].rearrange("a b -> (a b)").unsqueeze(0),
                     "ldw_mls")
            c.wait("tensor", "ldw_mls", b2)
            c.barrier()

        stream_ctr = [0]

        def stream_chunk(dst_ap, src_ap, key=None):
            sslot = stream_ctr[0] % 2
            stream_ctr[0] += 1
            c.acquire("sync", "f1stage%d" % sslot)
            ld = dma(c, "sync", stage[:, sslot, :], src_ap, "ldw_st%d" % sslot)
            c.wait("gpsimd", "ldw_st%d" % sslot, ld)
            if key:
                c.acquire("gpsimd", key)
            av = g_op(c, lambda eng, o=dst_ap, i=stage[:, sslot, :]:
                      eng.tensor_copy(o, i))
            c.release("f1stage%d" % sslot, "gp", av)
            return av

        def stream_expert(e):
            slot = e % 2
            key = "f1w%d" % slot
            if e in c.wc_written["expw"]:
                c.wait_dep("sync", c.wc_written["expw"][e])
                c.acquire("sync", key)
                ldv = dma(c, "sync", web[:, slot].rearrange("p a b -> p (a b)"),
                          c.wcache["expw"][e], "ldex%d" % slot)
                last = ("ldex%d" % slot, ldv)
            else:
                lv = None
                for ch in range(KC):
                    lv = stream_chunk(web[:, slot, ch],
                                      c.dram["expW"][e, ch * 128:(ch + 1) * 128],
                                      key)
                wb_cache(c, "expw", e,
                         web[:, slot].rearrange("p a b -> p (a b)"), ("gp", lv),
                         frees=("f1w%d" % slot,))
                last = ("gp", lv)
            c.acquire("sync", "f1sel%d" % slot)
            lds = dma(c, "sync", sel_sb[:, slot, :],
                      c.dram["sel8"][:, e * 128:(e + 1) * 128],
                      "ldw_sel%d" % slot)
            return last, lds

        w_done = {}
        sel_done = {}
        w_done[0], sel_done[0] = stream_expert(0)
        w_done[1], sel_done[1] = stream_expert(1)
        acc_last = {}
        for e in range(E1):
            slot = e % 2
            c.wait_dep("tensor", w_done[e])
            c.wait("tensor", "ldw_sel%d" % slot, sel_done[e])
            for it in range(NT):
                ts = slice(it * TOK, (it + 1) * TOK)
                ji = e * NT + it
                jslot = ji % 2
                c.acquire("tensor", "f1pcb")
                cbm = t_mm(c, p_cb[:], sel_sb[:, slot, :], cT[:, ts],
                           start=True, stop=True, inc=True)
                c.wait("scalar", "pe", cbm)
                c.acquire("scalar", "f1cbs%d" % jslot)
                cbe = s_op(c, lambda eng, o=cbs[:, jslot], i=p_cb[:]:
                           eng.activation(out=o, in_=i, func=AF.Copy))
                c.release("f1pcb", "ac", cbe)
                c.wait("vector", "ac", cbe)
                c.require("vector", "xn")
                c.acquire("vector", "f1xnp%d" % jslot)
                g_h = None
                for kc in range(KC):
                    g_h = v_op(c, lambda eng, o=xnp[:, jslot, kc], a=xn[:, kc, ts],
                               b=cbs[:, jslot]: eng.tensor_tensor(o, a, b, op=ALU.mult))
                c.release("f1cbs%d" % jslot, "dv", g_h)
                c.wait("tensor", "dv", g_h)
                for mc in range(KC):
                    bank = mc % 3
                    c.acquire("tensor", "f1ped%d" % bank)
                    first = True
                    if e == 0:
                        t_mm(c, p_ed[:, bank, :], c.bufs["onerow_f"][0:1, :],
                             beta_row[0:1, ts], start=True, stop=False)
                        first = False
                    last = None
                    for kc in range(KC):
                        last = t_mm(c, p_ed[:, bank, :],
                                    web[:, slot, kc, mc * 128:(mc + 1) * 128],
                                    xnp[:, jslot, kc], start=first and kc == 0,
                                    stop=(kc == KC - 1), inc=(kc == KC - 1))
                        first = False
                    if e == 0:
                        c.wait("scalar", "pe", last)
                        a_i = s_op(c, lambda eng, o=acc[:, mc, ts], i=p_ed[:, bank, :]:
                                   eng.activation(out=o, in_=i, func=AF.Copy))
                        c.release("f1ped%d" % bank, "ac", a_i)
                        acc_last[(mc, it)] = ("ac", a_i)
                    else:
                        c.wait("vector", "pe", last)
                        c.wait_dep("vector", acc_last[(mc, it)])
                        vadd = v_op(c, lambda eng, o=acc[:, mc, ts],
                                    a=acc[:, mc, ts], b=p_ed[:, bank, :]:
                                    eng.tensor_add(o, a, b))
                        c.release("f1ped%d" % bank, "dv", vadd)
                        acc_last[(mc, it)] = ("dv", vadd)
                c.release("f1xnp%d" % jslot, "pe", c.counts["pe"])
            c.release("f1w%d" % slot, "pe", c.counts["pe"])
            c.release("f1sel%d" % slot, "pe", c.counts["pe"])
            if e + 2 < E1:
                w_done[e + 2], sel_done[e + 2] = stream_expert(e + 2)

        c.wait("vector", "dv", c.counts["dv"])
        c.wait("vector", "ac", c.counts["ac"])
        for kc in range(KC):
            v_op(c, lambda eng, o=xT[:, kc, tok0:tok0 + HALF],
                 a=acc[:, kc, :], x2=xT[:, kc, tok0:tok0 + HALF]:
                 eng.scalar_tensor_tensor(o, a, 0.5, x2, op0=ALU.mult, op1=ALU.add))
        c.produce("xT", "dv", c.counts["dv"])
        c.barrier()



DSHAPES_F32 = {
    "sgT": (D, ISH), "suT": (D, ISH), "sdT": (ISH, D),
    "gateT": (D, E0), "gbias": (E0, 1),
    "egT": (E0, D, INTER0), "euT": (E0, D, INTER0), "edT": (E0, INTER0, D),
    "gamma": (128, KC), "wqkv0": (NHP, KC, 128, 3, 128), "wo0": (D, D),
    "WoT": (D, D), "AoT": (D, 8), "BoTr": (8, D),
    "AqT": (D, 8), "AkT": (D, 8), "AvT": (D, 8),
    "BqTr": (8, D), "BkTr": (8, D), "BvTr": (8, D),
    "wqkv1": (NHP, KC, 128, 3, 128), "aqkv1": (KC, 128, 96),
    "bqkv1": (NHP, 8, 3, 128), "biasqkv1": (NHP, 128, 3),
    "bo1": (128, KC), "rz": (128, 1),
    "routerT": (D, E1), "expW": (E1, D, D),
    "ident": (128, 128), "ones_f": (128, 128), "onerow_f": (1, 128),
    "ones33": (33, 128),
    "alibiB": (HEADS, 128, S), "alibiC": (128, HEADS, S // 128),
}
DSHAPES_BF16 = {"identb": (128, 128), "sel64": (E0, E0 * 128),
                "sel8": (E1, E1 * 128),
                "ones_b": (128, 128), "onerow_b": (1, 128)}


def build_program(parts):
    nc = bass.Bass("TRN2", target_bir_lowering=False)
    es = ExitStack()
    c = Ctx(nc, es)
    for k, shp in DSHAPES_F32.items():
        c.dram[k] = nc.dram_tensor(k, list(shp), F32, kind="ExternalInput").ap()
    for k, shp in DSHAPES_BF16.items():
        c.dram[k] = nc.dram_tensor(k, list(shp), BF16, kind="ExternalInput").ap()
    c.dram["xT"] = nc.dram_tensor("xT", [SPC, KC, 128, S], F32,
                                  kind="ExternalInput").ap()
    c.dram["outT"] = nc.dram_tensor("outT", [SPC, KC, 128, S], F32,
                                    kind="ExternalOutput").ap()

    xT = asb(c, "xT_sb", [128, KC, S], F32)[:]
    c.bufs["dummy"] = asb(c, "dummy", [128, 4], F32)[:]
    for nm, shp, dt in [("ident", (128, 128), F32), ("identb", (128, 128), BF16),
                        ("ones_f", (128, 128), F32), ("ones_b", (128, 128), BF16),
                        ("onerow_b", (1, 128), BF16), ("gamma", (128, KC), F32),
                        ("gbias", (E0, 1), F32), ("rz", (128, 1), F32),
                        ("bo1", (128, KC), F32),
                        ("alibiC", (128, HEADS, S // 128), F32)]:
        c.bufs[nm] = asb(c, "c_" + nm, list(shp), dt)[:]
        dma(c, "sync", c.bufs[nm], c.dram[nm], "ld")
    c.bufs["onerow_f"] = asb(c, "c_onerowf", [1, 128], F32)[:]
    dma(c, "sync", c.bufs["onerow_f"], c.dram["onerow_f"], "ld")
    c.bufs["ones33"] = asb(c, "c_ones33", [33, 128], F32)[:]
    dma(c, "sync", c.bufs["ones33"], c.dram["ones33"], "ld")
    c.dram["mls_bounce"] = nc.dram_tensor("mls_bounce", [16, 128], F32,
                                          kind="Internal").ap()
    c.wcache = {}
    c.wc_written = {}
    for fam, n, cols in [("eg", E0, 4096), ("eu", E0, 4096), ("ed", E0, 4096),
                         ("sh", 3, 8192), ("expw", E1, 8192),
                         ("qkv0", NHP, 3072), ("qkv1", NHP, 3072),
                         ("wo0", KC, 1024), ("wo1", KC, 1024)]:
        c.wcache[fam] = nc.dram_tensor("wc_" + fam, [n, 128, cols], BF16,
                                       kind="Internal").ap()
        c.wc_written[fam] = {}
    c.dram["beta_bounce"] = nc.dram_tensor("beta_bounce", [NTC, 128], F32,
                                           kind="Internal").ap()
    cb = asb(c, "c_eps", [128, 8], F32)[:]
    c.bufs["gp1"] = asb(c, "c_gp1", [128, KC], F32)[:]
    vals = {"eps_ln": (0, 1e-5), "eps_rms": (1, 1e-12), "one": (4, 1.0),
            "neg_big": (5, NEG_BIG), "zero": (6, 0.0), "c32": (7, 32.0)}
    c.const_cols = {k: cb[:, i:i + 1] for k, (i, v) in vals.items()}
    for k, (i, v) in vals.items():
        c.emit("vector", lambda eng, ap=cb[:, i:i + 1], vv=v: eng.memset(ap, vv))
    c.emit("vector", lambda eng, s=c.sem("dv"): eng.engine_nop().then_inc(s, 1))
    c.inc("dv")
    ld_consts = c.counts["ld"]
    dv0 = c.counts["dv"]
    for e in ENGS[:4]:
        c.wait(e, "ld", ld_consts)
        c.wait(e, "dv", dv0)
    a_gp = v_op(c, lambda eng, o=c.bufs["gp1"], i=c.bufs["gamma"]:
                eng.tensor_scalar(o, i, 32.0, 32.0, op0=ALU.mult, op1=ALU.add))
    for e in ENGS[:4]:
        c.wait(e, "dv", a_gp)

    for seq in range(SPC):
        lx = None
        c.acquire("sync", "xTbuf")
        for kc in range(KC):
            lx = dma(c, "sync", xT[:, kc, :], c.dram["xT"][seq, kc], "ldx")
        for e in ("tensor", "vector", "scalar"):
            c.wait(e, "ldx", lx)
        c.produce("xT", "ldx", lx)
        for part in parts:
            if part == "ffn0":
                for half in range(S // HALF):
                    emit_ffn0(c, xT, half)
            elif part == "attn0":
                emit_attn(c, xT, layer=0)
            elif part == "ffn1":
                for half in range(S // HALF):
                    emit_ffn1(c, xT, half)
            elif part == "attn1":
                emit_attn(c, xT, layer=1)
            else:
                raise ValueError(part)
        c.wait_dep("sync", c.ready.get("xT"))
        sx = None
        for kc in range(KC):
            sx = dma(c, "sync", c.dram["outT"][seq, kc], xT[:, kc, :], "sto")
        c.release("xTbuf", "sto", sx)
    c.wait("sync", "sto", c.counts["sto"])

    with nc.Block() as block:
        @block.tensor
        def _(eng):
            for fn in c.prog["tensor"]:
                fn(eng)

        @block.vector
        def _(eng):
            for fn in c.prog["vector"]:
                fn(eng)

        @block.scalar
        def _(eng):
            for fn in c.prog["scalar"]:
                fn(eng)

        @block.gpsimd
        def _(eng):
            for fn in c.prog["gpsimd"]:
                fn(eng)

        @block.sync
        def _(eng):
            for fn in c.prog["sync"]:
                fn(eng)
    es.close()
    return nc



_PARTS_ALL = ["ffn0", "attn0", "ffn0", "ffn1", "attn1", "ffn1"]


def run_parts(inputs, parts):
    h, percore = prep_host(inputs)
    nc = build_program(parts)
    names = set(DSHAPES_F32) | set(DSHAPES_BF16) | {"xT"}
    in_maps = []
    for cid in range(NCORE):
        m = {k: v for k, v in h.items() if k in names}
        m["xT"] = percore[cid]["xT"]
        in_maps.append(m)
    res = run_bass_kernel_spmd(nc, in_maps, core_ids=list(range(NCORE)))
    out = np.empty((BATCH, S, D), dtype=np.float32)
    for cid in range(NCORE):
        ot = res.results[cid]["outT"].reshape(SPC, D, S)
        out[cid * SPC:(cid + 1) * SPC] = ot.transpose(0, 2, 1)
    return out


def kernel(**inputs):
    return run_parts(inputs, _PARTS_ALL)


# version 06
# speedup vs baseline: 1.1354x; 1.1354x over previous
import numpy as np
import ml_dtypes
from contextlib import ExitStack

import concourse.bass as bass
import concourse.mybir as mybir
from concourse.bass_utils import run_bass_kernel_spmd

F32 = mybir.dt.float32
BF16 = mybir.dt.bfloat16
AF = mybir.ActivationFunctionType
ALU = mybir.AluOpType
AX = mybir.AxisListType
BF = ml_dtypes.bfloat16

D = 1024
S = 2048
BATCH = 16
NCORE = 8
SPC = BATCH // NCORE
KC = D // 128
NEG_BIG = -1e30

E0 = 64; INTER0 = 512; IC0 = INTER0 // 128
ISH = 1024
E1 = 8
HEADS = 16; DH = 64; NHP = HEADS // 2
TOK = 512
HALF = 1024
NT = HALF // TOK
NTC = HALF // 128

ENGS = ("tensor", "vector", "scalar", "gpsimd", "sync")



def _c(a):
    return np.ascontiguousarray(a)


def prep_host(inp):
    h = {}
    h["sgT"] = _c(inp["L0_ffn_sg_w"].astype(np.float32).T)
    h["suT"] = _c(inp["L0_ffn_su_w"].astype(np.float32).T)
    h["sdT"] = _c(inp["L0_ffn_sd_w"].astype(np.float32).T)
    h["gateT"] = _c(inp["L0_ffn_gate_w"].astype(np.float32).T)
    h["gbias"] = _c(inp["L0_ffn_gate_bias"].astype(np.float32).reshape(E0, 1))
    h["egT"] = _c(inp["L0_ffn_eg_w"].astype(np.float32).transpose(0, 2, 1))
    h["euT"] = _c(inp["L0_ffn_eu_w"].astype(np.float32).transpose(0, 2, 1))
    h["edT"] = _c(inp["L0_ffn_ed_w"].astype(np.float32).transpose(0, 2, 1))
    h["gamma"] = _c(inp["L0_attn_gamma"].astype(np.float32).reshape(KC, 128).T)
    wq = inp["L0_attn_w_qkv"].astype(np.float32).reshape(8, 128, 3, NHP, 128)
    h["wqkv0"] = _c(wq.transpose(3, 0, 1, 2, 4))
    h["wo0"] = _c(inp["L0_attn_w_out"].astype(np.float32))
    WT = {}
    for n in ["q", "k", "v", "o"]:
        WT[n] = _c(inp["L1_attn_W" + n].astype(np.float32).T)
        h["A%sT" % n] = _c(inp["L1_attn_A" + n].astype(np.float32).T)
        h["B%sTr" % n] = _c(inp["L1_attn_B" + n].astype(np.float32).T)
    h["WoT"] = WT["o"]
    wq1 = np.stack([WT["q"], WT["k"], WT["v"]], 0).reshape(3, 8, 128, NHP, 128)
    h["wqkv1"] = _c(wq1.transpose(3, 1, 2, 0, 4))
    aq1 = np.stack([h["AqT"], h["AkT"], h["AvT"]], 0)
    apad = np.zeros((D, 96), dtype=np.float32)
    for p in range(3):
        apad[:, p * 32:p * 32 + 8] = aq1[p]
    h["aqkv1"] = _c(apad.reshape(8, 128, 96))
    bq1 = np.stack([h["BqTr"], h["BkTr"], h["BvTr"]], 0).reshape(3, 8, NHP, 128)
    h["bqkv1"] = _c(bq1.transpose(2, 1, 0, 3))
    bb = np.stack([inp["L1_attn_bq"], inp["L1_attn_bk"], inp["L1_attn_bv"]], 0)
    h["biasqkv1"] = _c(bb.astype(np.float32).reshape(3, 8, 128).transpose(1, 2, 0))
    h["bo1"] = _c(inp["L1_attn_bo"].astype(np.float32).reshape(KC, 128).T)
    h["rz"] = _c(np.broadcast_to(
        inp["L1_attn_rezero"].astype(np.float32).reshape(1, 1), (128, 1)))
    h["routerT"] = _c(inp["L1_ffn_router_w"].astype(np.float32).T)
    h["expW"] = _c(inp["L1_ffn_expert_weights"].astype(np.float32))
    h["ident"] = np.eye(128, dtype=np.float32)
    h["identb"] = np.eye(128, dtype=BF)
    sel64 = np.zeros((E0, E0 * 128), dtype=BF)
    for e in range(E0):
        sel64[e, e * 128:(e + 1) * 128] = 1
    h["sel64"] = sel64
    sel8 = np.zeros((E1, E1 * 128), dtype=BF)
    for e in range(E1):
        sel8[e, e * 128:(e + 1) * 128] = 1
    h["sel8"] = sel8
    h["onerow_f"] = np.ones((1, 128), dtype=np.float32)
    o33 = np.zeros((33, 128), dtype=np.float32)
    o33[0] = 1.0; o33[32] = 1.0
    h["ones33"] = o33
    h["ones_f"] = np.ones((128, 128), dtype=np.float32)
    h["ones_b"] = np.ones((128, 128), dtype=BF)
    h["onerow_b"] = np.ones((1, 128), dtype=BF)
    slopes = 0.5 ** np.arange(HEADS, dtype=np.float32)
    blk2 = np.zeros((128, 65), dtype=BF)
    blk2[0:64, 0] = 1.0
    blk2[64:128, 64] = 1.0
    h["blk2"] = blk2
    slp2 = np.ones((65, 2, NHP), dtype=np.float32)
    for hp in range(NHP):
        for hh in range(2):
            slp2[64 * hh, 1, hp] = slopes[hp * 2 + hh] * float(S - 1) + 1.0
    h["slp2"] = _c(slp2)
    pcol = np.arange(128, dtype=np.float32)
    ali_c = np.zeros((128, HEADS, S // 128), dtype=np.float32)
    for hh in range(HEADS):
        for jc in range(S // 128):
            ali_c[:, hh, jc] = slopes[hh] * (jc * 128 + pcol)
    h["alibiC"] = ali_c
    x = inp["L0_attn_x"].astype(np.float32)
    percore = []
    for cid in range(NCORE):
        xs = x[cid * SPC:(cid + 1) * SPC]
        xt = _c(xs.transpose(0, 2, 1)).reshape(SPC, KC, 128, S)
        percore.append({"xT": xt})
    return h, percore



class Ctx:
    def __init__(self, nc, es):
        self.nc = nc
        self.es = es
        self.prog = {e: [] for e in ENGS}
        self.sems = {}
        self.counts = {}
        self.waited = {e: {} for e in ENGS}
        self.ready = {}
        self.freed = {}
        self.bufs = {}
        self.dram = {}
        self.const_cols = {}

    def sem(self, name):
        if name not in self.sems:
            self.sems[name] = self.es.enter_context(self.nc.semaphore(name))
            self.counts[name] = 0
        return self.sems[name]

    def inc(self, name, n=1):
        self.sem(name)
        self.counts[name] += n
        return self.counts[name]

    def emit(self, engine, fn):
        self.prog[engine].append(fn)

    def wait(self, engine, sem_name, val):
        if val is None or val <= 0:
            return
        sm = self.sem(sem_name)
        w = self.waited[engine]
        if w.get(sem_name, 0) >= val:
            return
        w[sem_name] = val
        self.prog[engine].append(lambda eng, s=sm, v=val: eng.wait_ge(s, v))

    def wait_dep(self, engine, dep):
        if dep is not None:
            self.wait(engine, dep[0], dep[1])

    def require(self, engine, res):
        self.wait_dep(engine, self.ready.get(res))

    def acquire(self, engine, res):
        deps = self.freed.get(res)
        if deps:
            for d in deps:
                self.wait_dep(engine, d)

    def produce(self, res, sem_name, val):
        self.ready[res] = (sem_name, val)

    def release(self, res, sem_name, val):
        lst = self.freed.setdefault(res, [])
        for i, (s, v) in enumerate(lst):
            if s == sem_name:
                lst[i] = (sem_name, max(v, val))
                break
        else:
            lst.append((sem_name, val))

    release_add = release

    def barrier(self, engines=("tensor", "vector", "scalar", "gpsimd")):
        bar = self.sem("bar")
        for snm, v in list(self.counts.items()):
            if snm.startswith("wb_") and v:
                self.wait("sync", snm, v)
        tgt = self.counts["bar"] + len(engines) + 1
        self.counts["bar"] = tgt
        dummy = self.bufs["dummy"]
        self.prog["sync"].append(lambda eng, s=bar: eng.nop().then_inc(s, 1))
        for e in engines:
            if e == "tensor":
                self.prog[e].append(lambda eng, s=bar: eng.nop().then_inc(s, 1))
            elif e == "vector":
                self.prog[e].append(lambda eng, s=bar: eng.engine_nop().then_inc(s, 1))
            elif e == "scalar":
                self.prog[e].append(lambda eng, s=bar, d=dummy: eng.memzero(d[0:1, 0:1]).then_inc(s, 1))
            elif e == "gpsimd":
                self.prog[e].append(lambda eng, s=bar, d=dummy: eng.memset(d[0:1, 1:2], 0.0).then_inc(s, 1))
        for e in engines:
            self.wait(e, "bar", tgt)
        self.wait("sync", "bar", tgt)
        for e in ENGS:
            w = self.waited[e]
            for s in ("pe", "dv", "ac", "gp"):
                if s in self.counts:
                    w[s] = max(w.get(s, 0), self.counts[s])
        return ("bar", tgt)


_NAME_CTR = [0]


def _uname(name):
    _NAME_CTR[0] += 1
    return "k%d_%s" % (_NAME_CTR[0], name)


def asb(c, name, shape, dtype, es=None):
    return (es or c.es).enter_context(
        c.nc.sbuf_tensor(_uname(name), list(shape), dtype))


def aps(c, name, shape, dtype=F32, es=None):
    return (es or c.es).enter_context(
        c.nc.psum_tensor(_uname(name), list(shape), dtype))


def t_mm(c, out_ap, lhsT_ap, rhs_ap, start, stop, inc=False):
    if inc:
        c.emit("tensor", lambda eng, o=out_ap, l=lhsT_ap, r=rhs_ap, a=start, b=stop,
               s=c.sem("pe"): eng.matmul(o, l, r, start=a, stop=b).then_inc(s, 1))
        return c.inc("pe")
    c.emit("tensor", lambda eng, o=out_ap, l=lhsT_ap, r=rhs_ap, a=start, b=stop:
           eng.matmul(o, l, r, start=a, stop=b))
    return None


def t_transpose(c, out_ap, in_ap, ident_ap):
    c.emit("tensor", lambda eng, o=out_ap, i=in_ap, d=ident_ap,
           s=c.sem("pe"): eng.transpose(o, i, d).then_inc(s, 1))
    return c.inc("pe")


def v_op(c, fn, inc=True):
    if inc:
        c.emit("vector", lambda eng, f=fn, s=c.sem("dv"): f(eng).then_inc(s, 1))
        return c.inc("dv")
    c.emit("vector", fn)
    return None


def s_op(c, fn, inc=True):
    if inc:
        c.emit("scalar", lambda eng, f=fn, s=c.sem("ac"): f(eng).then_inc(s, 1))
        return c.inc("ac")
    c.emit("scalar", fn)
    return None


def g_op(c, fn, inc=True):
    if inc:
        c.emit("gpsimd", lambda eng, f=fn, s=c.sem("gp"): f(eng).then_inc(s, 1))
        return c.inc("gp")
    c.emit("gpsimd", fn)
    return None


def wb_cache(c, fam, item, src_ap, cast_dep, frees=()):
    c.wait_dep("sync", cast_dep)
    sem = "wb_" + (frees[0] if frees else fam)
    wb = dma(c, "sync", c.wcache[fam][item], src_ap, sem)
    c.wc_written[fam][item] = (sem, wb)
    for r in frees:
        c.release_add(r, sem, wb)
    return wb


def dma(c, queue, out_ap, in_ap, sem_name):
    c.emit(queue, lambda eng, o=out_ap, i=in_ap, s=c.sem(sem_name):
           eng.dma_start(out=o, in_=i).then_inc(s, 16))
    return c.inc(sem_name, 16)



def emit_ln(c, xT, xn, tok0, ntok, sc):
    ones_f = c.bufs["ones_f"]
    p_sum, p_sq = sc["p0"], sc["p1"]
    statB, statB2, sq = sc["statB"], sc["statB2"], sc["sq"]
    ntile = ntok // TOK
    sq_mm = {}
    stats_read = None
    prev_apply = None
    for it in range(ntile):
        t0 = tok0 + it * TOK
        c.require("tensor", "xT")
        c.require("vector", "xT")
        if stats_read is not None:
            c.wait("tensor", "dv", stats_read)
        if prev_apply is not None:
            c.wait("vector", "dv", prev_apply)
        for kc in range(KC):
            par = kc % 2
            gi = it * KC + kc
            if gi >= 2:
                c.wait("vector", "pe", sq_mm[gi - 2])
            vq = v_op(c, lambda eng, o=sq[:, par], i=xT[:, kc, t0:t0 + TOK]:
                      eng.tensor_tensor(o, i, i, op=ALU.mult))
            c.wait("tensor", "dv", vq)
            t_mm(c, p_sum[:], ones_f[:], xT[:, kc, t0:t0 + TOK],
                 start=(kc == 0), stop=(kc == KC - 1))
            sq_mm[gi] = t_mm(c, p_sq[:], ones_f[:], sq[:, par],
                             start=(kc == 0), stop=(kc == KC - 1), inc=True)
        pe_done = c.counts["pe"]
        sB = statB[:, it % 2]
        sB2 = statB2[:, it % 2]
        c.wait("vector", "pe", pe_done)
        if prev_apply is not None:
            c.wait("scalar", "dv", prev_apply)
        v_op(c, lambda eng, o=sB2, i=p_sum[:]:
             eng.tensor_single_scalar(o, i, 1.0 / D, op=ALU.mult))
        v1 = v_op(c, lambda eng, o=sB, i=p_sq[:]:
                  eng.tensor_single_scalar(o, i, 1.0 / D, op=ALU.mult))
        stats_read = v1
        c.wait("scalar", "dv", v1)
        a1 = s_op(c, lambda eng, o=sq[:, 0], i=sB2:
                  eng.activation(out=o, in_=i, func=AF.Square))
        c.wait("vector", "ac", a1)
        v2 = v_op(c, lambda eng, o=sB, a=sB, b=sq[:, 0]: eng.tensor_sub(o, a, b))
        c.wait("scalar", "dv", v2)
        a2 = s_op(c, lambda eng, o=sB, i=sB, b=c.const_cols["eps_ln"]:
                  eng.activation(out=o, in_=i, func=AF.Sqrt, bias=b))
        c.wait("vector", "ac", a2)
        v3 = v_op(c, lambda eng, o=sB, i=sB: eng.reciprocal(o, i))
        c.wait("vector", "dv", v3)
        v4 = v_op(c, lambda eng, o=sB2, i=sB2, r=sB:
                  eng.scalar_tensor_tensor(o, i, -1.0, r, op0=ALU.mult, op1=ALU.mult))
        c.wait("vector", "dv", v4)
        sq_w = {}
        for kc in range(KC):
            par = kc % 2
            if kc >= 2:
                c.wait("vector", "dv", sq_w[kc - 2])
            vv = v_op(c, lambda eng, o=sq[:, par], x=xT[:, kc, t0:t0 + TOK], r=sB:
                      eng.tensor_mul(o, x, r))
            c.wait("vector", "dv", vv)
            sq_w[kc] = v_op(c, lambda eng, o=xn[:, kc, it * TOK:(it + 1) * TOK],
                            a=sq[:, par], b=sB2: eng.tensor_add(o, a, b))
        prev_apply = c.counts["dv"]
    c.produce("xn", "dv", c.counts["dv"])



DBG = {"on": False, "maps": []}


def dbg_dump(c, tag, ap, shape, dtype):
    if not DBG["on"] or (DBG.get("only") and not any(t in tag for t in DBG["only"])):
        return
    nm = "dbg_%s" % tag
    d = c.nc.dram_tensor(nm, list(shape), dtype, kind="ExternalOutput").ap()
    bt = c.barrier()
    c.wait_dep("sync", bt)
    ld = dma(c, "sync", d, ap, "dbgs")
    c.wait("sync", "dbgs", ld)
    for e in ("tensor", "vector", "scalar", "gpsimd"):
        c.wait(e, "dbgs", ld)
    DBG["maps"].append(nm)


def emit_ffn0(c, xT, half):
    tok0 = half * HALF
    with ExitStack() as es:
        xn = asb(c, "f0xn", [128, KC, HALF], BF16, es)[:]
        acc = asb(c, "f0acc", [128, KC, HALF], F32, es)[:]
        combT = asb(c, "f0cT", [E0, HALF], BF16, es)[:]
        sel_sb = asb(c, "f0sel", [E0, 2, 128], BF16, es)[:]
        gps = [aps(c, "f0g%d" % i, [128, TOK], F32, es)[:] for i in range(2)]
        ups = [aps(c, "f0u%d" % i, [128, TOK], F32, es)[:] for i in range(2)]
        p_ed = aps(c, "f0ed", [128, 3, TOK], F32, es)[:]
        p_cb = aps(c, "f0pcb", [128, TOK], F32, es)[:]

        with ExitStack() as bs:
            gate_f = asb(c, "f0gtf", [128, KC, E0], F32, bs)[:]
            gate_b = asb(c, "f0gtb", [128, KC, E0], BF16, bs)[:]
            statB = asb(c, "f0statB", [128, 2, TOK], F32, bs)[:]
            statB2 = asb(c, "f0statB2", [128, 2, TOK], F32, bs)[:]
            sq = asb(c, "f0sq", [128, 2, TOK], F32, bs)[:]
            scoresT = asb(c, "f0scT", [E0, HALF], F32, bs)[:]
            scoresT2 = asb(c, "f0scT2", [E0, HALF], F32, bs)[:]
            stok = asb(c, "f0stok", [128, NTC, E0], F32, bs)[:]
            r8 = asb(c, "f0r8", [128, 4, 8], F32, bs)[:]
            r64 = asb(c, "f0r64", [128, 2, E0], F32, bs)[:]
            r1 = asb(c, "f0r1", [128, 4], F32, bs)[:]
            ctok = asb(c, "f0ctok", [128, E0], F32, bs)[:]

            ldg = dma(c, "sync", gate_f[:], c.dram["gateT"].rearrange(
                "(kc kp) e -> kp kc e", kp=128), "ldw_gate")
            c.wait("scalar", "ldw_gate", ldg)
            a_g = s_op(c, lambda eng, o=gate_b[:], i=gate_f[:]:
                       eng.activation(out=o, in_=i, func=AF.Copy))

            ln_sc = {"p0": gps[0], "p1": gps[1], "statB": statB,
                     "statB2": statB2, "sq": sq}
            emit_ln(c, xT, xn, tok0, HALF, ln_sc)
            dbg_dump(c, "xn_h%d" % half, xn, [128, KC, HALF], BF16)

            c.require("tensor", "xn")
            c.wait("tensor", "ac", a_g)
            lg_done = []
            for it in range(NT):
                last = None
                if lg_done:
                    c.wait("tensor", "ac", lg_done[-1])
                for kc in range(KC):
                    last = t_mm(c, p_cb[0:E0, :], gate_b[:, kc],
                                xn[:, kc, it * TOK:(it + 1) * TOK],
                                start=(kc == 0), stop=(kc == KC - 1),
                                inc=(kc == KC - 1))
                c.wait("scalar", "pe", last)
                a1 = s_op(c, lambda eng, o=scoresT[:, it * TOK:(it + 1) * TOK],
                          i=p_cb[0:E0, :]: eng.activation(out=o, in_=i, func=AF.Sigmoid))
                lg_done.append(a1)
            c.wait("vector", "ac", lg_done[-1])
            gb = c.bufs["gbias"]
            vsc = v_op(c, lambda eng, o=scoresT2[:], i=scoresT[:], s=gb[:]:
                       eng.tensor_scalar(o, i, s, None, op0=ALU.add))
            ident = c.bufs["ident"]
            c.wait("tensor", "dv", vsc)
            tr_done = {}
            for tc in range(NTC):
                if tc >= 2:
                    c.wait("tensor", "ac", tr_done[tc - 2])
                pe1 = t_transpose(c, p_ed[:, tc % 2, 0:E0],
                                  scoresT2[:, tc * 128:(tc + 1) * 128],
                                  ident[0:E0, 0:E0])
                c.wait("scalar", "pe", pe1)
                tr_done[tc] = s_op(c, lambda eng, o=stok[:, tc], i=p_ed[:, tc % 2, 0:E0]:
                                   eng.activation(out=o, in_=i, func=AF.Copy))
            last_cT = None
            last_pe2 = None
            for tc in range(NTC):
                c.wait("vector", "ac", tr_done[tc])
                st_ap = stok[:, tc]
                v1 = v_op(c, lambda eng, o=r8[:, 0],
                          i=st_ap.rearrange("p (g j) -> p g j", g=8):
                          eng.reduce_max(o, i, axis=AX.X))
                c.wait("vector", "dv", v1)
                v2 = v_op(c, lambda eng, o=r8[:, 1], i=r8[:, 0]: eng.max(out=o, in_=i))
                c.wait("vector", "dv", v2)
                v3 = v_op(c, lambda eng, o=r8[:, 2], i=r8[:, 0], s=r8[:, 1, 3:4]:
                          eng.tensor_scalar(o, i, s, None, op0=ALU.is_ge))
                c.wait("vector", "dv", v3)
                for g in range(8):
                    v_op(c, lambda eng, o=r64[:, 0, g * 8:(g + 1) * 8],
                         i=st_ap[:, g * 8:(g + 1) * 8], s=r8[:, 2, g:g + 1]:
                         eng.tensor_scalar(o, i, s, None, op0=ALU.mult),
                         inc=(g == 7))
                c.wait("vector", "dv", c.counts["dv"])
                v5 = v_op(c, lambda eng, o=r8[:, 3], i=r64[:, 0]: eng.max(out=o, in_=i))
                c.wait("vector", "dv", v5)
                v6 = v_op(c, lambda eng, o=r64[:, 1], i=r64[:, 0], s=r8[:, 3, 5:6]:
                          eng.tensor_scalar(o, i, s, None, op0=ALU.is_ge))
                c.wait("vector", "dv", v6)
                v7 = v_op(c, lambda eng, o=r64[:, 1], a=r64[:, 1], b=r64[:, 0]:
                          eng.tensor_mul(o, a, b))
                c.wait("vector", "dv", v7)
                v8 = v_op(c, lambda eng, o=r1[:, 0:1], i=r64[:, 1]:
                          eng.reduce_sum(o, i, axis=AX.X))
                c.wait("vector", "dv", v8)
                v9 = v_op(c, lambda eng, o=r1[:, 1:2], i=r1[:, 0:1]:
                          eng.tensor_single_scalar(o, i, 1e-8, op=ALU.add))
                c.wait("vector", "dv", v9)
                v10 = v_op(c, lambda eng, o=r1[:, 2:3], i=r1[:, 1:2]:
                           eng.reciprocal(o, i))
                c.wait("vector", "dv", v10)
                if last_pe2 is not None:
                    c.wait("vector", "pe", last_pe2)
                v11 = v_op(c, lambda eng, o=ctok[:], i=r64[:, 1], s=r1[:, 2:3]:
                           eng.tensor_scalar(o, i, s, None, op0=ALU.mult))
                c.wait("tensor", "dv", v11)
                if last_cT is not None:
                    c.wait("tensor", "ac", last_cT)
                pe2 = t_transpose(c, p_ed[0:E0, 2, 0:128], ctok, ident[:])
                last_pe2 = pe2
                c.wait("scalar", "pe", pe2)
                last_cT = s_op(c, lambda eng, o=combT[:, tc * 128:(tc + 1) * 128],
                               i=p_ed[0:E0, 2, 0:128]:
                               eng.activation(out=o, in_=i, func=AF.Copy))
            c.produce("combT", "ac", last_cT)
            dbg_dump(c, "scoresT_h%d" % half, scoresT2, [E0, HALF], F32)
            dbg_dump(c, "combT_h%d" % half, combT, [E0, HALF], BF16)
            rt_dv = c.counts["dv"]; rt_ac = c.counts["ac"]; rt_pe = c.counts["pe"]
            for r in ("wgu0", "wgu1", "wd0", "wd1", "stage0", "stage1"):
                c.release_add(r, "dv", rt_dv)
                c.release_add(r, "ac", rt_ac)
                c.release_add(r, "pe", rt_pe)
            for b in range(3):
                c.release_add("ped%d" % b, "ac", rt_ac)
                c.release_add("ped%d" % b, "dv", rt_dv)
            c.release_add("pcb", "ac", rt_ac)
            for p in range(2):
                c.release_add("gps%d" % p, "dv", rt_dv)
                c.release_add("ups%d" % p, "dv", rt_dv)

        wg = asb(c, "f0wg", [128, 2, KC * 512], BF16, es)[:]
        wu = asb(c, "f0wu", [128, 2, KC * 512], BF16, es)[:]
        wd = asb(c, "f0wd", [128, 2, IC0 * 1024], BF16, es)[:]
        stage = asb(c, "f0st", [128, 2, 1024], F32, es)[:]
        hst = asb(c, "f0h", [128, 8, TOK], BF16, es)[:]
        sg = asb(c, "f0sg", [128, 2, TOK], F32, es)[:]
        ht = asb(c, "f0ht", [128, 2, TOK], F32, es)[:]
        cbs = asb(c, "f0cb", [128, 2, TOK], BF16, es)[:]

        stream_ctr = [0]

        def stream_chunk(dst_ap, src_ap, key=None):
            sslot = stream_ctr[0] % 2
            stream_ctr[0] += 1
            c.acquire("sync", "stage%d" % sslot)
            dst_stage = stage[:, sslot, :]
            if len(src_ap.shape) == 3:
                dst_stage = dst_stage.rearrange("p (q m) -> p q m",
                                                q=src_ap.shape[1])
            ld = dma(c, "sync", dst_stage, src_ap, "ldw_st%d" % sslot)
            c.wait("gpsimd", "ldw_st%d" % sslot, ld)
            if key:
                c.acquire("gpsimd", key)
            av = g_op(c, lambda eng, o=dst_ap, i=stage[:, sslot, :]:
                      eng.tensor_copy(o, i))
            c.release("stage%d" % sslot, "gp", av)
            return av

        wg_f = wg.rearrange("p s m -> p (s m)")
        wu_f = wu.rearrange("p s m -> p (s m)")
        wd_f = wd.rearrange("p s m -> p (s m)")
        ld_sh = None
        for i, (dst, drn) in enumerate([(wg_f, "sgT"), (wu_f, "suT"),
                                        (wd_f, "sdT")]):
            for r in ("wgu0", "wgu1", "wd0", "wd1"):
                c.acquire("sync", r)
            if i in c.wc_written["sh"]:
                c.wait_dep("sync", c.wc_written["sh"][i])
                ldv = dma(c, "sync", dst, c.wcache["sh"][i], "ldshc")
                ld_sh = ("ldshc", ldv)
            else:
                last = None
                for ch in range(8):
                    last = stream_chunk(dst[:, ch * 1024:(ch + 1) * 1024],
                                        c.dram[drn][ch * 128:(ch + 1) * 128])
                wb_cache(c, "sh", i, dst, ("gp", last), frees=("wgu0", "wgu1", "wd0", "wd1"))
                ld_sh = ("gp", last)
        c.wait_dep("tensor", ld_sh)
        acc_last = {}
        h_done = {}
        for it in range(NT):
            ts = slice(it * TOK, (it + 1) * TOK)
            for ic in range(8):
                par = ic % 2
                c.acquire("tensor", "gps%d" % par)
                c.acquire("tensor", "ups%d" % par)
                for kc in range(KC):
                    t_mm(c, gps[par][:], wg_f[:, kc * 1024 + ic * 128: kc * 1024 + ic * 128 + 128],
                         xn[:, kc, ts], start=(kc == 0), stop=(kc == KC - 1))
                last_u = None
                for kc in range(KC):
                    last_u = t_mm(c, ups[par][:], wu_f[:, kc * 1024 + ic * 128: kc * 1024 + ic * 128 + 128],
                                  xn[:, kc, ts], start=(kc == 0), stop=(kc == KC - 1),
                                  inc=(kc == KC - 1))
                c.wait("scalar", "pe", last_u)
                c.acquire("scalar", "sg%d" % par)
                a_s = s_op(c, lambda eng, o=sg[:, par], i=gps[par][:]:
                           eng.activation(out=o, in_=i, func=AF.Silu))
                c.release("gps%d" % par, "ac", a_s)
                c.wait("vector", "ac", a_s)
                c.acquire("vector", "hst%d" % ic)
                v_h = v_op(c, lambda eng, o=hst[:, ic], a=sg[:, par], b=ups[par][:]:
                           eng.tensor_mul(o, a, b))
                c.release("ups%d" % par, "dv", v_h)
                c.release("sg%d" % par, "dv", v_h)
                h_done[ic] = v_h
            c.wait("tensor", "dv", h_done[7])
            for hc in range(KC):
                bank = hc % 3
                c.acquire("tensor", "ped%d" % bank)
                last = None
                for ic in range(8):
                    last = t_mm(c, p_ed[:, bank, :],
                                wd_f[:, ic * 1024 + hc * 128: ic * 1024 + hc * 128 + 128],
                                hst[:, ic], start=(ic == 0), stop=(ic == 7),
                                inc=(ic == 7))
                c.wait("scalar", "pe", last)
                a_i = s_op(c, lambda eng, o=acc[:, hc, ts], i=p_ed[:, bank, :]:
                           eng.activation(out=o, in_=i, func=AF.Copy))
                c.release("ped%d" % bank, "ac", a_i)
                acc_last[(hc, it)] = ("ac", a_i)
            pe_now = c.counts["pe"]
            for ic in range(8):
                c.release("hst%d" % ic, "pe", pe_now)

        dbg_dump(c, "accsh_h%d" % half, acc, [128, KC, HALF], F32)

        def stream_expert_gu(e):
            slot = e % 2
            key = "wgu%d" % slot
            if e in c.wc_written["eg"]:
                c.wait_dep("sync", c.wc_written["eg"][e])
                c.wait_dep("sync", c.wc_written["eu"][e])
                c.acquire("sync", key)
                dma(c, "sync", wg[:, slot], c.wcache["eg"][e], "ldgu%d" % slot)
                ldv = dma(c, "sync", wu[:, slot], c.wcache["eu"][e],
                          "ldgu%d" % slot)
                last = ("ldgu%d" % slot, ldv)
            else:
                lv = None
                for ch in range(IC0):
                    lv = stream_chunk(wg[:, slot, ch * 1024:(ch + 1) * 1024],
                                      c.dram["egT"][e, ch * 256:(ch + 1) * 256].rearrange(
                                          "(q kp) m -> kp q m", kp=128), key)
                wb_cache(c, "eg", e, wg[:, slot], ("gp", lv), frees=("wgu%d" % slot,))
                for ch in range(IC0):
                    lv = stream_chunk(wu[:, slot, ch * 1024:(ch + 1) * 1024],
                                      c.dram["euT"][e, ch * 256:(ch + 1) * 256].rearrange(
                                          "(q kp) m -> kp q m", kp=128), key)
                wb_cache(c, "eu", e, wu[:, slot], ("gp", lv), frees=("wgu%d" % slot,))
                last = ("gp", lv)
            c.acquire("sync", "sel%d" % slot)
            lds = dma(c, "sync", sel_sb[:, slot, :],
                      c.dram["sel64"][:, e * 128:(e + 1) * 128],
                      "ldw_sel%d" % slot)
            return last, lds

        def stream_expert_ed(e):
            slot = e % 2
            key = "wd%d" % slot
            if e in c.wc_written["ed"]:
                c.wait_dep("sync", c.wc_written["ed"][e])
                c.acquire("sync", key)
                ldv = dma(c, "sync", wd[:, slot], c.wcache["ed"][e],
                          "lded%d" % slot)
                return ("lded%d" % slot, ldv)
            lv = None
            for ch in range(IC0):
                lv = stream_chunk(wd[:, slot, ch * 1024:(ch + 1) * 1024],
                                  c.dram["edT"][e, ch * 128:(ch + 1) * 128], key)
            wb_cache(c, "ed", e, wd[:, slot], ("gp", lv), frees=("wd%d" % slot,))
            return ("gp", lv)

        pe_sh = c.counts["pe"]
        for s in (0, 1):
            c.release("wgu%d" % s, "pe", pe_sh)
            c.release("wd%d" % s, "pe", pe_sh)
        gu_done = {}
        ed_cast = {}
        sel_done = {}
        gu_done[0], sel_done[0] = stream_expert_gu(0)
        ed_cast[0] = stream_expert_ed(0)
        gu_done[1], sel_done[1] = stream_expert_gu(1)
        ed_cast[1] = stream_expert_ed(1)
        hprime = {}

        def emit_ed(e, it):
            ts = slice(it * TOK, (it + 1) * TOK)
            slot = e % 2
            jslot = (e * NT + it) % 2
            c.wait_dep("tensor", ed_cast[e])
            c.wait("tensor", "dv", hprime[(e, it)])
            for hc in range(KC):
                bank = hc % 3
                c.acquire("tensor", "ped%d" % bank)
                last = None
                for ic in range(IC0):
                    last = t_mm(c, p_ed[:, bank, :],
                                wd[:, slot, ic * 1024 + hc * 128: ic * 1024 + hc * 128 + 128],
                                hst[:, jslot * 4 + ic],
                                start=(ic == 0), stop=(ic == IC0 - 1),
                                inc=(ic == IC0 - 1))
                c.wait("vector", "pe", last)
                c.wait_dep("vector", acc_last.get((hc, it)))
                vadd = v_op(c, lambda eng, o=acc[:, hc, ts], a=acc[:, hc, ts],
                            b=p_ed[:, bank, :]: eng.tensor_add(o, a, b))
                c.release("ped%d" % bank, "dv", vadd)
                acc_last[(hc, it)] = ("dv", vadd)
            pe_now = c.counts["pe"]
            for ic in range(IC0):
                c.release("hst%d" % (jslot * 4 + ic), "pe", pe_now)
            if it == NT - 1:
                c.release("wd%d" % slot, "pe", pe_now)
                c.release("sel%d" % slot, "pe", pe_now)

        jobs = [(e, it) for e in range(E0) for it in range(NT)]
        for ji, (e, it) in enumerate(jobs):
            slot = e % 2
            jslot = ji % 2
            ts = slice(it * TOK, (it + 1) * TOK)
            if it == 0:
                c.wait_dep("tensor", gu_done[e])
                c.wait("tensor", "ldw_sel%d" % slot, sel_done[e])
            c.acquire("tensor", "pcb")
            c.require("tensor", "combT")
            cbm = t_mm(c, p_cb[:], sel_sb[:, slot, :], combT[:, ts],
                       start=True, stop=True, inc=True)
            c.wait("scalar", "pe", cbm)
            c.acquire("scalar", "cbs%d" % jslot)
            cbe = s_op(c, lambda eng, o=cbs[:, jslot], i=p_cb[:]:
                       eng.activation(out=o, in_=i, func=AF.Copy))
            c.release("pcb", "ac", cbe)
            for ic in range(IC0):
                par = ic % 2
                c.acquire("tensor", "gps%d" % par)
                c.acquire("tensor", "ups%d" % par)
                for kc in range(KC):
                    t_mm(c, gps[par][:], wg[:, slot, kc * 512 + ic * 128: kc * 512 + ic * 128 + 128],
                         xn[:, kc, ts], start=(kc == 0), stop=(kc == KC - 1))
                last_u = None
                for kc in range(KC):
                    last_u = t_mm(c, ups[par][:], wu[:, slot, kc * 512 + ic * 128: kc * 512 + ic * 128 + 128],
                                  xn[:, kc, ts], start=(kc == 0), stop=(kc == KC - 1),
                                  inc=(kc == KC - 1))
                c.wait("scalar", "pe", last_u)
                c.acquire("scalar", "sg%d" % par)
                a_s = s_op(c, lambda eng, o=sg[:, par], i=gps[par][:]:
                           eng.activation(out=o, in_=i, func=AF.Silu))
                c.release("gps%d" % par, "ac", a_s)
                c.wait("vector", "ac", a_s)
                c.acquire("vector", "ht%d" % par)
                v_h = v_op(c, lambda eng, o=ht[:, par], a=sg[:, par], b=ups[par][:]:
                           eng.tensor_mul(o, a, b))
                c.release("ups%d" % par, "dv", v_h)
                c.release("sg%d" % par, "dv", v_h)
                c.wait("vector", "dv", v_h)
                c.wait("vector", "ac", cbe)
                c.acquire("vector", "hst%d" % (jslot * 4 + ic))
                g_h = v_op(c, lambda eng, o=hst[:, jslot * 4 + ic], a=ht[:, par],
                           b=cbs[:, jslot]: eng.tensor_tensor(o, a, b, op=ALU.mult))
                c.release("ht%d" % par, "dv", g_h)
                if ic == IC0 - 1:
                    c.release("cbs%d" % jslot, "dv", g_h)
                    hprime[(e, it)] = g_h
            if it == NT - 1:
                pe_now = c.counts["pe"]
                c.release("wgu%d" % slot, "pe", pe_now)
                c.release("sel%d" % slot, "pe", pe_now)
                if e + 2 < E0:
                    gu_done[e + 2], sel_done[e + 2] = stream_expert_gu(e + 2)
            if ji >= 1:
                pe_, pit_ = jobs[ji - 1]
                emit_ed(pe_, pit_)
                if pit_ == NT - 1 and pe_ + 2 < E0:
                    ed_cast[pe_ + 2] = stream_expert_ed(pe_ + 2)
        emit_ed(*jobs[-1])

        c.wait("vector", "dv", c.counts["dv"])
        for kc in range(KC):
            v_op(c, lambda eng, o=xT[:, kc, tok0:tok0 + HALF],
                 a=acc[:, kc, :], x2=xT[:, kc, tok0:tok0 + HALF]:
                 eng.scalar_tensor_tensor(o, a, 0.5, x2, op0=ALU.mult, op1=ALU.add))
        dbg_dump(c, "acc_h%d" % half, acc, [128, KC, HALF], F32)
        c.produce("xT", "dv", c.counts["dv"])
        c.barrier()


NQC = S // 128
NJC = S // 128
QT4 = S // TOK
AVT = 128
NAV = S // AVT


_ATTN_UID = [0]


def emit_attn(c, xT, layer):
    _ATTN_UID[0] += 1
    uid = _ATTN_UID[0]
    ident = c.bufs["ident"]
    identb = c.bufs["identb"]
    ones_f = c.bufs["ones_f"]
    with ExitStack() as es:
        xn = asb(c, "atxn", [128, KC, S], BF16, es)[:]
        attnout = asb(c, "atao", [128, NHP, S], BF16, es)[:]
        qt = asb(c, "atqkv", [128, 3, S], BF16, es)[:]
        vtm = asb(c, "atvtm", [128, NJC, 130], BF16, es)[:]
        wqf = asb(c, "atwqf", [128, KC, 128], F32, es)[:]
        wqb = asb(c, "atwqb", [128, KC, 384], BF16, es)[:]
        Ss = asb(c, "atSs", [128, S], F32, es)[:]
        atile = asb(c, "atat", [128, 2, S], BF16, es)[:]
        sqb = asb(c, "atsqb", [128, S], BF16, es)[:]
        mstat = asb(c, "atmst", [65, 16], F32, es)[:]
        alCM = asb(c, "atalcm", [128, NJC], F32, es)[:]
        orec = asb(c, "atorec", [1, 2, TOK], F32, es)[:]
        nrm = asb(c, "atnrm", [64, 2, TOK], F32, es)[:]
        if layer == 1:
            xab = asb(c, "atxab", [96, S], BF16, es)[:]
            aqb = asb(c, "ataqb", [128, KC, 96], BF16, es)[:]
            bqf = asb(c, "atbqf", [72, 128], F32, es)[:]
            bqb = asb(c, "atbqb", [72, 128], BF16, es)[:]
            bias_sb = asb(c, "atbias", [128, 3], F32, es)[:]
            aob = asb(c, "ataob", [128, KC, 8], BF16, es)[:]
            xaO = xab[0:8, :]
        PS = aps(c, "atPS", [128, 8, TOK], F32, es)[:]
        psA = PS[:, 0, :]
        psB = PS[:, 1, :]
        psE = PS[:, 2, :]

        von = v_op(c, lambda eng, o=vtm[:, :, 64:65]: eng.memset(o, 1.0))
        von = v_op(c, lambda eng, o=vtm[:, :, 129:130]: eng.memset(o, 1.0))

        with ExitStack() as bs:
            statB = asb(c, "atstatB", [128, 2, TOK], F32, bs)[:]
            statB2 = asb(c, "atstatB2", [128, 2, TOK], F32, bs)[:]
            sq = asb(c, "atsq", [128, 2, TOK], F32, bs)[:]
            ln_sc = {"p0": psA, "p1": psB, "statB": statB, "statB2": statB2,
                     "sq": sq}
            emit_ln(c, xT, xn, 0, S, ln_sc)

            if layer == 0:
                c.require("vector", "xn")
                gp1 = c.bufs["gp1"]
                sqr_mm = {}
                stats_read = None
                prev_apply = None
                for it in range(QT4):
                    t0 = it * TOK
                    if stats_read is not None:
                        c.wait("tensor", "dv", stats_read)
                    if prev_apply is not None:
                        c.wait("vector", "dv", prev_apply)
                    for kc in range(KC):
                        par = kc % 2
                        gi = it * KC + kc
                        if gi >= 2:
                            c.wait("vector", "pe", sqr_mm[gi - 2])
                        vq = v_op(c, lambda eng, o=sq[:, par], i=xn[:, kc, t0:t0 + TOK]:
                                  eng.tensor_tensor(o, i, i, op=ALU.mult))
                        c.wait("tensor", "dv", vq)
                        sqr_mm[gi] = t_mm(c, psA[:], c.bufs["ones_f"][:], sq[:, par],
                                          start=(kc == 0), stop=(kc == KC - 1),
                                          inc=True)
                    pe_done = c.counts["pe"]
                    sB = statB[:, it % 2]
                    c.wait("scalar", "pe", pe_done)
                    if prev_apply is not None:
                        c.wait("scalar", "dv", prev_apply)
                    a1 = s_op(c, lambda eng, o=sB, i=psA[:]:
                              eng.activation(out=o, in_=i, func=AF.Sqrt))
                    c.wait("vector", "ac", a1)
                    v1 = v_op(c, lambda eng, o=sB, i=sB:
                              eng.tensor_single_scalar(o, i, 1e-12, op=ALU.max))
                    c.wait("vector", "dv", v1)
                    v2 = v_op(c, lambda eng, o=sB, i=sB: eng.reciprocal(o, i))
                    stats_read = v2
                    c.wait("vector", "dv", v2)
                    for kc in range(KC):
                        par = kc % 2
                        vv = v_op(c, lambda eng, o=sq[:, par], x=xn[:, kc, t0:t0 + TOK],
                                  r=sB: eng.tensor_mul(o, x, r))
                        c.wait("vector", "dv", vv)
                        v_op(c, lambda eng, o=xn[:, kc, t0:t0 + TOK], i=sq[:, par],
                             s=gp1[:, kc:kc + 1]:
                             eng.tensor_scalar(o, i, s, None, op0=ALU.mult))
                    prev_apply = c.counts["dv"]
                c.produce("xn", "dv", c.counts["dv"])
            c.barrier()

        xn_dep = c.ready["xn"]
        wname = "wqkv0" if layer == 0 else "wqkv1"
        if layer == 1:
            aqf_v = wqf.rearrange("p a b -> p (a b)")[:, 0:96 * KC].rearrange(
                "p (a b) -> p a b", a=KC)
            lda = dma(c, "sync", aqf_v, c.dram["aqkv1"].rearrange(
                "kc kp m -> kp kc m"), "ldw_aq")
            c.wait("gpsimd", "ldw_aq", lda)
            ga = g_op(c, lambda eng, o=aqb[:], i=aqf_v: eng.tensor_copy(o, i))
            aof_v = wqf.rearrange("p a b -> p (a b)")[:, 0:8 * KC].rearrange(
                "p (a b) -> p a b", a=KC)
            c.wait("sync", "gp", ga)
            ldao = dma(c, "sync", aof_v, c.dram["AoT"].rearrange(
                "(kc kp) r -> kp kc r", kp=128), "ldw_ao")
            c.wait("gpsimd", "ldw_ao", ldao)
            ga2a = g_op(c, lambda eng, o=aob[:], i=aof_v: eng.tensor_copy(o, i))
            c.release("atwqf", "gp", ga2a)

        for hp in range(NHP):
            qfam = "qkv%d" % layer
            if hp in c.wc_written[qfam]:
                c.wait_dep("sync", c.wc_written[qfam][hp])
                c.acquire("sync", "atwqb")
                ldq = dma(c, "sync", wqb.rearrange("p a b -> p (a b)"),
                          c.wcache[qfam][hp], "ldqkvc")
                gw_dep = ("ldqkvc", ldq)
            else:
                c.acquire("gpsimd", "atwqb")
                gw = None
                for p in range(3):
                    c.acquire("sync", "atwqf")
                    ldw = dma(c, "sync", wqf[:], c.dram[wname][hp, :, :, p].rearrange(
                        "kc kp m -> kp kc m"), "ldw_qkv")
                    c.wait("gpsimd", "ldw_qkv", ldw)
                    gw = g_op(c, lambda eng, o=wqb[:, :, p * 128:(p + 1) * 128], i=wqf[:]:
                              eng.tensor_copy(o, i))
                    c.release("atwqf", "gp", gw)
                wb_cache(c, qfam, hp, wqb.rearrange("p a b -> p (a b)"),
                         ("gp", gw), frees=("atwqb",))
                gw_dep = ("gp", gw)
            if layer == 1:
                c.acquire("sync", "atbq")
                for p in range(3):
                    dma(c, "sync", bqf[32 * p:32 * p + 8, :],
                        c.dram["bqkv1"][hp, :, p, :], "ldw_bq")
                ldbias = dma(c, "sync", bias_sb[:], c.dram["biasqkv1"][hp], "ldw_bq")
                c.wait("gpsimd", "ldw_bq", ldbias)
                gb = None
                for p in range(3):
                    gb = g_op(c, lambda eng, o=bqb[32 * p:32 * p + 8, :],
                              i=bqf[32 * p:32 * p + 8, :]: eng.tensor_copy(o, i))
            c.wait_dep("tensor", xn_dep)
            c.wait_dep("tensor", gw_dep)
            if layer == 1:
                c.wait("tensor", "gp", gb)
                c.wait("tensor", "gp", ga)
                c.wait("scalar", "ldw_bq", ldbias)
                for it in range(QT4):
                    ts = slice(it * TOK, (it + 1) * TOK)
                    c.acquire("tensor", "atb2")
                    last = None
                    for kc in range(KC):
                        last = t_mm(c, psE[0:96, :], aqb[:, kc, :], xn[:, kc, ts],
                                    start=(kc == 0), stop=(kc == KC - 1),
                                    inc=(kc == KC - 1))
                    c.wait("scalar", "pe", last)
                    ax = s_op(c, lambda eng, o=xab[:, ts], i=psE[0:96, :]:
                              eng.activation(out=o, in_=i, func=AF.Copy, scale=0.125))
                    c.release("atb2", "ac", ax)
                xab_ready = c.counts["ac"]
                c.wait("tensor", "ac", xab_ready)
            for p in range(3):
                for it in range(QT4):
                    ts = slice(it * TOK, (it + 1) * TOK)
                    bank, bname = (psA, "atb0") if (p * QT4 + it) % 2 == 0 else (psB, "atb1")
                    c.acquire("tensor", bname)
                    nk = KC if layer == 1 else KC - 1
                    for kc in range(nk):
                        t_mm(c, bank[:], wqb[:, kc, p * 128:(p + 1) * 128],
                             xn[:, kc, ts], start=(kc == 0), stop=False)
                    if layer == 1:
                        last = t_mm(c, bank[:], bqb[32 * p:32 * p + 8, :],
                                    xab[32 * p:32 * p + 8, ts],
                                    start=False, stop=True, inc=True)
                    else:
                        last = t_mm(c, bank[:], wqb[:, KC - 1, p * 128:(p + 1) * 128],
                                    xn[:, KC - 1, ts], start=False, stop=True, inc=True)
                    if layer == 0:
                        c.wait("scalar", "pe", last)
                        scale = 0.125 if p == 0 else 1.0
                        ev = s_op(c, lambda eng, o=qt[:, p, ts], i=bank[:], sc=scale:
                                  eng.activation(out=o, in_=i, func=AF.Copy, scale=sc))
                    else:
                        c.wait("vector", "pe", last)
                        c.wait("vector", "ldw_bq", ldbias)
                        ev = v_op(c, lambda eng, o=qt[:, p, ts], i=bank[:],
                                  b=bias_sb[:, p:p + 1]:
                                  eng.tensor_scalar(o, i, b, None, op0=ALU.add))
                        c.release(bname, "dv", ev)
                        continue
                    c.release(bname, "ac", ev)
            c.release("atwqb", "pe", c.counts["pe"])
            if layer == 1:
                c.release("atbq", "ac", c.counts["ac"])
            c.wait("tensor", "ac", c.counts["ac"])
            c.wait("tensor", "dv", c.counts["dv"])
            for jc in range(NJC):
                bname = "atb%d" % (2 + jc % 2)
                c.acquire("tensor", bname)
                pv = t_mm(c, PS[:, 2 + jc % 2, 0:128],
                          qt[:, 2, jc * 128:(jc + 1) * 128], identb[:],
                          start=True, stop=True, inc=True)
                c.wait("vector", "pe", pv)
                c.wait("vector", "dv", von)
                v_op(c, lambda eng, o=vtm[:, jc, 0:64], i=PS[:, 2 + jc % 2, 0:64]:
                     eng.tensor_copy(o, i))
                v2 = v_op(c, lambda eng, o=vtm[:, jc, 65:129],
                          i=PS[:, 2 + jc % 2, 64:128]: eng.tensor_copy(o, i))
                c.release(bname, "dv", v2)
            vtm_ready = c.counts["dv"]
            blk2 = c.bufs["blk2"]
            c.wait("vector", "ac", c.counts["ac"])
            c.wait("vector", "dv", c.counts["dv"])
            sq_done = None
            for side in range(2):
                if sq_done is not None:
                    c.wait("vector", "pe", sq_done)
                vq = v_op(c, lambda eng, o=sqb[:], i=qt[:, side, :]:
                          eng.tensor_tensor(o, i, i, op=ALU.mult))
                c.wait("tensor", "dv", vq)
                for t in range(4):
                    c.acquire("tensor", "atb%d" % (4 + t))
                    sq_done = t_mm(c, PS[0:65, 4 + t, :], blk2[:, 0:65],
                                   sqb[:, t * TOK:(t + 1) * TOK],
                                   start=True, stop=True, inc=True)
                c.wait("vector", "pe", sq_done)
                for t in range(4):
                    vr = v_op(c, lambda eng,
                              o=mstat[:, side * 5 + t:side * 5 + t + 1],
                              i=PS[0:65, 4 + t, :]: eng.reduce_max(o, i, axis=AX.X))
                    c.release("atb%d" % (4 + t), "dv", vr)
                c.wait("vector", "dv", c.counts["dv"])
                v_op(c, lambda eng, o=mstat[:, side * 5 + 4:side * 5 + 5],
                     i=mstat[:, side * 5:side * 5 + 4]:
                     eng.reduce_max(o, i, axis=AX.X))
            c.wait("vector", "dv", c.counts["dv"])
            vmul = v_op(c, lambda eng, o=mstat[:, 10:11], a=mstat[:, 4:5],
                        b=mstat[:, 9:10]:
                        eng.tensor_scalar(o, a, b, None, op0=ALU.mult))
            c.wait("scalar", "dv", vmul)
            asq = s_op(c, lambda eng, o=mstat[:, 11:12], i=mstat[:, 10:11]:
                       eng.activation(out=o, in_=i, func=AF.Sqrt))
            c.wait("vector", "ac", asq)
            slp = c.bufs["slp2"]
            vB = v_op(c, lambda eng, o=mstat[:, 12:13], i=mstat[:, 11:12],
                      s=slp[:, layer, hp:hp + 1]:
                      eng.tensor_scalar(o, i, s, None, op0=ALU.add))
            if hp == 0:
                dbg_dump(c, "qt_u%d" % uid, qt, [128, 3, S], BF16)
                dbg_dump(c, "vtm_u%d" % uid, vtm, [128, NJC, 130], BF16)
                dbg_dump(c, "mst_u%d" % uid, mstat, [65, 16], F32)
            for hh in range(2):
                h = hp * 2 + hh
                r0 = 64 * hh
                c.wait("tensor", "dv", vB)
                c.acquire("tensor", "atb2")
                pb = t_mm(c, PS[:, 2, 0:1], ones_f[r0:r0 + 1, 0:128],
                          mstat[r0:r0 + 1, 12:13], start=True, stop=True, inc=True)
                c.wait("vector", "pe", pb)
                if layer == 1:
                    alC = c.bufs["alibiC"]
                    vbias = v_op(c, lambda eng, o=alCM[:, 0:NJC], i=alC[:, h, :],
                                 s=PS[:, 2, 0:1]:
                                 eng.tensor_scalar(o, i, s, None, op0=ALU.subtract))
                else:
                    vbias = v_op(c, lambda eng, o=alCM[:, 0:1], i=PS[:, 2, 0:1]:
                                 eng.tensor_single_scalar(o, i, -1.0, op=ALU.mult))
                c.release("atb2", "dv", vbias)
                c.wait("scalar", "dv", vbias)
                if hp == 0 and hh == 0:
                    dbg_dump(c, "alcm_u%d" % uid, alCM, [128, NJC], F32)
                c.wait("scalar", "dv", vtm_ready)
                c.wait("tensor", "dv", vtm_ready)
                pe_st = {}
                pe_av = {}
                av15 = {}
                exa = {}
                exb = {}
                for jc in range(NJC + 1):
                    if jc > 0:
                        pj = jc - 1
                        sl = pj % 2
                        c.wait("tensor", "ac", exa[pj])
                        for t4 in (0, 1):
                            c.acquire("tensor", "atb%d" % (4 + t4))
                            v = t_mm(c, PS[0:65, 4 + t4, :],
                                     vtm[:, pj, hh * 65:hh * 65 + 65],
                                     atile[:, sl, t4 * TOK:(t4 + 1) * TOK],
                                     start=(pj == 0), stop=(pj == NJC - 1),
                                     inc=(t4 == 1 or pj == NJC - 1))
                            if pj == NJC - 1:
                                av15[t4] = v
                        c.wait("tensor", "ac", exb[pj])
                        for t4 in (2, 3):
                            c.acquire("tensor", "atb%d" % (4 + t4))
                            v = t_mm(c, PS[0:65, 4 + t4, :],
                                     vtm[:, pj, hh * 65:hh * 65 + 65],
                                     atile[:, sl, t4 * TOK:(t4 + 1) * TOK],
                                     start=(pj == 0), stop=(pj == NJC - 1),
                                     inc=(t4 == 3 or pj == NJC - 1))
                            if pj == NJC - 1:
                                av15[t4] = v
                        pe_av[pj] = v
                    if jc == NJC:
                        break
                    js = slice(jc * 128, (jc + 1) * 128)
                    if jc > 0:
                        c.wait("tensor", "ac", exa[jc - 1])
                    for t4 in (0, 1):
                        c.acquire("tensor", "atb%d" % t4)
                        v = t_mm(c, PS[:, t4, :], qt[r0:r0 + 64, 1, js],
                                 qt[r0:r0 + 64, 0, t4 * TOK:(t4 + 1) * TOK],
                                 start=True, stop=True, inc=(t4 == 1))
                    pe_st[(jc, 0)] = v
                    if jc > 0:
                        c.wait("tensor", "ac", exb[jc - 1])
                    for t4 in (2, 3):
                        c.acquire("tensor", "atb%d" % t4)
                        v = t_mm(c, PS[:, t4, :], qt[r0:r0 + 64, 1, js],
                                 qt[r0:r0 + 64, 0, t4 * TOK:(t4 + 1) * TOK],
                                 start=True, stop=True, inc=(t4 == 3))
                    pe_st[(jc, 1)] = v
                    sl = jc % 2
                    bias_ap = alCM[:, jc:jc + 1] if layer == 1 else alCM[:, 0:1]
                    c.wait("scalar", "pe", pe_st[(jc, 0)])
                    if jc >= 2:
                        c.wait("scalar", "pe", pe_av[jc - 2])
                    exa[jc] = s_op(c, lambda eng, o=atile[:, sl, 0:2 * TOK],
                                   i=PS[:, 0:2, :].rearrange("p a b -> p (a b)"),
                                   b=bias_ap:
                                   eng.activation(out=o, in_=i, func=AF.Exp, bias=b))
                    c.wait("scalar", "pe", pe_st[(jc, 1)])
                    exb[jc] = s_op(c, lambda eng, o=atile[:, sl, 2 * TOK:4 * TOK],
                                   i=PS[:, 2:4, :].rearrange("p a b -> p (a b)"),
                                   b=bias_ap:
                                   eng.activation(out=o, in_=i, func=AF.Exp, bias=b))
                pb2s = {}
                for t4 in range(4):
                    osl = t4 % 2
                    c.wait("vector", "pe", av15[t4])
                    if t4 >= 2:
                        c.wait("vector", "pe", pb2s[t4 - 2])
                    vr = v_op(c, lambda eng, o=orec[:, osl, :],
                              i=PS[64:65, 4 + t4, :]: eng.reciprocal(o, i))
                    c.wait("tensor", "dv", vr)
                    c.wait("tensor", "ac", exb[NJC - 1])
                    c.wait("tensor", "ac", exa[NJC - 1])
                    c.acquire("tensor", "atb%d" % t4)
                    pb2 = t_mm(c, PS[0:64, t4, :], ones_f[0:1, 0:64],
                               orec[0:1, osl, :], start=True, stop=True, inc=True)
                    pb2s[t4] = pb2
                    c.wait("vector", "pe", pb2)
                    vc = v_op(c, lambda eng, o=nrm[:, osl, :], i=PS[0:64, t4, :]:
                              eng.tensor_copy(o, i))
                    c.release("atb%d" % t4, "dv", vc)
                    c.wait("vector", "dv", vc)
                    vn = v_op(c, lambda eng,
                              o=attnout[r0:r0 + 64, hp, t4 * TOK:(t4 + 1) * TOK],
                              a=PS[0:64, 4 + t4, :], b=nrm[:, osl, :]:
                              eng.tensor_tensor(o, a, b, op=ALU.mult))
                    c.release("atb%d" % (4 + t4), "dv", vn)
        ao_ready = c.counts["dv"]
        dbg_dump(c, "ao_u%d" % uid, attnout, [128, NHP, S], BF16)

        if layer == 1:
            c.wait("tensor", "dv", ao_ready)
            for it in range(QT4):
                ts = slice(it * TOK, (it + 1) * TOK)
                c.acquire("tensor", "atb2")
                last = None
                for kc in range(KC):
                    last = t_mm(c, psE[0:8, :], aob[:, kc, :], attnout[:, kc, ts],
                                start=(kc == 0), stop=(kc == KC - 1),
                                inc=(kc == KC - 1))
                c.wait("scalar", "pe", last)
                axo = s_op(c, lambda eng, o=xaO[:, ts], i=psE[0:8, :]:
                           eng.activation(out=o, in_=i, func=AF.Copy, scale=0.125))
                c.release("atb2", "ac", axo)
            xaO_ready = c.counts["ac"]
            c.require("vector", "xT")
            c.require("vector", "xn")
            for kc in range(KC):
                v_op(c, lambda eng, o=xT[:, kc, :], a=xT[:, kc, :], b=xn[:, kc, :]:
                     eng.tensor_add(o, a, b))
            xplus = c.counts["dv"]
        c.wait("tensor", "dv", ao_ready)
        woname = "wo0" if layer == 0 else "WoT"
        bo1 = c.bufs["bo1"]
        wqb_flat = wqb.rearrange("p a b -> p (a b)")
        wo_b = wqb_flat[:, 0:1024].rearrange("p (a b) -> p a b", a=KC)
        if layer == 1:
            bob = wqb_flat[0:8, 2048:3072].rearrange("p (a b) -> p a b", a=KC)
            bof_v = wqf[0:8, :, :]
            c.acquire("sync", "atwqf")
            ldbo = dma(c, "sync", bof_v, c.dram["BoTr"].rearrange(
                "r (kc m) -> r kc m", kc=KC), "ldw_bo")
            c.wait("gpsimd", "ldw_bo", ldbo)
            c.acquire("gpsimd", "atwqb")
            ga2 = g_op(c, lambda eng, o=bob, i=bof_v: eng.tensor_copy(o, i))
            c.release("atwqf", "gp", ga2)
            c.wait("tensor", "gp", ga2)
        wofam = "wo%d" % layer
        for mc in range(KC):
            if mc in c.wc_written[wofam]:
                c.wait_dep("sync", c.wc_written[wofam][mc])
                c.acquire("sync", "atwqb")
                ldq = dma(c, "sync", wo_b,
                          c.wcache[wofam][mc].rearrange("p (a b) -> p a b", a=KC),
                          "ldqkvc")
                gw_dep = ("ldqkvc", ldq)
            else:
                c.acquire("sync", "atwqf")
                ldw = dma(c, "sync", wqf[:, 0:KC, 0:128],
                          c.dram[woname][:, mc * 128:(mc + 1) * 128].rearrange(
                              "(kc kp) m -> kp kc m", kp=128), "ldw_qkv")
                c.wait("gpsimd", "ldw_qkv", ldw)
                c.acquire("gpsimd", "atwqb")
                gw = g_op(c, lambda eng, o=wo_b, i=wqf[:, 0:KC, 0:128]:
                          eng.tensor_copy(o, i))
                c.release("atwqf", "gp", gw)
                wb_cache(c, wofam, mc, wo_b, ("gp", gw), frees=("atwqb",))
                gw_dep = ("gp", gw)
            c.wait_dep("tensor", gw_dep)
            if layer == 1:
                c.wait("tensor", "ac", xaO_ready)
            for it in range(QT4):
                ts = slice(it * TOK, (it + 1) * TOK)
                bank, bname = (psA, "atb0") if (mc * QT4 + it) % 2 == 0 else (psB, "atb1")
                c.acquire("tensor", bname)
                nk2 = KC if layer == 1 else KC - 1
                for kc in range(nk2):
                    t_mm(c, bank[:], wo_b[:, kc, :],
                         attnout[:, kc, ts], start=(kc == 0), stop=False)
                if layer == 1:
                    last = t_mm(c, bank[:], bob[0:8, mc, :],
                                xaO[0:8, ts], start=False, stop=True, inc=True)
                    c.wait("vector", "pe", last)
                    c.acquire("vector", "atSs_wo")
                    evs = v_op(c, lambda eng, o=Ss[:, ts], i=bank[:],
                               b=bo1[:, mc:mc + 1]:
                               eng.tensor_scalar(o, i, b, None, op0=ALU.add))
                    c.release(bname, "dv", evs)
                    c.wait("vector", "dv", evs)
                    c.wait("vector", "dv", xplus)
                    rz = c.bufs["rz"]
                    rv = v_op(c, lambda eng, o=xT[:, mc, ts], a=Ss[:, ts],
                              x2=xT[:, mc, ts], s=rz[:]:
                              eng.scalar_tensor_tensor(o, a, s, x2, op0=ALU.mult,
                                                       op1=ALU.add))
                    c.release("atSs_wo", "dv", rv)
                else:
                    last = t_mm(c, bank[:], wo_b[:, KC - 1, :],
                                attnout[:, KC - 1, ts], start=False, stop=True,
                                inc=True)
                    c.wait("vector", "pe", last)
                    rv = v_op(c, lambda eng, o=xT[:, mc, ts], a=xT[:, mc, ts],
                              b=bank[:]: eng.tensor_add(o, a, b))
                    c.release(bname, "dv", rv)
            c.release("atwqb", "pe", c.counts["pe"])
        c.produce("xT", "dv", c.counts["dv"])
        c.barrier()


def emit_ffn1(c, xT, half):
    tok0 = half * HALF
    ident = c.bufs["ident"]
    with ExitStack() as es:
        xn = asb(c, "f1xn", [128, KC, HALF], BF16, es)[:]
        acc = asb(c, "f1acc", [128, KC, HALF], F32, es)[:]
        cT = asb(c, "f1cT", [E1, HALF], BF16, es)[:]
        beta_row = asb(c, "f1brow", [1, HALF], F32, es)[:]
        web = asb(c, "f1web", [128, 2, KC, D], BF16, es)[:]
        stage = asb(c, "f1st", [128, 2, 1024], F32, es)[:]
        xnp = asb(c, "f1xnp", [128, 2, KC, TOK], BF16, es)[:]
        cbs = asb(c, "f1cb", [128, 2, TOK], BF16, es)[:]
        sel_sb = asb(c, "f1sel", [E1, 2, 128], BF16, es)[:]
        rt_f = asb(c, "f1rtf", [128, KC, E1], F32, es)[:]
        rt_b = asb(c, "f1rtb", [128, KC, E1], BF16, es)[:]
        p_ed = aps(c, "f1ed", [128, 3, TOK], F32, es)[:]
        p_cb = aps(c, "f1pcb", [128, TOK], F32, es)[:]
        p_a = aps(c, "f1pa", [128, TOK], F32, es)[:]
        p_b = aps(c, "f1pb", [128, TOK], F32, es)[:]

        with ExitStack() as bs:
            statB = asb(c, "f1statB", [128, 2, TOK], F32, bs)[:]
            statB2 = asb(c, "f1statB2", [128, 2, TOK], F32, bs)[:]
            sq = asb(c, "f1sq", [128, 2, TOK], F32, bs)[:]
            lgT = asb(c, "f1lgT", [E1, HALF], F32, bs)[:]
            ltok = asb(c, "f1ltok", [128, NTC, E1], F32, bs)[:]
            r8 = asb(c, "f1r8", [128, 4, E1], F32, bs)[:]
            r1 = asb(c, "f1r1", [128, 6], F32, bs)[:]
            ctokb = asb(c, "f1ctok", [128, E1], F32, bs)[:]
            bcol = asb(c, "f1bcol", [128, NTC], F32, bs)[:]
            brt = asb(c, "f1brt", [16, 128], F32, bs)[:]

            ldr = dma(c, "sync", rt_f[:], c.dram["routerT"].rearrange(
                "(kc kp) e -> kp kc e", kp=128), "ldw_gate")
            c.wait("scalar", "ldw_gate", ldr)
            a_r = s_op(c, lambda eng, o=rt_b[:], i=rt_f[:]:
                       eng.activation(out=o, in_=i, func=AF.Copy))

            ln_sc = {"p0": p_a, "p1": p_b, "statB": statB, "statB2": statB2,
                     "sq": sq}
            emit_ln(c, xT, xn, tok0, HALF, ln_sc)

            c.require("tensor", "xn")
            c.wait("tensor", "ac", a_r)
            lg_done = []
            for it in range(NT):
                if lg_done:
                    c.wait("tensor", "ac", lg_done[-1])
                last = None
                for kc in range(KC):
                    last = t_mm(c, p_cb[0:E1, :], rt_b[:, kc],
                                xn[:, kc, it * TOK:(it + 1) * TOK],
                                start=(kc == 0), stop=(kc == KC - 1),
                                inc=(kc == KC - 1))
                c.wait("scalar", "pe", last)
                a1 = s_op(c, lambda eng, o=lgT[:, it * TOK:(it + 1) * TOK],
                          i=p_cb[0:E1, :]: eng.activation(out=o, in_=i, func=AF.Copy))
                lg_done.append(a1)
            c.wait("tensor", "ac", lg_done[-1])
            tr_done = {}
            for tc in range(NTC):
                if tc >= 2:
                    c.wait("tensor", "ac", tr_done[tc - 2])
                pe1 = t_transpose(c, p_ed[:, tc % 2, 0:E1],
                                  lgT[:, tc * 128:(tc + 1) * 128],
                                  ident[0:E1, 0:E1])
                c.wait("scalar", "pe", pe1)
                tr_done[tc] = s_op(c, lambda eng, o=ltok[:, tc], i=p_ed[:, tc % 2, 0:E1]:
                                   eng.activation(out=o, in_=i, func=AF.Copy))
            last_cT = None
            last_pe2 = None
            for tc in range(NTC):
                c.wait("vector", "ac", tr_done[tc])
                lt = ltok[:, tc]
                v1 = v_op(c, lambda eng, o=r1[:, 0:1], i=lt:
                          eng.reduce_max(o, i, axis=AX.X, negate=True))
                c.wait("scalar", "dv", v1)
                a1 = s_op(c, lambda eng, o=r8[:, 0], i=lt, b=r1[:, 0:1],
                          acc=r1[:, 1:2]:
                          eng.activation(out=o, in_=i, func=AF.Exp, bias=b,
                                         accum_out=acc))
                c.wait("vector", "ac", a1)
                v2 = v_op(c, lambda eng, o=r1[:, 2:3], i=r1[:, 1:2]:
                          eng.reciprocal(o, i))
                c.wait("vector", "dv", v2)
                v3 = v_op(c, lambda eng, o=r8[:, 1], i=r8[:, 0], s=r1[:, 2:3]:
                          eng.tensor_scalar(o, i, s, None, op0=ALU.mult))
                c.wait("vector", "dv", v3)
                v4 = v_op(c, lambda eng, o=r8[:, 2], i=r8[:, 1]: eng.max(out=o, in_=i))
                c.wait("vector", "dv", v4)
                v5 = v_op(c, lambda eng, o=r1[:, 3:4], a=r8[:, 2, 0:1], b=r8[:, 2, 1:2]:
                          eng.tensor_add(o, a, b))
                c.wait("vector", "dv", v5)
                v6 = v_op(c, lambda eng, o=r1[:, 3:4], i=r1[:, 3:4]:
                          eng.tensor_single_scalar(o, i, 1e-6, op=ALU.add))
                c.wait("vector", "dv", v6)
                v7 = v_op(c, lambda eng, o=r1[:, 4:5], i=r1[:, 3:4]:
                          eng.reciprocal(o, i))
                c.wait("vector", "dv", v7)
                v8 = v_op(c, lambda eng, o=r8[:, 3], i=r8[:, 1], s=r8[:, 2, 1:2]:
                          eng.tensor_scalar(o, i, s, None, op0=ALU.is_ge))
                c.wait("vector", "dv", v8)
                v9 = v_op(c, lambda eng, o=r8[:, 3], a=r8[:, 3], b=r8[:, 1]:
                          eng.tensor_mul(o, a, b))
                c.wait("vector", "dv", v9)
                if last_pe2 is not None:
                    c.wait("vector", "pe", last_pe2)
                v10 = v_op(c, lambda eng, o=ctokb[:], i=r8[:, 3], s=r1[:, 4:5]:
                           eng.tensor_scalar(o, i, s, None, op0=ALU.mult))
                c.wait("vector", "dv", v10)
                v11 = v_op(c, lambda eng, o=r8[:, 3], a=ctokb[:], b=r8[:, 1]:
                           eng.tensor_mul(o, a, b))
                c.wait("vector", "dv", v11)
                v12 = v_op(c, lambda eng, o=bcol[:, tc:tc + 1], i=r8[:, 3]:
                           eng.reduce_sum(o, i, axis=AX.X))
                c.wait("tensor", "dv", v12)
                if last_cT is not None:
                    c.wait("tensor", "ac", last_cT)
                pe2 = t_transpose(c, p_ed[0:E1, 2, 0:128], ctokb, ident[:])
                last_pe2 = pe2
                c.wait("scalar", "pe", pe2)
                last_cT = s_op(c, lambda eng, o=cT[:, tc * 128:(tc + 1) * 128],
                               i=p_ed[0:E1, 2, 0:128]:
                               eng.activation(out=o, in_=i, func=AF.Copy))
            c.wait("tensor", "dv", c.counts["dv"])
            c.wait("tensor", "ac", last_cT)
            pe3 = t_transpose(c, p_ed[0:NTC, 2, 0:128], bcol, ident[:])
            c.wait("scalar", "pe", pe3)
            ab = s_op(c, lambda eng, o=brt[0:NTC, :], i=p_ed[0:NTC, 2, 0:128]:
                      eng.activation(out=o, in_=i, func=AF.Copy))
            c.wait("sync", "ac", ab)
            b1 = dma(c, "sync", c.dram["beta_bounce"][:],
                     brt[0:NTC, :], "ldw_mls")
            c.wait("sync", "ldw_mls", b1)
            b2 = dma(c, "sync", beta_row[:],
                     c.dram["beta_bounce"].rearrange("a b -> (a b)").unsqueeze(0),
                     "ldw_mls")
            c.wait("tensor", "ldw_mls", b2)
            c.barrier()

        stream_ctr = [0]

        def stream_chunk(dst_ap, src_ap, key=None):
            sslot = stream_ctr[0] % 2
            stream_ctr[0] += 1
            c.acquire("sync", "f1stage%d" % sslot)
            ld = dma(c, "sync", stage[:, sslot, :], src_ap, "ldw_st%d" % sslot)
            c.wait("gpsimd", "ldw_st%d" % sslot, ld)
            if key:
                c.acquire("gpsimd", key)
            av = g_op(c, lambda eng, o=dst_ap, i=stage[:, sslot, :]:
                      eng.tensor_copy(o, i))
            c.release("f1stage%d" % sslot, "gp", av)
            return av

        def stream_expert(e):
            slot = e % 2
            key = "f1w%d" % slot
            if e in c.wc_written["expw"]:
                c.wait_dep("sync", c.wc_written["expw"][e])
                c.acquire("sync", key)
                ldv = dma(c, "sync", web[:, slot].rearrange("p a b -> p (a b)"),
                          c.wcache["expw"][e], "ldex%d" % slot)
                last = ("ldex%d" % slot, ldv)
            else:
                lv = None
                for ch in range(KC):
                    lv = stream_chunk(web[:, slot, ch],
                                      c.dram["expW"][e, ch * 128:(ch + 1) * 128],
                                      key)
                wb_cache(c, "expw", e,
                         web[:, slot].rearrange("p a b -> p (a b)"), ("gp", lv),
                         frees=("f1w%d" % slot,))
                last = ("gp", lv)
            c.acquire("sync", "f1sel%d" % slot)
            lds = dma(c, "sync", sel_sb[:, slot, :],
                      c.dram["sel8"][:, e * 128:(e + 1) * 128],
                      "ldw_sel%d" % slot)
            return last, lds

        w_done = {}
        sel_done = {}
        w_done[0], sel_done[0] = stream_expert(0)
        w_done[1], sel_done[1] = stream_expert(1)
        acc_last = {}
        for e in range(E1):
            slot = e % 2
            c.wait_dep("tensor", w_done[e])
            c.wait("tensor", "ldw_sel%d" % slot, sel_done[e])
            for it in range(NT):
                ts = slice(it * TOK, (it + 1) * TOK)
                ji = e * NT + it
                jslot = ji % 2
                c.acquire("tensor", "f1pcb")
                cbm = t_mm(c, p_cb[:], sel_sb[:, slot, :], cT[:, ts],
                           start=True, stop=True, inc=True)
                c.wait("scalar", "pe", cbm)
                c.acquire("scalar", "f1cbs%d" % jslot)
                cbe = s_op(c, lambda eng, o=cbs[:, jslot], i=p_cb[:]:
                           eng.activation(out=o, in_=i, func=AF.Copy))
                c.release("f1pcb", "ac", cbe)
                c.wait("vector", "ac", cbe)
                c.require("vector", "xn")
                c.acquire("vector", "f1xnp%d" % jslot)
                g_h = None
                for kc in range(KC):
                    g_h = v_op(c, lambda eng, o=xnp[:, jslot, kc], a=xn[:, kc, ts],
                               b=cbs[:, jslot]: eng.tensor_tensor(o, a, b, op=ALU.mult))
                c.release("f1cbs%d" % jslot, "dv", g_h)
                c.wait("tensor", "dv", g_h)
                for mc in range(KC):
                    bank = mc % 3
                    c.acquire("tensor", "f1ped%d" % bank)
                    first = True
                    if e == 0:
                        t_mm(c, p_ed[:, bank, :], c.bufs["onerow_f"][0:1, :],
                             beta_row[0:1, ts], start=True, stop=False)
                        first = False
                    last = None
                    for kc in range(KC):
                        last = t_mm(c, p_ed[:, bank, :],
                                    web[:, slot, kc, mc * 128:(mc + 1) * 128],
                                    xnp[:, jslot, kc], start=first and kc == 0,
                                    stop=(kc == KC - 1), inc=(kc == KC - 1))
                        first = False
                    if e == 0:
                        c.wait("scalar", "pe", last)
                        a_i = s_op(c, lambda eng, o=acc[:, mc, ts], i=p_ed[:, bank, :]:
                                   eng.activation(out=o, in_=i, func=AF.Copy))
                        c.release("f1ped%d" % bank, "ac", a_i)
                        acc_last[(mc, it)] = ("ac", a_i)
                    else:
                        c.wait("vector", "pe", last)
                        c.wait_dep("vector", acc_last[(mc, it)])
                        vadd = v_op(c, lambda eng, o=acc[:, mc, ts],
                                    a=acc[:, mc, ts], b=p_ed[:, bank, :]:
                                    eng.tensor_add(o, a, b))
                        c.release("f1ped%d" % bank, "dv", vadd)
                        acc_last[(mc, it)] = ("dv", vadd)
                c.release("f1xnp%d" % jslot, "pe", c.counts["pe"])
            c.release("f1w%d" % slot, "pe", c.counts["pe"])
            c.release("f1sel%d" % slot, "pe", c.counts["pe"])
            if e + 2 < E1:
                w_done[e + 2], sel_done[e + 2] = stream_expert(e + 2)

        c.wait("vector", "dv", c.counts["dv"])
        c.wait("vector", "ac", c.counts["ac"])
        for kc in range(KC):
            v_op(c, lambda eng, o=xT[:, kc, tok0:tok0 + HALF],
                 a=acc[:, kc, :], x2=xT[:, kc, tok0:tok0 + HALF]:
                 eng.scalar_tensor_tensor(o, a, 0.5, x2, op0=ALU.mult, op1=ALU.add))
        c.produce("xT", "dv", c.counts["dv"])
        c.barrier()



DSHAPES_F32 = {
    "sgT": (D, ISH), "suT": (D, ISH), "sdT": (ISH, D),
    "gateT": (D, E0), "gbias": (E0, 1),
    "egT": (E0, D, INTER0), "euT": (E0, D, INTER0), "edT": (E0, INTER0, D),
    "gamma": (128, KC), "wqkv0": (NHP, KC, 128, 3, 128), "wo0": (D, D),
    "WoT": (D, D), "AoT": (D, 8), "BoTr": (8, D),
    "AqT": (D, 8), "AkT": (D, 8), "AvT": (D, 8),
    "BqTr": (8, D), "BkTr": (8, D), "BvTr": (8, D),
    "wqkv1": (NHP, KC, 128, 3, 128), "aqkv1": (KC, 128, 96),
    "bqkv1": (NHP, 8, 3, 128), "biasqkv1": (NHP, 128, 3),
    "bo1": (128, KC), "rz": (128, 1),
    "routerT": (D, E1), "expW": (E1, D, D),
    "ident": (128, 128), "ones_f": (128, 128), "onerow_f": (1, 128),
    "ones33": (33, 128),
    "alibiC": (128, HEADS, S // 128), "slp2": (65, 2, NHP),
}
DSHAPES_BF16 = {"identb": (128, 128), "sel64": (E0, E0 * 128),
                "blk2": (128, 65),
                "sel8": (E1, E1 * 128),
                "ones_b": (128, 128), "onerow_b": (1, 128)}


def build_program(parts):
    nc = bass.Bass("TRN2", target_bir_lowering=False)
    es = ExitStack()
    c = Ctx(nc, es)
    for k, shp in DSHAPES_F32.items():
        c.dram[k] = nc.dram_tensor(k, list(shp), F32, kind="ExternalInput").ap()
    for k, shp in DSHAPES_BF16.items():
        c.dram[k] = nc.dram_tensor(k, list(shp), BF16, kind="ExternalInput").ap()
    c.dram["xT"] = nc.dram_tensor("xT", [SPC, KC, 128, S], F32,
                                  kind="ExternalInput").ap()
    c.dram["outT"] = nc.dram_tensor("outT", [SPC, KC, 128, S], F32,
                                    kind="ExternalOutput").ap()

    xT = asb(c, "xT_sb", [128, KC, S], F32)[:]
    c.bufs["dummy"] = asb(c, "dummy", [128, 4], F32)[:]
    for nm, shp, dt in [("ident", (128, 128), F32), ("identb", (128, 128), BF16),
                        ("ones_f", (128, 128), F32), ("ones_b", (128, 128), BF16),
                        ("onerow_b", (1, 128), BF16), ("gamma", (128, KC), F32),
                        ("gbias", (E0, 1), F32), ("rz", (128, 1), F32),
                        ("bo1", (128, KC), F32),
                        ("alibiC", (128, HEADS, S // 128), F32),
                        ("slp2", (65, 2, NHP), F32), ("blk2", (128, 65), BF16)]:
        c.bufs[nm] = asb(c, "c_" + nm, list(shp), dt)[:]
        dma(c, "sync", c.bufs[nm], c.dram[nm], "ld")
    c.bufs["onerow_f"] = asb(c, "c_onerowf", [1, 128], F32)[:]
    dma(c, "sync", c.bufs["onerow_f"], c.dram["onerow_f"], "ld")
    c.bufs["ones33"] = asb(c, "c_ones33", [33, 128], F32)[:]
    dma(c, "sync", c.bufs["ones33"], c.dram["ones33"], "ld")
    c.dram["mls_bounce"] = nc.dram_tensor("mls_bounce", [16, 128], F32,
                                          kind="Internal").ap()
    c.wcache = {}
    c.wc_written = {}
    for fam, n, cols in [("eg", E0, 4096), ("eu", E0, 4096), ("ed", E0, 4096),
                         ("sh", 3, 8192), ("expw", E1, 8192),
                         ("qkv0", NHP, 3072), ("qkv1", NHP, 3072),
                         ("wo0", KC, 1024), ("wo1", KC, 1024)]:
        c.wcache[fam] = nc.dram_tensor("wc_" + fam, [n, 128, cols], BF16,
                                       kind="Internal").ap()
        c.wc_written[fam] = {}
    c.dram["beta_bounce"] = nc.dram_tensor("beta_bounce", [NTC, 128], F32,
                                           kind="Internal").ap()
    cb = asb(c, "c_eps", [128, 8], F32)[:]
    c.bufs["gp1"] = asb(c, "c_gp1", [128, KC], F32)[:]
    vals = {"eps_ln": (0, 1e-5), "eps_rms": (1, 1e-12), "one": (4, 1.0),
            "neg_big": (5, NEG_BIG), "zero": (6, 0.0), "c32": (7, 32.0)}
    c.const_cols = {k: cb[:, i:i + 1] for k, (i, v) in vals.items()}
    for k, (i, v) in vals.items():
        c.emit("vector", lambda eng, ap=cb[:, i:i + 1], vv=v: eng.memset(ap, vv))
    c.emit("vector", lambda eng, s=c.sem("dv"): eng.engine_nop().then_inc(s, 1))
    c.inc("dv")
    ld_consts = c.counts["ld"]
    dv0 = c.counts["dv"]
    for e in ENGS[:4]:
        c.wait(e, "ld", ld_consts)
        c.wait(e, "dv", dv0)
    a_gp = v_op(c, lambda eng, o=c.bufs["gp1"], i=c.bufs["gamma"]:
                eng.tensor_scalar(o, i, 32.0, 32.0, op0=ALU.mult, op1=ALU.add))
    for e in ENGS[:4]:
        c.wait(e, "dv", a_gp)

    for seq in range(SPC):
        lx = None
        c.acquire("sync", "xTbuf")
        for kc in range(KC):
            lx = dma(c, "sync", xT[:, kc, :], c.dram["xT"][seq, kc], "ldx")
        for e in ("tensor", "vector", "scalar"):
            c.wait(e, "ldx", lx)
        c.produce("xT", "ldx", lx)
        for part in parts:
            if part == "ffn0":
                for half in range(S // HALF):
                    emit_ffn0(c, xT, half)
            elif part == "attn0":
                emit_attn(c, xT, layer=0)
            elif part == "ffn1":
                for half in range(S // HALF):
                    emit_ffn1(c, xT, half)
            elif part == "attn1":
                emit_attn(c, xT, layer=1)
            else:
                raise ValueError(part)
        c.wait_dep("sync", c.ready.get("xT"))
        sx = None
        for kc in range(KC):
            sx = dma(c, "sync", c.dram["outT"][seq, kc], xT[:, kc, :], "sto")
        c.release("xTbuf", "sto", sx)
    c.wait("sync", "sto", c.counts["sto"])

    with nc.Block() as block:
        @block.tensor
        def _(eng):
            for fn in c.prog["tensor"]:
                fn(eng)

        @block.vector
        def _(eng):
            for fn in c.prog["vector"]:
                fn(eng)

        @block.scalar
        def _(eng):
            for fn in c.prog["scalar"]:
                fn(eng)

        @block.gpsimd
        def _(eng):
            for fn in c.prog["gpsimd"]:
                fn(eng)

        @block.sync
        def _(eng):
            for fn in c.prog["sync"]:
                fn(eng)
    es.close()
    return nc



_PARTS_ALL = ["ffn0", "attn0", "ffn0", "ffn1", "attn1", "ffn1"]


def run_parts(inputs, parts):
    h, percore = prep_host(inputs)
    nc = build_program(parts)
    names = set(DSHAPES_F32) | set(DSHAPES_BF16) | {"xT"}
    in_maps = []
    for cid in range(NCORE):
        m = {k: v for k, v in h.items() if k in names}
        m["xT"] = percore[cid]["xT"]
        in_maps.append(m)
    res = run_bass_kernel_spmd(nc, in_maps, core_ids=list(range(NCORE)))
    out = np.empty((BATCH, S, D), dtype=np.float32)
    for cid in range(NCORE):
        ot = res.results[cid]["outT"].reshape(SPC, D, S)
        out[cid * SPC:(cid + 1) * SPC] = ot.transpose(0, 2, 1)
    return out


def kernel(**inputs):
    return run_parts(inputs, _PARTS_ALL)


# version 07
# speedup vs baseline: 1.2813x; 1.1285x over previous
import numpy as np
import ml_dtypes
from contextlib import ExitStack

import concourse.bass as bass
import concourse.mybir as mybir
from concourse.bass_utils import run_bass_kernel_spmd

F32 = mybir.dt.float32
BF16 = mybir.dt.bfloat16
AF = mybir.ActivationFunctionType
ALU = mybir.AluOpType
AX = mybir.AxisListType
BF = ml_dtypes.bfloat16

D = 1024
S = 2048
BATCH = 16
NCORE = 8
SPC = BATCH // NCORE
KC = D // 128
NEG_BIG = -1e30

E0 = 64; INTER0 = 512; IC0 = INTER0 // 128
ISH = 1024
E1 = 8
HEADS = 16; DH = 64; NHP = HEADS // 2
TOK = 512
HALF = 1024
NT = HALF // TOK
NTC = HALF // 128

ENGS = ("tensor", "vector", "scalar", "gpsimd", "sync")



def _c(a):
    return np.ascontiguousarray(a)


def prep_host(inp):
    h = {}
    h["sgT"] = _c(inp["L0_ffn_sg_w"].astype(np.float32).T)
    h["suT"] = _c(inp["L0_ffn_su_w"].astype(np.float32).T)
    h["sdT"] = _c(inp["L0_ffn_sd_w"].astype(np.float32).T)
    h["gateT"] = _c(inp["L0_ffn_gate_w"].astype(np.float32).T)
    h["gbias"] = _c(inp["L0_ffn_gate_bias"].astype(np.float32).reshape(E0, 1))
    h["egT"] = _c(inp["L0_ffn_eg_w"].astype(np.float32).transpose(0, 2, 1))
    h["euT"] = _c(inp["L0_ffn_eu_w"].astype(np.float32).transpose(0, 2, 1))
    h["edT"] = _c(inp["L0_ffn_ed_w"].astype(np.float32).transpose(0, 2, 1))
    h["gamma"] = _c(inp["L0_attn_gamma"].astype(np.float32).reshape(KC, 128).T)
    wq = inp["L0_attn_w_qkv"].astype(np.float32).reshape(8, 128, 3, NHP, 128)
    h["wqkv0"] = _c(wq.transpose(3, 0, 1, 2, 4))
    h["wo0"] = _c(inp["L0_attn_w_out"].astype(np.float32))
    WT = {}
    for n in ["q", "k", "v", "o"]:
        WT[n] = _c(inp["L1_attn_W" + n].astype(np.float32).T)
        h["A%sT" % n] = _c(inp["L1_attn_A" + n].astype(np.float32).T)
        h["B%sTr" % n] = _c(inp["L1_attn_B" + n].astype(np.float32).T)
    h["WoT"] = WT["o"]
    wq1 = np.stack([WT["q"], WT["k"], WT["v"]], 0).reshape(3, 8, 128, NHP, 128)
    h["wqkv1"] = _c(wq1.transpose(3, 1, 2, 0, 4))
    aq1 = np.stack([h["AqT"], h["AkT"], h["AvT"]], 0)
    apad = np.zeros((D, 96), dtype=np.float32)
    for p in range(3):
        apad[:, p * 32:p * 32 + 8] = aq1[p]
    h["aqkv1"] = _c(apad.reshape(8, 128, 96))
    bq1 = np.stack([h["BqTr"], h["BkTr"], h["BvTr"]], 0).reshape(3, 8, NHP, 128)
    h["bqkv1"] = _c(bq1.transpose(2, 1, 0, 3))
    bb = np.stack([inp["L1_attn_bq"], inp["L1_attn_bk"], inp["L1_attn_bv"]], 0)
    h["biasqkv1"] = _c(bb.astype(np.float32).reshape(3, 8, 128).transpose(1, 2, 0))
    h["bo1"] = _c(inp["L1_attn_bo"].astype(np.float32).reshape(KC, 128).T)
    h["rz"] = _c(np.broadcast_to(
        inp["L1_attn_rezero"].astype(np.float32).reshape(1, 1), (128, 1)))
    h["routerT"] = _c(inp["L1_ffn_router_w"].astype(np.float32).T)
    h["expW"] = _c(inp["L1_ffn_expert_weights"].astype(np.float32))
    h["ident"] = np.eye(128, dtype=np.float32)
    h["identb"] = np.eye(128, dtype=BF)
    sel64 = np.zeros((E0, E0 * 128), dtype=BF)
    for e in range(E0):
        sel64[e, e * 128:(e + 1) * 128] = 1
    h["sel64"] = sel64
    sel8 = np.zeros((E1, E1 * 128), dtype=BF)
    for e in range(E1):
        sel8[e, e * 128:(e + 1) * 128] = 1
    h["sel8"] = sel8
    h["onerow_f"] = np.ones((1, 128), dtype=np.float32)
    o33 = np.zeros((33, 128), dtype=np.float32)
    o33[0] = 1.0; o33[32] = 1.0
    h["ones33"] = o33
    h["ones_f"] = np.ones((128, 128), dtype=np.float32)
    h["ones_b"] = np.ones((128, 128), dtype=BF)
    h["onerow_b"] = np.ones((1, 128), dtype=BF)
    slopes = 0.5 ** np.arange(HEADS, dtype=np.float32)
    blk2 = np.zeros((128, 65), dtype=BF)
    blk2[0:64, 0] = 1.0
    blk2[64:128, 64] = 1.0
    h["blk2"] = blk2
    slp2 = np.ones((65, 2, NHP), dtype=np.float32)
    for hp in range(NHP):
        for hh in range(2):
            slp2[64 * hh, 1, hp] = slopes[hp * 2 + hh] * float(S - 1) + 1.0
    h["slp2"] = _c(slp2)
    pcol = np.arange(128, dtype=np.float32)
    ali_c = np.zeros((128, HEADS, S // 128), dtype=np.float32)
    for hh in range(HEADS):
        for jc in range(S // 128):
            ali_c[:, hh, jc] = slopes[hh] * (jc * 128 + pcol)
    h["alibiC"] = ali_c
    x = inp["L0_attn_x"].astype(np.float32)
    percore = []
    for cid in range(NCORE):
        xs = x[cid * SPC:(cid + 1) * SPC]
        xt = _c(xs.transpose(0, 2, 1)).reshape(SPC, KC, 128, S)
        percore.append({"xT": xt})
    return h, percore



class Ctx:
    def __init__(self, nc, es):
        self.nc = nc
        self.es = es
        self.prog = {e: [] for e in ENGS}
        self.sems = {}
        self.counts = {}
        self.waited = {e: {} for e in ENGS}
        self.ready = {}
        self.freed = {}
        self.bufs = {}
        self.dram = {}
        self.const_cols = {}

    def sem(self, name):
        if name not in self.sems:
            self.sems[name] = self.es.enter_context(self.nc.semaphore(name))
            self.counts[name] = 0
        return self.sems[name]

    def inc(self, name, n=1):
        self.sem(name)
        self.counts[name] += n
        return self.counts[name]

    def emit(self, engine, fn):
        self.prog[engine].append(fn)

    def wait(self, engine, sem_name, val):
        if val is None or val <= 0:
            return
        sm = self.sem(sem_name)
        w = self.waited[engine]
        if w.get(sem_name, 0) >= val:
            return
        w[sem_name] = val
        self.prog[engine].append(lambda eng, s=sm, v=val: eng.wait_ge(s, v))

    def wait_dep(self, engine, dep):
        if dep is not None:
            self.wait(engine, dep[0], dep[1])

    def require(self, engine, res):
        self.wait_dep(engine, self.ready.get(res))

    def acquire(self, engine, res):
        deps = self.freed.get(res)
        if deps:
            for d in deps:
                self.wait_dep(engine, d)

    def produce(self, res, sem_name, val):
        self.ready[res] = (sem_name, val)

    def release(self, res, sem_name, val):
        lst = self.freed.setdefault(res, [])
        for i, (s, v) in enumerate(lst):
            if s == sem_name:
                lst[i] = (sem_name, max(v, val))
                break
        else:
            lst.append((sem_name, val))

    release_add = release

    def barrier(self, engines=("tensor", "vector", "scalar", "gpsimd")):
        bar = self.sem("bar")
        for snm, v in list(self.counts.items()):
            if snm.startswith("wb_") and v:
                self.wait("sync", snm, v)
        tgt = self.counts["bar"] + len(engines) + 1
        self.counts["bar"] = tgt
        dummy = self.bufs["dummy"]
        self.prog["sync"].append(lambda eng, s=bar: eng.nop().then_inc(s, 1))
        for e in engines:
            if e == "tensor":
                self.prog[e].append(lambda eng, s=bar: eng.nop().then_inc(s, 1))
            elif e == "vector":
                self.prog[e].append(lambda eng, s=bar: eng.engine_nop().then_inc(s, 1))
            elif e == "scalar":
                self.prog[e].append(lambda eng, s=bar, d=dummy: eng.memzero(d[0:1, 0:1]).then_inc(s, 1))
            elif e == "gpsimd":
                self.prog[e].append(lambda eng, s=bar, d=dummy: eng.memset(d[0:1, 1:2], 0.0).then_inc(s, 1))
        for e in engines:
            self.wait(e, "bar", tgt)
        self.wait("sync", "bar", tgt)
        for e in ENGS:
            w = self.waited[e]
            for s in ("pe", "dv", "ac", "gp"):
                if s in self.counts:
                    w[s] = max(w.get(s, 0), self.counts[s])
        return ("bar", tgt)


_NAME_CTR = [0]


def _uname(name):
    _NAME_CTR[0] += 1
    return "k%d_%s" % (_NAME_CTR[0], name)


def asb(c, name, shape, dtype, es=None):
    return (es or c.es).enter_context(
        c.nc.sbuf_tensor(_uname(name), list(shape), dtype))


def aps(c, name, shape, dtype=F32, es=None):
    return (es or c.es).enter_context(
        c.nc.psum_tensor(_uname(name), list(shape), dtype))


def t_mm(c, out_ap, lhsT_ap, rhs_ap, start, stop, inc=False):
    if inc:
        c.emit("tensor", lambda eng, o=out_ap, l=lhsT_ap, r=rhs_ap, a=start, b=stop,
               s=c.sem("pe"): eng.matmul(o, l, r, start=a, stop=b).then_inc(s, 1))
        return c.inc("pe")
    c.emit("tensor", lambda eng, o=out_ap, l=lhsT_ap, r=rhs_ap, a=start, b=stop:
           eng.matmul(o, l, r, start=a, stop=b))
    return None


def t_transpose(c, out_ap, in_ap, ident_ap):
    c.emit("tensor", lambda eng, o=out_ap, i=in_ap, d=ident_ap,
           s=c.sem("pe"): eng.transpose(o, i, d).then_inc(s, 1))
    return c.inc("pe")


def v_op(c, fn, inc=True):
    if inc:
        c.emit("vector", lambda eng, f=fn, s=c.sem("dv"): f(eng).then_inc(s, 1))
        return c.inc("dv")
    c.emit("vector", fn)
    return None


def s_op(c, fn, inc=True):
    if inc:
        c.emit("scalar", lambda eng, f=fn, s=c.sem("ac"): f(eng).then_inc(s, 1))
        return c.inc("ac")
    c.emit("scalar", fn)
    return None


def g_op(c, fn, inc=True):
    if inc:
        c.emit("gpsimd", lambda eng, f=fn, s=c.sem("gp"): f(eng).then_inc(s, 1))
        return c.inc("gp")
    c.emit("gpsimd", fn)
    return None


def wb_cache(c, fam, item, src_ap, cast_dep, frees=()):
    c.wait_dep("sync", cast_dep)
    sem = "wb_" + (frees[0] if frees else fam)
    wb = dma(c, "sync", c.wcache[fam][item], src_ap, sem)
    c.wc_written[fam][item] = (sem, wb)
    for r in frees:
        c.release_add(r, sem, wb)
    return wb


def dma(c, queue, out_ap, in_ap, sem_name):
    c.emit(queue, lambda eng, o=out_ap, i=in_ap, s=c.sem(sem_name):
           eng.dma_start(out=o, in_=i).then_inc(s, 16))
    return c.inc(sem_name, 16)



def emit_ln(c, xT, xn, tok0, ntok, sc):
    ones_b = c.bufs["ones_b"]
    p_sum, p_sq = sc["p0"], sc["p1"]
    statB, statB2, sq = sc["statB"], sc["statB2"], sc["sq"]
    lnb = sc["lnb"]
    ntile = ntok // TOK
    sq_mm = {}
    stats_read = None
    prev_apply = None
    for it in range(ntile):
        t0 = tok0 + it * TOK
        c.require("tensor", "xT")
        c.require("vector", "xT")
        c.require("scalar", "xT")
        if stats_read is not None:
            c.wait("tensor", "dv", stats_read)
        if prev_apply is not None:
            c.wait("vector", "dv", prev_apply)
        for kc in range(KC):
            par = kc % 2
            gi = it * KC + kc
            if gi >= 2:
                c.wait("vector", "pe", sq_mm[gi - 2])
                c.wait("scalar", "pe", sq_mm[gi - 2])
            axb = s_op(c, lambda eng, o=lnb[:, par], i=xT[:, kc, t0:t0 + TOK]:
                       eng.activation(out=o, in_=i, func=AF.Copy))
            vq = v_op(c, lambda eng, o=lnb[:, 2 + par], i=xT[:, kc, t0:t0 + TOK]:
                      eng.tensor_tensor(o, i, i, op=ALU.mult))
            c.wait("tensor", "dv", vq)
            c.wait("tensor", "ac", axb)
            t_mm(c, p_sum[:], ones_b[:], lnb[:, par],
                 start=(kc == 0), stop=(kc == KC - 1))
            sq_mm[gi] = t_mm(c, p_sq[:], ones_b[:], lnb[:, 2 + par],
                             start=(kc == 0), stop=(kc == KC - 1), inc=True)
        pe_done = c.counts["pe"]
        sB = statB[:, it % 2]
        sB2 = statB2[:, it % 2]
        c.wait("vector", "pe", pe_done)
        if prev_apply is not None:
            c.wait("scalar", "dv", prev_apply)
        v_op(c, lambda eng, o=sB2, i=p_sum[:]:
             eng.tensor_single_scalar(o, i, 1.0 / D, op=ALU.mult))
        v1 = v_op(c, lambda eng, o=sB, i=p_sq[:]:
                  eng.tensor_single_scalar(o, i, 1.0 / D, op=ALU.mult))
        stats_read = v1
        c.wait("scalar", "dv", v1)
        a1 = s_op(c, lambda eng, o=sq[:, 0], i=sB2:
                  eng.activation(out=o, in_=i, func=AF.Square))
        c.wait("vector", "ac", a1)
        v2 = v_op(c, lambda eng, o=sB, a=sB, b=sq[:, 0]: eng.tensor_sub(o, a, b))
        c.wait("scalar", "dv", v2)
        a2 = s_op(c, lambda eng, o=sB, i=sB, b=c.const_cols["eps_ln"]:
                  eng.activation(out=o, in_=i, func=AF.Sqrt, bias=b))
        c.wait("vector", "ac", a2)
        v3 = v_op(c, lambda eng, o=sB, i=sB: eng.reciprocal(o, i))
        c.wait("vector", "dv", v3)
        v4 = v_op(c, lambda eng, o=sB2, i=sB2, r=sB:
                  eng.scalar_tensor_tensor(o, i, -1.0, r, op0=ALU.mult, op1=ALU.mult))
        c.wait("vector", "dv", v4)
        sq_w = {}
        for kc in range(KC):
            par = kc % 2
            if kc >= 2:
                c.wait("vector", "dv", sq_w[kc - 2])
            vv = v_op(c, lambda eng, o=sq[:, par], x=xT[:, kc, t0:t0 + TOK], r=sB:
                      eng.tensor_mul(o, x, r))
            c.wait("vector", "dv", vv)
            sq_w[kc] = v_op(c, lambda eng, o=xn[:, kc, it * TOK:(it + 1) * TOK],
                            a=sq[:, par], b=sB2: eng.tensor_add(o, a, b))
        prev_apply = c.counts["dv"]
    c.produce("xn", "dv", c.counts["dv"])



DBG = {"on": False, "maps": []}


def dbg_dump(c, tag, ap, shape, dtype):
    if not DBG["on"] or (DBG.get("only") and not any(t in tag for t in DBG["only"])):
        return
    nm = "dbg_%s" % tag
    d = c.nc.dram_tensor(nm, list(shape), dtype, kind="ExternalOutput").ap()
    bt = c.barrier()
    c.wait_dep("sync", bt)
    ld = dma(c, "sync", d, ap, "dbgs")
    c.wait("sync", "dbgs", ld)
    for e in ("tensor", "vector", "scalar", "gpsimd"):
        c.wait(e, "dbgs", ld)
    DBG["maps"].append(nm)


def emit_ffn0(c, xT, half):
    tok0 = half * HALF
    with ExitStack() as es:
        xn = asb(c, "f0xn", [128, KC, HALF], BF16, es)[:]
        acc = asb(c, "f0acc", [128, KC, HALF], F32, es)[:]
        combT = asb(c, "f0cT", [E0, HALF], BF16, es)[:]
        sel_sb = asb(c, "f0sel", [E0, 2, 128], BF16, es)[:]
        gps = [aps(c, "f0g%d" % i, [128, TOK], F32, es)[:] for i in range(2)]
        ups = [aps(c, "f0u%d" % i, [128, TOK], F32, es)[:] for i in range(2)]
        p_ed = aps(c, "f0ed", [128, 3, TOK], F32, es)[:]
        p_cb = aps(c, "f0pcb", [128, TOK], F32, es)[:]

        with ExitStack() as bs:
            gate_f = asb(c, "f0gtf", [128, KC, E0], F32, bs)[:]
            gate_b = asb(c, "f0gtb", [128, KC, E0], BF16, bs)[:]
            statB = asb(c, "f0statB", [128, 2, TOK], F32, bs)[:]
            lnb0 = asb(c, "f0lnb", [128, 4, TOK], BF16, bs)[:]
            statB2 = asb(c, "f0statB2", [128, 2, TOK], F32, bs)[:]
            sq = asb(c, "f0sq", [128, 2, TOK], F32, bs)[:]
            scoresT = asb(c, "f0scT", [E0, HALF], F32, bs)[:]
            scoresT2 = asb(c, "f0scT2", [E0, HALF], F32, bs)[:]
            stok = asb(c, "f0stok", [128, NTC, E0], F32, bs)[:]
            r8 = asb(c, "f0r8", [128, 4, 8], F32, bs)[:]
            r64 = asb(c, "f0r64", [128, 2, E0], F32, bs)[:]
            r1 = asb(c, "f0r1", [128, 4], F32, bs)[:]
            ctok = asb(c, "f0ctok", [128, E0], F32, bs)[:]

            ldg = dma(c, "sync", gate_f[:], c.dram["gateT"].rearrange(
                "(kc kp) e -> kp kc e", kp=128), "ldw_gate")
            c.wait("scalar", "ldw_gate", ldg)
            a_g = s_op(c, lambda eng, o=gate_b[:], i=gate_f[:]:
                       eng.activation(out=o, in_=i, func=AF.Copy))

            ln_sc = {"p0": gps[0], "p1": gps[1], "statB": statB,
                     "statB2": statB2, "sq": sq, "lnb": lnb0}
            emit_ln(c, xT, xn, tok0, HALF, ln_sc)
            dbg_dump(c, "xn_h%d" % half, xn, [128, KC, HALF], BF16)

            c.require("tensor", "xn")
            c.wait("tensor", "ac", a_g)
            lg_done = []
            for it in range(NT):
                last = None
                if lg_done:
                    c.wait("tensor", "ac", lg_done[-1])
                for kc in range(KC):
                    last = t_mm(c, p_cb[0:E0, :], gate_b[:, kc],
                                xn[:, kc, it * TOK:(it + 1) * TOK],
                                start=(kc == 0), stop=(kc == KC - 1),
                                inc=(kc == KC - 1))
                c.wait("scalar", "pe", last)
                a1 = s_op(c, lambda eng, o=scoresT[:, it * TOK:(it + 1) * TOK],
                          i=p_cb[0:E0, :]: eng.activation(out=o, in_=i, func=AF.Sigmoid))
                lg_done.append(a1)
            c.wait("vector", "ac", lg_done[-1])
            gb = c.bufs["gbias"]
            vsc = v_op(c, lambda eng, o=scoresT2[:], i=scoresT[:], s=gb[:]:
                       eng.tensor_scalar(o, i, s, None, op0=ALU.add))
            ident = c.bufs["ident"]
            c.wait("tensor", "dv", vsc)
            tr_done = {}
            for tc in range(NTC):
                if tc >= 2:
                    c.wait("tensor", "ac", tr_done[tc - 2])
                pe1 = t_transpose(c, p_ed[:, tc % 2, 0:E0],
                                  scoresT2[:, tc * 128:(tc + 1) * 128],
                                  ident[0:E0, 0:E0])
                c.wait("scalar", "pe", pe1)
                tr_done[tc] = s_op(c, lambda eng, o=stok[:, tc], i=p_ed[:, tc % 2, 0:E0]:
                                   eng.activation(out=o, in_=i, func=AF.Copy))
            last_cT = None
            last_pe2 = None
            for tc in range(NTC):
                c.wait("vector", "ac", tr_done[tc])
                st_ap = stok[:, tc]
                v1 = v_op(c, lambda eng, o=r8[:, 0],
                          i=st_ap.rearrange("p (g j) -> p g j", g=8):
                          eng.reduce_max(o, i, axis=AX.X))
                c.wait("vector", "dv", v1)
                v2 = v_op(c, lambda eng, o=r8[:, 1], i=r8[:, 0]: eng.max(out=o, in_=i))
                c.wait("vector", "dv", v2)
                v3 = v_op(c, lambda eng, o=r8[:, 2], i=r8[:, 0], s=r8[:, 1, 3:4]:
                          eng.tensor_scalar(o, i, s, None, op0=ALU.is_ge))
                c.wait("vector", "dv", v3)
                for g in range(8):
                    v_op(c, lambda eng, o=r64[:, 0, g * 8:(g + 1) * 8],
                         i=st_ap[:, g * 8:(g + 1) * 8], s=r8[:, 2, g:g + 1]:
                         eng.tensor_scalar(o, i, s, None, op0=ALU.mult),
                         inc=(g == 7))
                c.wait("vector", "dv", c.counts["dv"])
                v5 = v_op(c, lambda eng, o=r8[:, 3], i=r64[:, 0]: eng.max(out=o, in_=i))
                c.wait("vector", "dv", v5)
                v6 = v_op(c, lambda eng, o=r64[:, 1], i=r64[:, 0], s=r8[:, 3, 5:6]:
                          eng.tensor_scalar(o, i, s, None, op0=ALU.is_ge))
                c.wait("vector", "dv", v6)
                v7 = v_op(c, lambda eng, o=r64[:, 1], a=r64[:, 1], b=r64[:, 0]:
                          eng.tensor_mul(o, a, b))
                c.wait("vector", "dv", v7)
                v8 = v_op(c, lambda eng, o=r1[:, 0:1], i=r64[:, 1]:
                          eng.reduce_sum(o, i, axis=AX.X))
                c.wait("vector", "dv", v8)
                v9 = v_op(c, lambda eng, o=r1[:, 1:2], i=r1[:, 0:1]:
                          eng.tensor_single_scalar(o, i, 1e-8, op=ALU.add))
                c.wait("vector", "dv", v9)
                v10 = v_op(c, lambda eng, o=r1[:, 2:3], i=r1[:, 1:2]:
                           eng.reciprocal(o, i))
                c.wait("vector", "dv", v10)
                if last_pe2 is not None:
                    c.wait("vector", "pe", last_pe2)
                v11 = v_op(c, lambda eng, o=ctok[:], i=r64[:, 1], s=r1[:, 2:3]:
                           eng.tensor_scalar(o, i, s, None, op0=ALU.mult))
                c.wait("tensor", "dv", v11)
                if last_cT is not None:
                    c.wait("tensor", "ac", last_cT)
                pe2 = t_transpose(c, p_ed[0:E0, 2, 0:128], ctok, ident[:])
                last_pe2 = pe2
                c.wait("scalar", "pe", pe2)
                last_cT = s_op(c, lambda eng, o=combT[:, tc * 128:(tc + 1) * 128],
                               i=p_ed[0:E0, 2, 0:128]:
                               eng.activation(out=o, in_=i, func=AF.Copy))
            c.produce("combT", "ac", last_cT)
            dbg_dump(c, "scoresT_h%d" % half, scoresT2, [E0, HALF], F32)
            dbg_dump(c, "combT_h%d" % half, combT, [E0, HALF], BF16)
            rt_dv = c.counts["dv"]; rt_ac = c.counts["ac"]; rt_pe = c.counts["pe"]
            for r in ("wgu0", "wgu1", "wd0", "wd1", "stage0", "stage1"):
                c.release_add(r, "dv", rt_dv)
                c.release_add(r, "ac", rt_ac)
                c.release_add(r, "pe", rt_pe)
            for b in range(3):
                c.release_add("ped%d" % b, "ac", rt_ac)
                c.release_add("ped%d" % b, "dv", rt_dv)
            c.release_add("pcb", "ac", rt_ac)
            for p in range(2):
                c.release_add("gps%d" % p, "dv", rt_dv)
                c.release_add("ups%d" % p, "dv", rt_dv)

        wg = asb(c, "f0wg", [128, 2, KC * 512], BF16, es)[:]
        wu = asb(c, "f0wu", [128, 2, KC * 512], BF16, es)[:]
        wd = asb(c, "f0wd", [128, 2, IC0 * 1024], BF16, es)[:]
        stage = asb(c, "f0st", [128, 2, 1024], F32, es)[:]
        hst = asb(c, "f0h", [128, 8, TOK], BF16, es)[:]
        sg = asb(c, "f0sg", [128, 2, TOK], F32, es)[:]
        ht = asb(c, "f0ht", [128, 2, TOK], F32, es)[:]
        cbs = asb(c, "f0cb", [128, 2, TOK], BF16, es)[:]

        stream_ctr = [0]

        def stream_chunk(dst_ap, src_ap, key=None):
            sslot = stream_ctr[0] % 2
            stream_ctr[0] += 1
            c.acquire("sync", "stage%d" % sslot)
            dst_stage = stage[:, sslot, :]
            if len(src_ap.shape) == 3:
                dst_stage = dst_stage.rearrange("p (q m) -> p q m",
                                                q=src_ap.shape[1])
            ld = dma(c, "sync", dst_stage, src_ap, "ldw_st%d" % sslot)
            c.wait("gpsimd", "ldw_st%d" % sslot, ld)
            if key:
                c.acquire("gpsimd", key)
            av = g_op(c, lambda eng, o=dst_ap, i=stage[:, sslot, :]:
                      eng.tensor_copy(o, i))
            c.release("stage%d" % sslot, "gp", av)
            return av

        wg_f = wg.rearrange("p s m -> p (s m)")
        wu_f = wu.rearrange("p s m -> p (s m)")
        wd_f = wd.rearrange("p s m -> p (s m)")
        ld_sh = None
        for i, (dst, drn) in enumerate([(wg_f, "sgT"), (wu_f, "suT"),
                                        (wd_f, "sdT")]):
            for r in ("wgu0", "wgu1", "wd0", "wd1"):
                c.acquire("sync", r)
            if i in c.wc_written["sh"]:
                c.wait_dep("sync", c.wc_written["sh"][i])
                ldv = dma(c, "sync", dst, c.wcache["sh"][i], "ldshc")
                ld_sh = ("ldshc", ldv)
            else:
                last = None
                for ch in range(8):
                    last = stream_chunk(dst[:, ch * 1024:(ch + 1) * 1024],
                                        c.dram[drn][ch * 128:(ch + 1) * 128])
                wb_cache(c, "sh", i, dst, ("gp", last), frees=("wgu0", "wgu1", "wd0", "wd1"))
                ld_sh = ("gp", last)
        c.wait_dep("tensor", ld_sh)
        acc_last = {}
        h_done = {}
        for it in range(NT):
            ts = slice(it * TOK, (it + 1) * TOK)
            for ic in range(8):
                par = ic % 2
                c.acquire("tensor", "gps%d" % par)
                c.acquire("tensor", "ups%d" % par)
                for kc in range(KC):
                    t_mm(c, gps[par][:], wg_f[:, kc * 1024 + ic * 128: kc * 1024 + ic * 128 + 128],
                         xn[:, kc, ts], start=(kc == 0), stop=(kc == KC - 1))
                last_u = None
                for kc in range(KC):
                    last_u = t_mm(c, ups[par][:], wu_f[:, kc * 1024 + ic * 128: kc * 1024 + ic * 128 + 128],
                                  xn[:, kc, ts], start=(kc == 0), stop=(kc == KC - 1),
                                  inc=(kc == KC - 1))
                c.wait("scalar", "pe", last_u)
                c.acquire("scalar", "sg%d" % par)
                a_s = s_op(c, lambda eng, o=sg[:, par], i=gps[par][:]:
                           eng.activation(out=o, in_=i, func=AF.Silu))
                c.release("gps%d" % par, "ac", a_s)
                c.wait("vector", "ac", a_s)
                c.acquire("vector", "hst%d" % ic)
                v_h = v_op(c, lambda eng, o=hst[:, ic], a=sg[:, par], b=ups[par][:]:
                           eng.tensor_mul(o, a, b))
                c.release("ups%d" % par, "dv", v_h)
                c.release("sg%d" % par, "dv", v_h)
                h_done[ic] = v_h
            c.wait("tensor", "dv", h_done[7])
            for hc in range(KC):
                bank = hc % 3
                c.acquire("tensor", "ped%d" % bank)
                last = None
                for ic in range(8):
                    last = t_mm(c, p_ed[:, bank, :],
                                wd_f[:, ic * 1024 + hc * 128: ic * 1024 + hc * 128 + 128],
                                hst[:, ic], start=(ic == 0), stop=(ic == 7),
                                inc=(ic == 7))
                c.wait("scalar", "pe", last)
                a_i = s_op(c, lambda eng, o=acc[:, hc, ts], i=p_ed[:, bank, :]:
                           eng.activation(out=o, in_=i, func=AF.Copy))
                c.release("ped%d" % bank, "ac", a_i)
                acc_last[(hc, it)] = ("ac", a_i)
            pe_now = c.counts["pe"]
            for ic in range(8):
                c.release("hst%d" % ic, "pe", pe_now)

        dbg_dump(c, "accsh_h%d" % half, acc, [128, KC, HALF], F32)

        def stream_expert_gu(e):
            slot = e % 2
            key = "wgu%d" % slot
            if e in c.wc_written["eg"]:
                c.wait_dep("sync", c.wc_written["eg"][e])
                c.wait_dep("sync", c.wc_written["eu"][e])
                c.acquire("sync", key)
                dma(c, "sync", wg[:, slot], c.wcache["eg"][e], "ldgu%d" % slot)
                ldv = dma(c, "sync", wu[:, slot], c.wcache["eu"][e],
                          "ldgu%d" % slot)
                last = ("ldgu%d" % slot, ldv)
            else:
                lv = None
                for ch in range(IC0):
                    lv = stream_chunk(wg[:, slot, ch * 1024:(ch + 1) * 1024],
                                      c.dram["egT"][e, ch * 256:(ch + 1) * 256].rearrange(
                                          "(q kp) m -> kp q m", kp=128), key)
                wb_cache(c, "eg", e, wg[:, slot], ("gp", lv), frees=("wgu%d" % slot,))
                for ch in range(IC0):
                    lv = stream_chunk(wu[:, slot, ch * 1024:(ch + 1) * 1024],
                                      c.dram["euT"][e, ch * 256:(ch + 1) * 256].rearrange(
                                          "(q kp) m -> kp q m", kp=128), key)
                wb_cache(c, "eu", e, wu[:, slot], ("gp", lv), frees=("wgu%d" % slot,))
                last = ("gp", lv)
            c.acquire("sync", "sel%d" % slot)
            lds = dma(c, "sync", sel_sb[:, slot, :],
                      c.dram["sel64"][:, e * 128:(e + 1) * 128],
                      "ldw_sel%d" % slot)
            return last, lds

        def stream_expert_ed(e):
            slot = e % 2
            key = "wd%d" % slot
            if e in c.wc_written["ed"]:
                c.wait_dep("sync", c.wc_written["ed"][e])
                c.acquire("sync", key)
                ldv = dma(c, "sync", wd[:, slot], c.wcache["ed"][e],
                          "lded%d" % slot)
                return ("lded%d" % slot, ldv)
            lv = None
            for ch in range(IC0):
                lv = stream_chunk(wd[:, slot, ch * 1024:(ch + 1) * 1024],
                                  c.dram["edT"][e, ch * 128:(ch + 1) * 128], key)
            wb_cache(c, "ed", e, wd[:, slot], ("gp", lv), frees=("wd%d" % slot,))
            return ("gp", lv)

        pe_sh = c.counts["pe"]
        for s in (0, 1):
            c.release("wgu%d" % s, "pe", pe_sh)
            c.release("wd%d" % s, "pe", pe_sh)
        gu_done = {}
        ed_cast = {}
        sel_done = {}
        gu_done[0], sel_done[0] = stream_expert_gu(0)
        ed_cast[0] = stream_expert_ed(0)
        gu_done[1], sel_done[1] = stream_expert_gu(1)
        ed_cast[1] = stream_expert_ed(1)
        hprime = {}

        def emit_ed(e, it):
            ts = slice(it * TOK, (it + 1) * TOK)
            slot = e % 2
            jslot = (e * NT + it) % 2
            c.wait_dep("tensor", ed_cast[e])
            c.wait("tensor", "dv", hprime[(e, it)])
            for hc in range(KC):
                bank = hc % 3
                c.acquire("tensor", "ped%d" % bank)
                last = None
                for ic in range(IC0):
                    last = t_mm(c, p_ed[:, bank, :],
                                wd[:, slot, ic * 1024 + hc * 128: ic * 1024 + hc * 128 + 128],
                                hst[:, jslot * 4 + ic],
                                start=(ic == 0), stop=(ic == IC0 - 1),
                                inc=(ic == IC0 - 1))
                c.wait("vector", "pe", last)
                c.wait_dep("vector", acc_last.get((hc, it)))
                vadd = v_op(c, lambda eng, o=acc[:, hc, ts], a=acc[:, hc, ts],
                            b=p_ed[:, bank, :]: eng.tensor_add(o, a, b))
                c.release("ped%d" % bank, "dv", vadd)
                acc_last[(hc, it)] = ("dv", vadd)
            pe_now = c.counts["pe"]
            for ic in range(IC0):
                c.release("hst%d" % (jslot * 4 + ic), "pe", pe_now)
            if it == NT - 1:
                c.release("wd%d" % slot, "pe", pe_now)
                c.release("sel%d" % slot, "pe", pe_now)

        jobs = [(e, it) for e in range(E0) for it in range(NT)]
        for ji, (e, it) in enumerate(jobs):
            slot = e % 2
            jslot = ji % 2
            ts = slice(it * TOK, (it + 1) * TOK)
            if it == 0:
                c.wait_dep("tensor", gu_done[e])
                c.wait("tensor", "ldw_sel%d" % slot, sel_done[e])
            c.acquire("tensor", "pcb")
            c.require("tensor", "combT")
            cbm = t_mm(c, p_cb[:], sel_sb[:, slot, :], combT[:, ts],
                       start=True, stop=True, inc=True)
            c.wait("scalar", "pe", cbm)
            c.acquire("scalar", "cbs%d" % jslot)
            cbe = s_op(c, lambda eng, o=cbs[:, jslot], i=p_cb[:]:
                       eng.activation(out=o, in_=i, func=AF.Copy))
            c.release("pcb", "ac", cbe)
            for ic in range(IC0):
                par = ic % 2
                c.acquire("tensor", "gps%d" % par)
                c.acquire("tensor", "ups%d" % par)
                for kc in range(KC):
                    t_mm(c, gps[par][:], wg[:, slot, kc * 512 + ic * 128: kc * 512 + ic * 128 + 128],
                         xn[:, kc, ts], start=(kc == 0), stop=(kc == KC - 1))
                last_u = None
                for kc in range(KC):
                    last_u = t_mm(c, ups[par][:], wu[:, slot, kc * 512 + ic * 128: kc * 512 + ic * 128 + 128],
                                  xn[:, kc, ts], start=(kc == 0), stop=(kc == KC - 1),
                                  inc=(kc == KC - 1))
                c.wait("scalar", "pe", last_u)
                c.acquire("scalar", "sg%d" % par)
                a_s = s_op(c, lambda eng, o=sg[:, par], i=gps[par][:]:
                           eng.activation(out=o, in_=i, func=AF.Silu))
                c.release("gps%d" % par, "ac", a_s)
                c.wait("vector", "ac", a_s)
                c.acquire("vector", "ht%d" % par)
                v_h = v_op(c, lambda eng, o=ht[:, par], a=sg[:, par], b=ups[par][:]:
                           eng.tensor_mul(o, a, b))
                c.release("ups%d" % par, "dv", v_h)
                c.release("sg%d" % par, "dv", v_h)
                c.wait("vector", "dv", v_h)
                c.wait("vector", "ac", cbe)
                c.acquire("vector", "hst%d" % (jslot * 4 + ic))
                g_h = v_op(c, lambda eng, o=hst[:, jslot * 4 + ic], a=ht[:, par],
                           b=cbs[:, jslot]: eng.tensor_tensor(o, a, b, op=ALU.mult))
                c.release("ht%d" % par, "dv", g_h)
                if ic == IC0 - 1:
                    c.release("cbs%d" % jslot, "dv", g_h)
                    hprime[(e, it)] = g_h
            if it == NT - 1:
                pe_now = c.counts["pe"]
                c.release("wgu%d" % slot, "pe", pe_now)
                c.release("sel%d" % slot, "pe", pe_now)
                if e + 2 < E0:
                    gu_done[e + 2], sel_done[e + 2] = stream_expert_gu(e + 2)
            if ji >= 1:
                pe_, pit_ = jobs[ji - 1]
                emit_ed(pe_, pit_)
                if pit_ == NT - 1 and pe_ + 2 < E0:
                    ed_cast[pe_ + 2] = stream_expert_ed(pe_ + 2)
        emit_ed(*jobs[-1])

        c.wait("vector", "dv", c.counts["dv"])
        for kc in range(KC):
            v_op(c, lambda eng, o=xT[:, kc, tok0:tok0 + HALF],
                 a=acc[:, kc, :], x2=xT[:, kc, tok0:tok0 + HALF]:
                 eng.scalar_tensor_tensor(o, a, 0.5, x2, op0=ALU.mult, op1=ALU.add))
        dbg_dump(c, "acc_h%d" % half, acc, [128, KC, HALF], F32)
        c.produce("xT", "dv", c.counts["dv"])
        c.barrier()


NQC = S // 128
NJC = S // 128
QT4 = S // TOK
AVT = 128
NAV = S // AVT


_ATTN_UID = [0]


def emit_attn(c, xT, layer):
    _ATTN_UID[0] += 1
    uid = _ATTN_UID[0]
    ident = c.bufs["ident"]
    identb = c.bufs["identb"]
    ones_f = c.bufs["ones_f"]
    with ExitStack() as es:
        xn = asb(c, "atxn", [128, KC, S], BF16, es)[:]
        attnout = asb(c, "atao", [128, NHP, S], BF16, es)[:]
        qt = asb(c, "atqkv", [128, 3, S], BF16, es)[:]
        wqf = asb(c, "atwqf", [128, KC, 128], F32, es)[:]
        wqb = asb(c, "atwqb", [128, KC, 384], BF16, es)[:]
        if layer == 1:
            xab = asb(c, "atxab", [96, S], BF16, es)[:]
            aqb = asb(c, "ataqb", [128, KC, 96], BF16, es)[:]
            bqf = asb(c, "atbqf", [72, 128], F32, es)[:]
            bqb = asb(c, "atbqb", [72, 128], BF16, es)[:]
            bias_sb = asb(c, "atbias", [128, 3], F32, es)[:]
            aob = asb(c, "ataob", [128, KC, 8], BF16, es)[:]
            xaO = xab[0:8, :]
        PS = aps(c, "atPS", [128, 8, TOK], F32, es)[:]
        psA = PS[:, 0, :]
        psB = PS[:, 1, :]
        psE = PS[:, 2, :]

        with ExitStack() as bs:
            statB = asb(c, "atstatB", [128, 2, TOK], F32, bs)[:]
            lnba = asb(c, "atlnb", [128, 4, TOK], BF16, bs)[:]
            statB2 = asb(c, "atstatB2", [128, 2, TOK], F32, bs)[:]
            sq = asb(c, "atsq", [128, 2, TOK], F32, bs)[:]
            ln_sc = {"p0": psA, "p1": psB, "statB": statB, "statB2": statB2,
                     "sq": sq, "lnb": lnba}
            emit_ln(c, xT, xn, 0, S, ln_sc)

            if layer == 0:
                c.require("vector", "xn")
                gp1 = c.bufs["gp1"]
                sqr_mm = {}
                stats_read = None
                prev_apply = None
                for it in range(QT4):
                    t0 = it * TOK
                    if stats_read is not None:
                        c.wait("tensor", "dv", stats_read)
                    if prev_apply is not None:
                        c.wait("vector", "dv", prev_apply)
                    for kc in range(KC):
                        par = kc % 2
                        gi = it * KC + kc
                        if gi >= 2:
                            c.wait("vector", "pe", sqr_mm[gi - 2])
                        vq = v_op(c, lambda eng, o=lnba[:, par], i=xn[:, kc, t0:t0 + TOK]:
                                  eng.tensor_tensor(o, i, i, op=ALU.mult))
                        c.wait("tensor", "dv", vq)
                        sqr_mm[gi] = t_mm(c, psA[:], c.bufs["ones_b"][:], lnba[:, par],
                                          start=(kc == 0), stop=(kc == KC - 1),
                                          inc=True)
                    pe_done = c.counts["pe"]
                    sB = statB[:, it % 2]
                    c.wait("scalar", "pe", pe_done)
                    if prev_apply is not None:
                        c.wait("scalar", "dv", prev_apply)
                    a1 = s_op(c, lambda eng, o=sB, i=psA[:]:
                              eng.activation(out=o, in_=i, func=AF.Sqrt))
                    c.wait("vector", "ac", a1)
                    v1 = v_op(c, lambda eng, o=sB, i=sB:
                              eng.tensor_single_scalar(o, i, 1e-12, op=ALU.max))
                    c.wait("vector", "dv", v1)
                    v2 = v_op(c, lambda eng, o=sB, i=sB: eng.reciprocal(o, i))
                    stats_read = v2
                    c.wait("vector", "dv", v2)
                    for kc in range(KC):
                        par = kc % 2
                        vv = v_op(c, lambda eng, o=sq[:, par], x=xn[:, kc, t0:t0 + TOK],
                                  r=sB: eng.tensor_mul(o, x, r))
                        c.wait("vector", "dv", vv)
                        v_op(c, lambda eng, o=xn[:, kc, t0:t0 + TOK], i=sq[:, par],
                             s=gp1[:, kc:kc + 1]:
                             eng.tensor_scalar(o, i, s, None, op0=ALU.mult))
                    prev_apply = c.counts["dv"]
                c.produce("xn", "dv", c.counts["dv"])
            c.barrier()

        vtm = asb(c, "atvtm", [128, NJC, 130], BF16, es)[:]
        atile = asb(c, "atat", [128, 3, S], BF16, es)[:]
        sqb = asb(c, "atsqb", [128, S], BF16, es)[:]
        mstat = asb(c, "atmst", [65, 16], F32, es)[:]
        alCM = asb(c, "atalcm", [128, NJC], F32, es)[:]
        orec = asb(c, "atorec", [1, 4, TOK], F32, es)[:]
        nrm = asb(c, "atnrm", [64, 4, TOK], F32, es)[:]
        if layer == 1:
            Ss = asb(c, "atSs", [128, S], F32, es)[:]
        von = v_op(c, lambda eng, o=vtm[:, :, 64:65]: eng.memset(o, 1.0))
        von = v_op(c, lambda eng, o=vtm[:, :, 129:130]: eng.memset(o, 1.0))

        xn_dep = c.ready["xn"]
        wname = "wqkv0" if layer == 0 else "wqkv1"
        if layer == 1:
            aqf_v = wqf.rearrange("p a b -> p (a b)")[:, 0:96 * KC].rearrange(
                "p (a b) -> p a b", a=KC)
            lda = dma(c, "sync", aqf_v, c.dram["aqkv1"].rearrange(
                "kc kp m -> kp kc m"), "ldw_aq")
            c.wait("gpsimd", "ldw_aq", lda)
            ga = g_op(c, lambda eng, o=aqb[:], i=aqf_v: eng.tensor_copy(o, i))
            aof_v = wqf.rearrange("p a b -> p (a b)")[:, 0:8 * KC].rearrange(
                "p (a b) -> p a b", a=KC)
            c.wait("sync", "gp", ga)
            ldao = dma(c, "sync", aof_v, c.dram["AoT"].rearrange(
                "(kc kp) r -> kp kc r", kp=128), "ldw_ao")
            c.wait("gpsimd", "ldw_ao", ldao)
            ga2a = g_op(c, lambda eng, o=aob[:], i=aof_v: eng.tensor_copy(o, i))
            c.release("atwqf", "gp", ga2a)

        for hp in range(NHP):
            qfam = "qkv%d" % layer
            if hp in c.wc_written[qfam]:
                c.wait_dep("sync", c.wc_written[qfam][hp])
                c.acquire("sync", "atwqb")
                ldq = dma(c, "sync", wqb.rearrange("p a b -> p (a b)"),
                          c.wcache[qfam][hp], "ldqkvc")
                gw_dep = ("ldqkvc", ldq)
            else:
                c.acquire("gpsimd", "atwqb")
                gw = None
                for p in range(3):
                    c.acquire("sync", "atwqf")
                    ldw = dma(c, "sync", wqf[:], c.dram[wname][hp, :, :, p].rearrange(
                        "kc kp m -> kp kc m"), "ldw_qkv")
                    c.wait("gpsimd", "ldw_qkv", ldw)
                    gw = g_op(c, lambda eng, o=wqb[:, :, p * 128:(p + 1) * 128], i=wqf[:]:
                              eng.tensor_copy(o, i))
                    c.release("atwqf", "gp", gw)
                wb_cache(c, qfam, hp, wqb.rearrange("p a b -> p (a b)"),
                         ("gp", gw), frees=("atwqb",))
                gw_dep = ("gp", gw)
            if layer == 1:
                c.acquire("sync", "atbq")
                for p in range(3):
                    dma(c, "sync", bqf[32 * p:32 * p + 8, :],
                        c.dram["bqkv1"][hp, :, p, :], "ldw_bq")
                ldbias = dma(c, "sync", bias_sb[:], c.dram["biasqkv1"][hp], "ldw_bq")
                c.wait("gpsimd", "ldw_bq", ldbias)
                gb = None
                for p in range(3):
                    gb = g_op(c, lambda eng, o=bqb[32 * p:32 * p + 8, :],
                              i=bqf[32 * p:32 * p + 8, :]: eng.tensor_copy(o, i))
            c.wait_dep("tensor", xn_dep)
            c.wait_dep("tensor", gw_dep)
            if layer == 1:
                c.wait("tensor", "gp", gb)
                c.wait("tensor", "gp", ga)
                c.wait("scalar", "ldw_bq", ldbias)
                for it in range(QT4):
                    ts = slice(it * TOK, (it + 1) * TOK)
                    c.acquire("tensor", "atb2")
                    last = None
                    for kc in range(KC):
                        last = t_mm(c, psE[0:96, :], aqb[:, kc, :], xn[:, kc, ts],
                                    start=(kc == 0), stop=(kc == KC - 1),
                                    inc=(kc == KC - 1))
                    c.wait("scalar", "pe", last)
                    ax = s_op(c, lambda eng, o=xab[:, ts], i=psE[0:96, :]:
                              eng.activation(out=o, in_=i, func=AF.Copy, scale=0.125))
                    c.release("atb2", "ac", ax)
                xab_ready = c.counts["ac"]
                c.wait("tensor", "ac", xab_ready)
            for p in range(3):
                for it in range(QT4):
                    ts = slice(it * TOK, (it + 1) * TOK)
                    bank, bname = (psA, "atb0") if (p * QT4 + it) % 2 == 0 else (psB, "atb1")
                    c.acquire("tensor", bname)
                    nk = KC if layer == 1 else KC - 1
                    for kc in range(nk):
                        t_mm(c, bank[:], wqb[:, kc, p * 128:(p + 1) * 128],
                             xn[:, kc, ts], start=(kc == 0), stop=False)
                    if layer == 1:
                        last = t_mm(c, bank[:], bqb[32 * p:32 * p + 8, :],
                                    xab[32 * p:32 * p + 8, ts],
                                    start=False, stop=True, inc=True)
                    else:
                        last = t_mm(c, bank[:], wqb[:, KC - 1, p * 128:(p + 1) * 128],
                                    xn[:, KC - 1, ts], start=False, stop=True, inc=True)
                    if layer == 0:
                        c.wait("scalar", "pe", last)
                        scale = 0.125 if p == 0 else 1.0
                        ev = s_op(c, lambda eng, o=qt[:, p, ts], i=bank[:], sc=scale:
                                  eng.activation(out=o, in_=i, func=AF.Copy, scale=sc))
                    else:
                        c.wait("vector", "pe", last)
                        c.wait("vector", "ldw_bq", ldbias)
                        ev = v_op(c, lambda eng, o=qt[:, p, ts], i=bank[:],
                                  b=bias_sb[:, p:p + 1]:
                                  eng.tensor_scalar(o, i, b, None, op0=ALU.add))
                        c.release(bname, "dv", ev)
                        continue
                    c.release(bname, "ac", ev)
            c.release("atwqb", "pe", c.counts["pe"])
            if layer == 1:
                c.release("atbq", "ac", c.counts["ac"])
            c.wait("tensor", "ac", c.counts["ac"])
            c.wait("tensor", "dv", c.counts["dv"])
            for jc in range(NJC):
                bname = "atb%d" % (2 + jc % 2)
                c.acquire("tensor", bname)
                pv = t_mm(c, PS[:, 2 + jc % 2, 0:128],
                          qt[:, 2, jc * 128:(jc + 1) * 128], identb[:],
                          start=True, stop=True, inc=True)
                c.wait("vector", "pe", pv)
                c.wait("vector", "dv", von)
                v_op(c, lambda eng, o=vtm[:, jc, 0:64], i=PS[:, 2 + jc % 2, 0:64]:
                     eng.tensor_copy(o, i))
                v2 = v_op(c, lambda eng, o=vtm[:, jc, 65:129],
                          i=PS[:, 2 + jc % 2, 64:128]: eng.tensor_copy(o, i))
                c.release(bname, "dv", v2)
            vtm_ready = c.counts["dv"]
            blk2 = c.bufs["blk2"]
            c.wait("vector", "ac", c.counts["ac"])
            c.wait("vector", "dv", c.counts["dv"])
            sq_done = None
            for side in range(2):
                if sq_done is not None:
                    c.wait("vector", "pe", sq_done)
                vq = v_op(c, lambda eng, o=sqb[:], i=qt[:, side, :]:
                          eng.tensor_tensor(o, i, i, op=ALU.mult))
                c.wait("tensor", "dv", vq)
                for t in range(4):
                    c.acquire("tensor", "atb%d" % (4 + t))
                    sq_done = t_mm(c, PS[0:65, 4 + t, :], blk2[:, 0:65],
                                   sqb[:, t * TOK:(t + 1) * TOK],
                                   start=True, stop=True, inc=True)
                c.wait("vector", "pe", sq_done)
                for t in range(4):
                    vr = v_op(c, lambda eng,
                              o=mstat[:, side * 5 + t:side * 5 + t + 1],
                              i=PS[0:65, 4 + t, :]: eng.reduce_max(o, i, axis=AX.X))
                    c.release("atb%d" % (4 + t), "dv", vr)
                c.wait("vector", "dv", c.counts["dv"])
                v_op(c, lambda eng, o=mstat[:, side * 5 + 4:side * 5 + 5],
                     i=mstat[:, side * 5:side * 5 + 4]:
                     eng.reduce_max(o, i, axis=AX.X))
            c.wait("vector", "dv", c.counts["dv"])
            vmul = v_op(c, lambda eng, o=mstat[:, 10:11], a=mstat[:, 4:5],
                        b=mstat[:, 9:10]:
                        eng.tensor_scalar(o, a, b, None, op0=ALU.mult))
            c.wait("scalar", "dv", vmul)
            asq = s_op(c, lambda eng, o=mstat[:, 11:12], i=mstat[:, 10:11]:
                       eng.activation(out=o, in_=i, func=AF.Sqrt))
            c.wait("vector", "ac", asq)
            slp = c.bufs["slp2"]
            vB = v_op(c, lambda eng, o=mstat[:, 12:13], i=mstat[:, 11:12],
                      s=slp[:, layer, hp:hp + 1]:
                      eng.tensor_scalar(o, i, s, None, op0=ALU.add))
            if hp == 0:
                dbg_dump(c, "qt_u%d" % uid, qt, [128, 3, S], BF16)
                dbg_dump(c, "vtm_u%d" % uid, vtm, [128, NJC, 130], BF16)
                dbg_dump(c, "mst_u%d" % uid, mstat, [65, 16], F32)
            for hh in range(2):
                h = hp * 2 + hh
                r0 = 64 * hh
                c.wait("tensor", "dv", vB)
                c.acquire("tensor", "atb2")
                pb = t_mm(c, PS[:, 2, 0:1], ones_f[r0:r0 + 1, 0:128],
                          mstat[r0:r0 + 1, 12:13], start=True, stop=True, inc=True)
                c.wait("vector", "pe", pb)
                if layer == 1:
                    alC = c.bufs["alibiC"]
                    vbias = v_op(c, lambda eng, o=alCM[:, 0:NJC], i=alC[:, h, :],
                                 s=PS[:, 2, 0:1]:
                                 eng.tensor_scalar(o, i, s, None, op0=ALU.subtract))
                else:
                    vbias = v_op(c, lambda eng, o=alCM[:, 0:1], i=PS[:, 2, 0:1]:
                                 eng.tensor_single_scalar(o, i, -1.0, op=ALU.mult))
                c.release("atb2", "dv", vbias)
                c.wait("scalar", "dv", vbias)
                if hp == 0 and hh == 0:
                    dbg_dump(c, "alcm_u%d" % uid, alCM, [128, NJC], F32)
                c.wait("scalar", "dv", vtm_ready)
                c.wait("tensor", "dv", vtm_ready)
                pe_st = {}
                pe_av = {}
                av15 = {}
                exa = {}
                exb = {}
                for jc in range(NJC + 1):
                    if jc > 0:
                        pj = jc - 1
                        sl = pj % 3
                        c.wait("tensor", "ac", exa[pj])
                        for t4 in (0, 1):
                            c.acquire("tensor", "atb%d" % (4 + t4))
                            v = t_mm(c, PS[0:65, 4 + t4, :],
                                     vtm[:, pj, hh * 65:hh * 65 + 65],
                                     atile[:, sl, t4 * TOK:(t4 + 1) * TOK],
                                     start=(pj == 0), stop=(pj == NJC - 1),
                                     inc=(t4 == 1 or pj == NJC - 1))
                            if pj == NJC - 1:
                                av15[t4] = v
                        c.wait("tensor", "ac", exb[pj])
                        for t4 in (2, 3):
                            c.acquire("tensor", "atb%d" % (4 + t4))
                            v = t_mm(c, PS[0:65, 4 + t4, :],
                                     vtm[:, pj, hh * 65:hh * 65 + 65],
                                     atile[:, sl, t4 * TOK:(t4 + 1) * TOK],
                                     start=(pj == 0), stop=(pj == NJC - 1),
                                     inc=(t4 == 3 or pj == NJC - 1))
                            if pj == NJC - 1:
                                av15[t4] = v
                        pe_av[pj] = v
                    if jc == NJC:
                        break
                    js = slice(jc * 128, (jc + 1) * 128)
                    if jc > 0:
                        c.wait("tensor", "ac", exa[jc - 1])
                    for t4 in (0, 1):
                        c.acquire("tensor", "atb%d" % t4)
                        v = t_mm(c, PS[:, t4, :], qt[r0:r0 + 64, 1, js],
                                 qt[r0:r0 + 64, 0, t4 * TOK:(t4 + 1) * TOK],
                                 start=True, stop=True, inc=(t4 == 1))
                    pe_st[(jc, 0)] = v
                    if jc > 0:
                        c.wait("tensor", "ac", exb[jc - 1])
                    for t4 in (2, 3):
                        c.acquire("tensor", "atb%d" % t4)
                        v = t_mm(c, PS[:, t4, :], qt[r0:r0 + 64, 1, js],
                                 qt[r0:r0 + 64, 0, t4 * TOK:(t4 + 1) * TOK],
                                 start=True, stop=True, inc=(t4 == 3))
                    pe_st[(jc, 1)] = v
                    sl = jc % 3
                    bias_ap = alCM[:, jc:jc + 1] if layer == 1 else alCM[:, 0:1]
                    c.wait("scalar", "pe", pe_st[(jc, 0)])
                    if jc >= 3:
                        c.wait("scalar", "pe", pe_av[jc - 3])
                    exa[jc] = s_op(c, lambda eng, o=atile[:, sl, 0:2 * TOK],
                                   i=PS[:, 0:2, :].rearrange("p a b -> p (a b)"),
                                   b=bias_ap:
                                   eng.activation(out=o, in_=i, func=AF.Exp, bias=b))
                    c.wait("scalar", "pe", pe_st[(jc, 1)])
                    exb[jc] = s_op(c, lambda eng, o=atile[:, sl, 2 * TOK:4 * TOK],
                                   i=PS[:, 2:4, :].rearrange("p a b -> p (a b)"),
                                   b=bias_ap:
                                   eng.activation(out=o, in_=i, func=AF.Exp, bias=b))
                vrs = {}
                pb2s = {}
                vcs = {}
                for t4 in range(4):
                    c.wait("vector", "pe", av15[t4])
                    vrs[t4] = v_op(c, lambda eng, o=orec[:, t4, :],
                                   i=PS[64:65, 4 + t4, :]: eng.reciprocal(o, i))
                c.wait("tensor", "ac", exb[NJC - 1])
                c.wait("tensor", "ac", exa[NJC - 1])
                for t4 in range(4):
                    c.wait("tensor", "dv", vrs[t4])
                    c.acquire("tensor", "atb%d" % t4)
                    pb2s[t4] = t_mm(c, PS[0:64, t4, :], ones_f[0:1, 0:64],
                                    orec[0:1, t4, :], start=True, stop=True,
                                    inc=True)
                for t4 in range(4):
                    c.wait("vector", "pe", pb2s[t4])
                    vcs[t4] = v_op(c, lambda eng, o=nrm[:, t4, :],
                                   i=PS[0:64, t4, :]: eng.tensor_copy(o, i))
                    c.release("atb%d" % t4, "dv", vcs[t4])
                for t4 in range(4):
                    c.wait("vector", "dv", vcs[t4])
                    vn = v_op(c, lambda eng,
                              o=attnout[r0:r0 + 64, hp, t4 * TOK:(t4 + 1) * TOK],
                              a=PS[0:64, 4 + t4, :], b=nrm[:, t4, :]:
                              eng.tensor_tensor(o, a, b, op=ALU.mult))
                    c.release("atb%d" % (4 + t4), "dv", vn)
        ao_ready = c.counts["dv"]
        dbg_dump(c, "ao_u%d" % uid, attnout, [128, NHP, S], BF16)

        if layer == 1:
            c.wait("tensor", "dv", ao_ready)
            for it in range(QT4):
                ts = slice(it * TOK, (it + 1) * TOK)
                c.acquire("tensor", "atb2")
                last = None
                for kc in range(KC):
                    last = t_mm(c, psE[0:8, :], aob[:, kc, :], attnout[:, kc, ts],
                                start=(kc == 0), stop=(kc == KC - 1),
                                inc=(kc == KC - 1))
                c.wait("scalar", "pe", last)
                axo = s_op(c, lambda eng, o=xaO[:, ts], i=psE[0:8, :]:
                           eng.activation(out=o, in_=i, func=AF.Copy, scale=0.125))
                c.release("atb2", "ac", axo)
            xaO_ready = c.counts["ac"]
            c.require("vector", "xT")
            c.require("vector", "xn")
            for kc in range(KC):
                v_op(c, lambda eng, o=xT[:, kc, :], a=xT[:, kc, :], b=xn[:, kc, :]:
                     eng.tensor_add(o, a, b))
            xplus = c.counts["dv"]
        c.wait("tensor", "dv", ao_ready)
        woname = "wo0" if layer == 0 else "WoT"
        bo1 = c.bufs["bo1"]
        wqb_flat = wqb.rearrange("p a b -> p (a b)")
        wo_b = wqb_flat[:, 0:1024].rearrange("p (a b) -> p a b", a=KC)
        if layer == 1:
            bob = wqb_flat[0:8, 2048:3072].rearrange("p (a b) -> p a b", a=KC)
            bof_v = wqf[0:8, :, :]
            c.acquire("sync", "atwqf")
            ldbo = dma(c, "sync", bof_v, c.dram["BoTr"].rearrange(
                "r (kc m) -> r kc m", kc=KC), "ldw_bo")
            c.wait("gpsimd", "ldw_bo", ldbo)
            c.acquire("gpsimd", "atwqb")
            ga2 = g_op(c, lambda eng, o=bob, i=bof_v: eng.tensor_copy(o, i))
            c.release("atwqf", "gp", ga2)
            c.wait("tensor", "gp", ga2)
        wofam = "wo%d" % layer
        for mc in range(KC):
            if mc in c.wc_written[wofam]:
                c.wait_dep("sync", c.wc_written[wofam][mc])
                c.acquire("sync", "atwqb")
                ldq = dma(c, "sync", wo_b,
                          c.wcache[wofam][mc].rearrange("p (a b) -> p a b", a=KC),
                          "ldqkvc")
                gw_dep = ("ldqkvc", ldq)
            else:
                c.acquire("sync", "atwqf")
                ldw = dma(c, "sync", wqf[:, 0:KC, 0:128],
                          c.dram[woname][:, mc * 128:(mc + 1) * 128].rearrange(
                              "(kc kp) m -> kp kc m", kp=128), "ldw_qkv")
                c.wait("gpsimd", "ldw_qkv", ldw)
                c.acquire("gpsimd", "atwqb")
                gw = g_op(c, lambda eng, o=wo_b, i=wqf[:, 0:KC, 0:128]:
                          eng.tensor_copy(o, i))
                c.release("atwqf", "gp", gw)
                wb_cache(c, wofam, mc, wo_b, ("gp", gw), frees=("atwqb",))
                gw_dep = ("gp", gw)
            c.wait_dep("tensor", gw_dep)
            if layer == 1:
                c.wait("tensor", "ac", xaO_ready)
            for it in range(QT4):
                ts = slice(it * TOK, (it + 1) * TOK)
                bank, bname = (psA, "atb0") if (mc * QT4 + it) % 2 == 0 else (psB, "atb1")
                c.acquire("tensor", bname)
                nk2 = KC if layer == 1 else KC - 1
                for kc in range(nk2):
                    t_mm(c, bank[:], wo_b[:, kc, :],
                         attnout[:, kc, ts], start=(kc == 0), stop=False)
                if layer == 1:
                    last = t_mm(c, bank[:], bob[0:8, mc, :],
                                xaO[0:8, ts], start=False, stop=True, inc=True)
                    c.wait("vector", "pe", last)
                    c.acquire("vector", "atSs_wo")
                    evs = v_op(c, lambda eng, o=Ss[:, ts], i=bank[:],
                               b=bo1[:, mc:mc + 1]:
                               eng.tensor_scalar(o, i, b, None, op0=ALU.add))
                    c.release(bname, "dv", evs)
                    c.wait("vector", "dv", evs)
                    c.wait("vector", "dv", xplus)
                    rz = c.bufs["rz"]
                    rv = v_op(c, lambda eng, o=xT[:, mc, ts], a=Ss[:, ts],
                              x2=xT[:, mc, ts], s=rz[:]:
                              eng.scalar_tensor_tensor(o, a, s, x2, op0=ALU.mult,
                                                       op1=ALU.add))
                    c.release("atSs_wo", "dv", rv)
                else:
                    last = t_mm(c, bank[:], wo_b[:, KC - 1, :],
                                attnout[:, KC - 1, ts], start=False, stop=True,
                                inc=True)
                    c.wait("vector", "pe", last)
                    rv = v_op(c, lambda eng, o=xT[:, mc, ts], a=xT[:, mc, ts],
                              b=bank[:]: eng.tensor_add(o, a, b))
                    c.release(bname, "dv", rv)
            c.release("atwqb", "pe", c.counts["pe"])
        c.produce("xT", "dv", c.counts["dv"])
        c.barrier()


def emit_ffn1(c, xT, half):
    tok0 = half * HALF
    ident = c.bufs["ident"]
    with ExitStack() as es:
        xn = asb(c, "f1xn", [128, KC, HALF], BF16, es)[:]
        acc = asb(c, "f1acc", [128, KC, HALF], F32, es)[:]
        cT = asb(c, "f1cT", [E1, HALF], BF16, es)[:]
        beta_row = asb(c, "f1brow", [1, HALF], F32, es)[:]
        web = asb(c, "f1web", [128, 2, KC, D], BF16, es)[:]
        stage = asb(c, "f1st", [128, 2, 1024], F32, es)[:]
        xnp = asb(c, "f1xnp", [128, 2, KC, TOK], BF16, es)[:]
        cbs = asb(c, "f1cb", [128, 2, TOK], BF16, es)[:]
        sel_sb = asb(c, "f1sel", [E1, 2, 128], BF16, es)[:]
        rt_f = asb(c, "f1rtf", [128, KC, E1], F32, es)[:]
        rt_b = asb(c, "f1rtb", [128, KC, E1], BF16, es)[:]
        p_ed = aps(c, "f1ed", [128, 3, TOK], F32, es)[:]
        p_cb = aps(c, "f1pcb", [128, TOK], F32, es)[:]
        p_a = aps(c, "f1pa", [128, TOK], F32, es)[:]
        p_b = aps(c, "f1pb", [128, TOK], F32, es)[:]

        with ExitStack() as bs:
            statB = asb(c, "f1statB", [128, 2, TOK], F32, bs)[:]
            lnb1 = asb(c, "f1lnb", [128, 4, TOK], BF16, bs)[:]
            statB2 = asb(c, "f1statB2", [128, 2, TOK], F32, bs)[:]
            sq = asb(c, "f1sq", [128, 2, TOK], F32, bs)[:]
            lgT = asb(c, "f1lgT", [E1, HALF], F32, bs)[:]
            ltok = asb(c, "f1ltok", [128, NTC, E1], F32, bs)[:]
            r8 = asb(c, "f1r8", [128, 4, E1], F32, bs)[:]
            r1 = asb(c, "f1r1", [128, 6], F32, bs)[:]
            ctokb = asb(c, "f1ctok", [128, E1], F32, bs)[:]
            bcol = asb(c, "f1bcol", [128, NTC], F32, bs)[:]
            brt = asb(c, "f1brt", [16, 128], F32, bs)[:]

            ldr = dma(c, "sync", rt_f[:], c.dram["routerT"].rearrange(
                "(kc kp) e -> kp kc e", kp=128), "ldw_gate")
            c.wait("scalar", "ldw_gate", ldr)
            a_r = s_op(c, lambda eng, o=rt_b[:], i=rt_f[:]:
                       eng.activation(out=o, in_=i, func=AF.Copy))

            ln_sc = {"p0": p_a, "p1": p_b, "statB": statB, "statB2": statB2,
                     "sq": sq, "lnb": lnb1}
            emit_ln(c, xT, xn, tok0, HALF, ln_sc)

            c.require("tensor", "xn")
            c.wait("tensor", "ac", a_r)
            lg_done = []
            for it in range(NT):
                if lg_done:
                    c.wait("tensor", "ac", lg_done[-1])
                last = None
                for kc in range(KC):
                    last = t_mm(c, p_cb[0:E1, :], rt_b[:, kc],
                                xn[:, kc, it * TOK:(it + 1) * TOK],
                                start=(kc == 0), stop=(kc == KC - 1),
                                inc=(kc == KC - 1))
                c.wait("scalar", "pe", last)
                a1 = s_op(c, lambda eng, o=lgT[:, it * TOK:(it + 1) * TOK],
                          i=p_cb[0:E1, :]: eng.activation(out=o, in_=i, func=AF.Copy))
                lg_done.append(a1)
            c.wait("tensor", "ac", lg_done[-1])
            tr_done = {}
            for tc in range(NTC):
                if tc >= 2:
                    c.wait("tensor", "ac", tr_done[tc - 2])
                pe1 = t_transpose(c, p_ed[:, tc % 2, 0:E1],
                                  lgT[:, tc * 128:(tc + 1) * 128],
                                  ident[0:E1, 0:E1])
                c.wait("scalar", "pe", pe1)
                tr_done[tc] = s_op(c, lambda eng, o=ltok[:, tc], i=p_ed[:, tc % 2, 0:E1]:
                                   eng.activation(out=o, in_=i, func=AF.Copy))
            last_cT = None
            last_pe2 = None
            for tc in range(NTC):
                c.wait("vector", "ac", tr_done[tc])
                lt = ltok[:, tc]
                v1 = v_op(c, lambda eng, o=r1[:, 0:1], i=lt:
                          eng.reduce_max(o, i, axis=AX.X, negate=True))
                c.wait("scalar", "dv", v1)
                a1 = s_op(c, lambda eng, o=r8[:, 0], i=lt, b=r1[:, 0:1],
                          acc=r1[:, 1:2]:
                          eng.activation(out=o, in_=i, func=AF.Exp, bias=b,
                                         accum_out=acc))
                c.wait("vector", "ac", a1)
                v2 = v_op(c, lambda eng, o=r1[:, 2:3], i=r1[:, 1:2]:
                          eng.reciprocal(o, i))
                c.wait("vector", "dv", v2)
                v3 = v_op(c, lambda eng, o=r8[:, 1], i=r8[:, 0], s=r1[:, 2:3]:
                          eng.tensor_scalar(o, i, s, None, op0=ALU.mult))
                c.wait("vector", "dv", v3)
                v4 = v_op(c, lambda eng, o=r8[:, 2], i=r8[:, 1]: eng.max(out=o, in_=i))
                c.wait("vector", "dv", v4)
                v5 = v_op(c, lambda eng, o=r1[:, 3:4], a=r8[:, 2, 0:1], b=r8[:, 2, 1:2]:
                          eng.tensor_add(o, a, b))
                c.wait("vector", "dv", v5)
                v6 = v_op(c, lambda eng, o=r1[:, 3:4], i=r1[:, 3:4]:
                          eng.tensor_single_scalar(o, i, 1e-6, op=ALU.add))
                c.wait("vector", "dv", v6)
                v7 = v_op(c, lambda eng, o=r1[:, 4:5], i=r1[:, 3:4]:
                          eng.reciprocal(o, i))
                c.wait("vector", "dv", v7)
                v8 = v_op(c, lambda eng, o=r8[:, 3], i=r8[:, 1], s=r8[:, 2, 1:2]:
                          eng.tensor_scalar(o, i, s, None, op0=ALU.is_ge))
                c.wait("vector", "dv", v8)
                v9 = v_op(c, lambda eng, o=r8[:, 3], a=r8[:, 3], b=r8[:, 1]:
                          eng.tensor_mul(o, a, b))
                c.wait("vector", "dv", v9)
                if last_pe2 is not None:
                    c.wait("vector", "pe", last_pe2)
                v10 = v_op(c, lambda eng, o=ctokb[:], i=r8[:, 3], s=r1[:, 4:5]:
                           eng.tensor_scalar(o, i, s, None, op0=ALU.mult))
                c.wait("vector", "dv", v10)
                v11 = v_op(c, lambda eng, o=r8[:, 3], a=ctokb[:], b=r8[:, 1]:
                           eng.tensor_mul(o, a, b))
                c.wait("vector", "dv", v11)
                v12 = v_op(c, lambda eng, o=bcol[:, tc:tc + 1], i=r8[:, 3]:
                           eng.reduce_sum(o, i, axis=AX.X))
                c.wait("tensor", "dv", v12)
                if last_cT is not None:
                    c.wait("tensor", "ac", last_cT)
                pe2 = t_transpose(c, p_ed[0:E1, 2, 0:128], ctokb, ident[:])
                last_pe2 = pe2
                c.wait("scalar", "pe", pe2)
                last_cT = s_op(c, lambda eng, o=cT[:, tc * 128:(tc + 1) * 128],
                               i=p_ed[0:E1, 2, 0:128]:
                               eng.activation(out=o, in_=i, func=AF.Copy))
            c.wait("tensor", "dv", c.counts["dv"])
            c.wait("tensor", "ac", last_cT)
            pe3 = t_transpose(c, p_ed[0:NTC, 2, 0:128], bcol, ident[:])
            c.wait("scalar", "pe", pe3)
            ab = s_op(c, lambda eng, o=brt[0:NTC, :], i=p_ed[0:NTC, 2, 0:128]:
                      eng.activation(out=o, in_=i, func=AF.Copy))
            c.wait("sync", "ac", ab)
            b1 = dma(c, "sync", c.dram["beta_bounce"][:],
                     brt[0:NTC, :], "ldw_mls")
            c.wait("sync", "ldw_mls", b1)
            b2 = dma(c, "sync", beta_row[:],
                     c.dram["beta_bounce"].rearrange("a b -> (a b)").unsqueeze(0),
                     "ldw_mls")
            c.wait("tensor", "ldw_mls", b2)
            c.barrier()

        stream_ctr = [0]

        def stream_chunk(dst_ap, src_ap, key=None):
            sslot = stream_ctr[0] % 2
            stream_ctr[0] += 1
            c.acquire("sync", "f1stage%d" % sslot)
            ld = dma(c, "sync", stage[:, sslot, :], src_ap, "ldw_st%d" % sslot)
            c.wait("gpsimd", "ldw_st%d" % sslot, ld)
            if key:
                c.acquire("gpsimd", key)
            av = g_op(c, lambda eng, o=dst_ap, i=stage[:, sslot, :]:
                      eng.tensor_copy(o, i))
            c.release("f1stage%d" % sslot, "gp", av)
            return av

        def stream_expert(e):
            slot = e % 2
            key = "f1w%d" % slot
            if e in c.wc_written["expw"]:
                c.wait_dep("sync", c.wc_written["expw"][e])
                c.acquire("sync", key)
                ldv = dma(c, "sync", web[:, slot].rearrange("p a b -> p (a b)"),
                          c.wcache["expw"][e], "ldex%d" % slot)
                last = ("ldex%d" % slot, ldv)
            else:
                lv = None
                for ch in range(KC):
                    lv = stream_chunk(web[:, slot, ch],
                                      c.dram["expW"][e, ch * 128:(ch + 1) * 128],
                                      key)
                wb_cache(c, "expw", e,
                         web[:, slot].rearrange("p a b -> p (a b)"), ("gp", lv),
                         frees=("f1w%d" % slot,))
                last = ("gp", lv)
            c.acquire("sync", "f1sel%d" % slot)
            lds = dma(c, "sync", sel_sb[:, slot, :],
                      c.dram["sel8"][:, e * 128:(e + 1) * 128],
                      "ldw_sel%d" % slot)
            return last, lds

        w_done = {}
        sel_done = {}
        w_done[0], sel_done[0] = stream_expert(0)
        w_done[1], sel_done[1] = stream_expert(1)
        acc_last = {}
        for e in range(E1):
            slot = e % 2
            c.wait_dep("tensor", w_done[e])
            c.wait("tensor", "ldw_sel%d" % slot, sel_done[e])
            for it in range(NT):
                ts = slice(it * TOK, (it + 1) * TOK)
                ji = e * NT + it
                jslot = ji % 2
                c.acquire("tensor", "f1pcb")
                cbm = t_mm(c, p_cb[:], sel_sb[:, slot, :], cT[:, ts],
                           start=True, stop=True, inc=True)
                c.wait("scalar", "pe", cbm)
                c.acquire("scalar", "f1cbs%d" % jslot)
                cbe = s_op(c, lambda eng, o=cbs[:, jslot], i=p_cb[:]:
                           eng.activation(out=o, in_=i, func=AF.Copy))
                c.release("f1pcb", "ac", cbe)
                c.wait("vector", "ac", cbe)
                c.require("vector", "xn")
                c.acquire("vector", "f1xnp%d" % jslot)
                g_h = None
                for kc in range(KC):
                    g_h = v_op(c, lambda eng, o=xnp[:, jslot, kc], a=xn[:, kc, ts],
                               b=cbs[:, jslot]: eng.tensor_tensor(o, a, b, op=ALU.mult))
                c.release("f1cbs%d" % jslot, "dv", g_h)
                c.wait("tensor", "dv", g_h)
                for mc in range(KC):
                    bank = mc % 3
                    c.acquire("tensor", "f1ped%d" % bank)
                    first = True
                    if e == 0:
                        t_mm(c, p_ed[:, bank, :], c.bufs["onerow_f"][0:1, :],
                             beta_row[0:1, ts], start=True, stop=False)
                        first = False
                    last = None
                    for kc in range(KC):
                        last = t_mm(c, p_ed[:, bank, :],
                                    web[:, slot, kc, mc * 128:(mc + 1) * 128],
                                    xnp[:, jslot, kc], start=first and kc == 0,
                                    stop=(kc == KC - 1), inc=(kc == KC - 1))
                        first = False
                    if e == 0:
                        c.wait("scalar", "pe", last)
                        a_i = s_op(c, lambda eng, o=acc[:, mc, ts], i=p_ed[:, bank, :]:
                                   eng.activation(out=o, in_=i, func=AF.Copy))
                        c.release("f1ped%d" % bank, "ac", a_i)
                        acc_last[(mc, it)] = ("ac", a_i)
                    else:
                        c.wait("vector", "pe", last)
                        c.wait_dep("vector", acc_last[(mc, it)])
                        vadd = v_op(c, lambda eng, o=acc[:, mc, ts],
                                    a=acc[:, mc, ts], b=p_ed[:, bank, :]:
                                    eng.tensor_add(o, a, b))
                        c.release("f1ped%d" % bank, "dv", vadd)
                        acc_last[(mc, it)] = ("dv", vadd)
                c.release("f1xnp%d" % jslot, "pe", c.counts["pe"])
            c.release("f1w%d" % slot, "pe", c.counts["pe"])
            c.release("f1sel%d" % slot, "pe", c.counts["pe"])
            if e + 2 < E1:
                w_done[e + 2], sel_done[e + 2] = stream_expert(e + 2)

        c.wait("vector", "dv", c.counts["dv"])
        c.wait("vector", "ac", c.counts["ac"])
        for kc in range(KC):
            v_op(c, lambda eng, o=xT[:, kc, tok0:tok0 + HALF],
                 a=acc[:, kc, :], x2=xT[:, kc, tok0:tok0 + HALF]:
                 eng.scalar_tensor_tensor(o, a, 0.5, x2, op0=ALU.mult, op1=ALU.add))
        c.produce("xT", "dv", c.counts["dv"])
        c.barrier()



DSHAPES_F32 = {
    "sgT": (D, ISH), "suT": (D, ISH), "sdT": (ISH, D),
    "gateT": (D, E0), "gbias": (E0, 1),
    "egT": (E0, D, INTER0), "euT": (E0, D, INTER0), "edT": (E0, INTER0, D),
    "gamma": (128, KC), "wqkv0": (NHP, KC, 128, 3, 128), "wo0": (D, D),
    "WoT": (D, D), "AoT": (D, 8), "BoTr": (8, D),
    "AqT": (D, 8), "AkT": (D, 8), "AvT": (D, 8),
    "BqTr": (8, D), "BkTr": (8, D), "BvTr": (8, D),
    "wqkv1": (NHP, KC, 128, 3, 128), "aqkv1": (KC, 128, 96),
    "bqkv1": (NHP, 8, 3, 128), "biasqkv1": (NHP, 128, 3),
    "bo1": (128, KC), "rz": (128, 1),
    "routerT": (D, E1), "expW": (E1, D, D),
    "ident": (128, 128), "ones_f": (128, 128), "onerow_f": (1, 128),
    "ones33": (33, 128),
    "alibiC": (128, HEADS, S // 128), "slp2": (65, 2, NHP),
}
DSHAPES_BF16 = {"identb": (128, 128), "sel64": (E0, E0 * 128),
                "blk2": (128, 65),
                "sel8": (E1, E1 * 128),
                "ones_b": (128, 128), "onerow_b": (1, 128)}


def build_program(parts):
    nc = bass.Bass("TRN2", target_bir_lowering=False)
    es = ExitStack()
    c = Ctx(nc, es)
    for k, shp in DSHAPES_F32.items():
        c.dram[k] = nc.dram_tensor(k, list(shp), F32, kind="ExternalInput").ap()
    for k, shp in DSHAPES_BF16.items():
        c.dram[k] = nc.dram_tensor(k, list(shp), BF16, kind="ExternalInput").ap()
    c.dram["xT"] = nc.dram_tensor("xT", [SPC, KC, 128, S], F32,
                                  kind="ExternalInput").ap()
    c.dram["outT"] = nc.dram_tensor("outT", [SPC, KC, 128, S], F32,
                                    kind="ExternalOutput").ap()

    xT = asb(c, "xT_sb", [128, KC, S], F32)[:]
    c.bufs["dummy"] = asb(c, "dummy", [128, 4], F32)[:]
    for nm, shp, dt in [("ident", (128, 128), F32), ("identb", (128, 128), BF16),
                        ("ones_f", (128, 128), F32), ("ones_b", (128, 128), BF16),
                        ("onerow_b", (1, 128), BF16), ("gamma", (128, KC), F32),
                        ("gbias", (E0, 1), F32), ("rz", (128, 1), F32),
                        ("bo1", (128, KC), F32),
                        ("alibiC", (128, HEADS, S // 128), F32),
                        ("slp2", (65, 2, NHP), F32), ("blk2", (128, 65), BF16)]:
        c.bufs[nm] = asb(c, "c_" + nm, list(shp), dt)[:]
        dma(c, "sync", c.bufs[nm], c.dram[nm], "ld")
    c.bufs["onerow_f"] = asb(c, "c_onerowf", [1, 128], F32)[:]
    dma(c, "sync", c.bufs["onerow_f"], c.dram["onerow_f"], "ld")
    c.bufs["ones33"] = asb(c, "c_ones33", [33, 128], F32)[:]
    dma(c, "sync", c.bufs["ones33"], c.dram["ones33"], "ld")
    c.dram["mls_bounce"] = nc.dram_tensor("mls_bounce", [16, 128], F32,
                                          kind="Internal").ap()
    c.wcache = {}
    c.wc_written = {}
    for fam, n, cols in [("eg", E0, 4096), ("eu", E0, 4096), ("ed", E0, 4096),
                         ("sh", 3, 8192), ("expw", E1, 8192),
                         ("qkv0", NHP, 3072), ("qkv1", NHP, 3072),
                         ("wo0", KC, 1024), ("wo1", KC, 1024)]:
        c.wcache[fam] = nc.dram_tensor("wc_" + fam, [n, 128, cols], BF16,
                                       kind="Internal").ap()
        c.wc_written[fam] = {}
    c.dram["beta_bounce"] = nc.dram_tensor("beta_bounce", [NTC, 128], F32,
                                           kind="Internal").ap()
    cb = asb(c, "c_eps", [128, 8], F32)[:]
    c.bufs["gp1"] = asb(c, "c_gp1", [128, KC], F32)[:]
    vals = {"eps_ln": (0, 1e-5), "eps_rms": (1, 1e-12), "one": (4, 1.0),
            "neg_big": (5, NEG_BIG), "zero": (6, 0.0), "c32": (7, 32.0)}
    c.const_cols = {k: cb[:, i:i + 1] for k, (i, v) in vals.items()}
    for k, (i, v) in vals.items():
        c.emit("vector", lambda eng, ap=cb[:, i:i + 1], vv=v: eng.memset(ap, vv))
    c.emit("vector", lambda eng, s=c.sem("dv"): eng.engine_nop().then_inc(s, 1))
    c.inc("dv")
    ld_consts = c.counts["ld"]
    dv0 = c.counts["dv"]
    for e in ENGS[:4]:
        c.wait(e, "ld", ld_consts)
        c.wait(e, "dv", dv0)
    a_gp = v_op(c, lambda eng, o=c.bufs["gp1"], i=c.bufs["gamma"]:
                eng.tensor_scalar(o, i, 32.0, 32.0, op0=ALU.mult, op1=ALU.add))
    for e in ENGS[:4]:
        c.wait(e, "dv", a_gp)

    for seq in range(SPC):
        lx = None
        c.acquire("sync", "xTbuf")
        for kc in range(KC):
            lx = dma(c, "sync", xT[:, kc, :], c.dram["xT"][seq, kc], "ldx")
        for e in ("tensor", "vector", "scalar"):
            c.wait(e, "ldx", lx)
        c.produce("xT", "ldx", lx)
        for part in parts:
            if part == "ffn0":
                for half in range(S // HALF):
                    emit_ffn0(c, xT, half)
            elif part == "attn0":
                emit_attn(c, xT, layer=0)
            elif part == "ffn1":
                for half in range(S // HALF):
                    emit_ffn1(c, xT, half)
            elif part == "attn1":
                emit_attn(c, xT, layer=1)
            else:
                raise ValueError(part)
        c.wait_dep("sync", c.ready.get("xT"))
        sx = None
        for kc in range(KC):
            sx = dma(c, "sync", c.dram["outT"][seq, kc], xT[:, kc, :], "sto")
        c.release("xTbuf", "sto", sx)
    c.wait("sync", "sto", c.counts["sto"])

    with nc.Block() as block:
        @block.tensor
        def _(eng):
            for fn in c.prog["tensor"]:
                fn(eng)

        @block.vector
        def _(eng):
            for fn in c.prog["vector"]:
                fn(eng)

        @block.scalar
        def _(eng):
            for fn in c.prog["scalar"]:
                fn(eng)

        @block.gpsimd
        def _(eng):
            for fn in c.prog["gpsimd"]:
                fn(eng)

        @block.sync
        def _(eng):
            for fn in c.prog["sync"]:
                fn(eng)
    es.close()
    return nc



_PARTS_ALL = ["ffn0", "attn0", "ffn0", "ffn1", "attn1", "ffn1"]


def run_parts(inputs, parts):
    h, percore = prep_host(inputs)
    nc = build_program(parts)
    names = set(DSHAPES_F32) | set(DSHAPES_BF16) | {"xT"}
    in_maps = []
    for cid in range(NCORE):
        m = {k: v for k, v in h.items() if k in names}
        m["xT"] = percore[cid]["xT"]
        in_maps.append(m)
    res = run_bass_kernel_spmd(nc, in_maps, core_ids=list(range(NCORE)))
    out = np.empty((BATCH, S, D), dtype=np.float32)
    for cid in range(NCORE):
        ot = res.results[cid]["outT"].reshape(SPC, D, S)
        out[cid * SPC:(cid + 1) * SPC] = ot.transpose(0, 2, 1)
    return out


def kernel(**inputs):
    return run_parts(inputs, _PARTS_ALL)


# version 08 (final)
# speedup vs baseline: 1.3425x; 1.0478x over previous
import numpy as np
import ml_dtypes
from contextlib import ExitStack

import concourse.bass as bass
import concourse.mybir as mybir
from concourse.bass_utils import run_bass_kernel_spmd

F32 = mybir.dt.float32
BF16 = mybir.dt.bfloat16
AF = mybir.ActivationFunctionType
ALU = mybir.AluOpType
AX = mybir.AxisListType
BF = ml_dtypes.bfloat16

D = 1024
S = 2048
BATCH = 16
NCORE = 8
SPC = BATCH // NCORE
KC = D // 128
NEG_BIG = -1e30

E0 = 64; INTER0 = 512; IC0 = INTER0 // 128
ISH = 1024
E1 = 8
HEADS = 16; DH = 64; NHP = HEADS // 2
TOK = 512
HALF = 1024
NT = HALF // TOK
NTC = HALF // 128

ENGS = ("tensor", "vector", "scalar", "gpsimd", "sync")



def _c(a):
    return np.ascontiguousarray(a)


def prep_host(inp):
    h = {}
    h["sgT"] = _c(inp["L0_ffn_sg_w"].astype(np.float32).T)
    h["suT"] = _c(inp["L0_ffn_su_w"].astype(np.float32).T)
    h["sdT"] = _c(inp["L0_ffn_sd_w"].astype(np.float32).T)
    h["gateT"] = _c(inp["L0_ffn_gate_w"].astype(np.float32).T)
    h["gbias"] = _c(inp["L0_ffn_gate_bias"].astype(np.float32).reshape(E0, 1))
    h["egT"] = _c(inp["L0_ffn_eg_w"].astype(np.float32).transpose(0, 2, 1))
    h["euT"] = _c(inp["L0_ffn_eu_w"].astype(np.float32).transpose(0, 2, 1))
    h["edT"] = _c(inp["L0_ffn_ed_w"].astype(np.float32).transpose(0, 2, 1))
    h["gamma"] = _c(inp["L0_attn_gamma"].astype(np.float32).reshape(KC, 128).T)
    wq = inp["L0_attn_w_qkv"].astype(np.float32).reshape(8, 128, 3, NHP, 128)
    h["wqkv0"] = _c(wq.transpose(3, 0, 1, 2, 4))
    h["wo0"] = _c(inp["L0_attn_w_out"].astype(np.float32))
    WT = {}
    for n in ["q", "k", "v", "o"]:
        WT[n] = _c(inp["L1_attn_W" + n].astype(np.float32).T)
        h["A%sT" % n] = _c(inp["L1_attn_A" + n].astype(np.float32).T)
        h["B%sTr" % n] = _c(inp["L1_attn_B" + n].astype(np.float32).T)
    h["WoT"] = WT["o"]
    wq1 = np.stack([WT["q"], WT["k"], WT["v"]], 0).reshape(3, 8, 128, NHP, 128)
    h["wqkv1"] = _c(wq1.transpose(3, 1, 2, 0, 4))
    aq1 = np.stack([h["AqT"], h["AkT"], h["AvT"]], 0)
    apad = np.zeros((D, 96), dtype=np.float32)
    for p in range(3):
        apad[:, p * 32:p * 32 + 8] = aq1[p]
    h["aqkv1"] = _c(apad.reshape(8, 128, 96))
    bq1 = np.stack([h["BqTr"], h["BkTr"], h["BvTr"]], 0).reshape(3, 8, NHP, 128)
    h["bqkv1"] = _c(bq1.transpose(2, 1, 0, 3))
    bb = np.stack([inp["L1_attn_bq"], inp["L1_attn_bk"], inp["L1_attn_bv"]], 0)
    h["biasqkv1"] = _c(bb.astype(np.float32).reshape(3, 8, 128).transpose(1, 2, 0))
    h["bo1"] = _c(inp["L1_attn_bo"].astype(np.float32).reshape(KC, 128).T)
    h["rz"] = _c(np.broadcast_to(
        inp["L1_attn_rezero"].astype(np.float32).reshape(1, 1), (128, 1)))
    h["routerT"] = _c(inp["L1_ffn_router_w"].astype(np.float32).T)
    h["expW"] = _c(inp["L1_ffn_expert_weights"].astype(np.float32))
    h["ident"] = np.eye(128, dtype=np.float32)
    h["identb"] = np.eye(128, dtype=BF)
    sel64 = np.zeros((E0, E0 * 128), dtype=BF)
    for e in range(E0):
        sel64[e, e * 128:(e + 1) * 128] = 1
    h["sel64"] = sel64
    sel8 = np.zeros((E1, E1 * 128), dtype=BF)
    for e in range(E1):
        sel8[e, e * 128:(e + 1) * 128] = 1
    h["sel8"] = sel8
    h["onerow_f"] = np.ones((1, 128), dtype=np.float32)
    o33 = np.zeros((33, 128), dtype=np.float32)
    o33[0] = 1.0; o33[32] = 1.0
    h["ones33"] = o33
    h["ones_f"] = np.ones((128, 128), dtype=np.float32)
    h["ones_b"] = np.ones((128, 128), dtype=BF)
    h["onerow_b"] = np.ones((1, 128), dtype=BF)
    slopes = 0.5 ** np.arange(HEADS, dtype=np.float32)
    blk2 = np.zeros((128, 65), dtype=BF)
    blk2[0:64, 0] = 1.0
    blk2[64:128, 64] = 1.0
    h["blk2"] = blk2
    slp2 = np.ones((65, 2, NHP), dtype=np.float32)
    for hp in range(NHP):
        for hh in range(2):
            slp2[64 * hh, 1, hp] = slopes[hp * 2 + hh] * float(S - 1) + 1.0
    h["slp2"] = _c(slp2)
    pcol = np.arange(128, dtype=np.float32)
    ali_c = np.zeros((128, HEADS, S // 128), dtype=np.float32)
    for hh in range(HEADS):
        for jc in range(S // 128):
            ali_c[:, hh, jc] = slopes[hh] * (jc * 128 + pcol)
    h["alibiC"] = ali_c
    x = inp["L0_attn_x"].astype(np.float32)
    percore = []
    for cid in range(NCORE):
        xs = x[cid * SPC:(cid + 1) * SPC]
        xt = _c(xs.transpose(0, 2, 1)).reshape(SPC, KC, 128, S)
        percore.append({"xT": xt})
    return h, percore



class Ctx:
    def __init__(self, nc, es):
        self.nc = nc
        self.es = es
        self.prog = {e: [] for e in ENGS}
        self.sems = {}
        self.counts = {}
        self.waited = {e: {} for e in ENGS}
        self.ready = {}
        self.freed = {}
        self.bufs = {}
        self.dram = {}
        self.const_cols = {}

    def sem(self, name):
        if name not in self.sems:
            self.sems[name] = self.es.enter_context(self.nc.semaphore(name))
            self.counts[name] = 0
        return self.sems[name]

    def inc(self, name, n=1):
        self.sem(name)
        self.counts[name] += n
        return self.counts[name]

    def emit(self, engine, fn):
        self.prog[engine].append(fn)

    def wait(self, engine, sem_name, val):
        if val is None or val <= 0:
            return
        sm = self.sem(sem_name)
        w = self.waited[engine]
        if w.get(sem_name, 0) >= val:
            return
        w[sem_name] = val
        self.prog[engine].append(lambda eng, s=sm, v=val: eng.wait_ge(s, v))

    def wait_dep(self, engine, dep):
        if dep is not None:
            self.wait(engine, dep[0], dep[1])

    def require(self, engine, res):
        self.wait_dep(engine, self.ready.get(res))

    def acquire(self, engine, res):
        deps = self.freed.get(res)
        if deps:
            for d in deps:
                self.wait_dep(engine, d)

    def produce(self, res, sem_name, val):
        self.ready[res] = (sem_name, val)

    def release(self, res, sem_name, val):
        lst = self.freed.setdefault(res, [])
        for i, (s, v) in enumerate(lst):
            if s == sem_name:
                lst[i] = (sem_name, max(v, val))
                break
        else:
            lst.append((sem_name, val))

    release_add = release

    def barrier(self, engines=("tensor", "vector", "scalar", "gpsimd")):
        bar = self.sem("bar")
        for snm, v in list(self.counts.items()):
            if snm.startswith("wb_") and v:
                self.wait("sync", snm, v)
        tgt = self.counts["bar"] + len(engines) + 1
        self.counts["bar"] = tgt
        dummy = self.bufs["dummy"]
        self.prog["sync"].append(lambda eng, s=bar: eng.nop().then_inc(s, 1))
        for e in engines:
            if e == "tensor":
                self.prog[e].append(lambda eng, s=bar: eng.nop().then_inc(s, 1))
            elif e == "vector":
                self.prog[e].append(lambda eng, s=bar: eng.engine_nop().then_inc(s, 1))
            elif e == "scalar":
                self.prog[e].append(lambda eng, s=bar, d=dummy: eng.memzero(d[0:1, 0:1]).then_inc(s, 1))
            elif e == "gpsimd":
                self.prog[e].append(lambda eng, s=bar, d=dummy: eng.memset(d[0:1, 1:2], 0.0).then_inc(s, 1))
        for e in engines:
            self.wait(e, "bar", tgt)
        self.wait("sync", "bar", tgt)
        for e in ENGS:
            w = self.waited[e]
            for s in ("pe", "dv", "ac", "gp"):
                if s in self.counts:
                    w[s] = max(w.get(s, 0), self.counts[s])
        return ("bar", tgt)


_NAME_CTR = [0]


def _uname(name):
    _NAME_CTR[0] += 1
    return "k%d_%s" % (_NAME_CTR[0], name)


def asb(c, name, shape, dtype, es=None):
    return (es or c.es).enter_context(
        c.nc.sbuf_tensor(_uname(name), list(shape), dtype))


def aps(c, name, shape, dtype=F32, es=None):
    return (es or c.es).enter_context(
        c.nc.psum_tensor(_uname(name), list(shape), dtype))


def t_mm(c, out_ap, lhsT_ap, rhs_ap, start, stop, inc=False):
    if inc:
        c.emit("tensor", lambda eng, o=out_ap, l=lhsT_ap, r=rhs_ap, a=start, b=stop,
               s=c.sem("pe"): eng.matmul(o, l, r, start=a, stop=b).then_inc(s, 1))
        return c.inc("pe")
    c.emit("tensor", lambda eng, o=out_ap, l=lhsT_ap, r=rhs_ap, a=start, b=stop:
           eng.matmul(o, l, r, start=a, stop=b))
    return None


def t_transpose(c, out_ap, in_ap, ident_ap):
    c.emit("tensor", lambda eng, o=out_ap, i=in_ap, d=ident_ap,
           s=c.sem("pe"): eng.transpose(o, i, d).then_inc(s, 1))
    return c.inc("pe")


def v_op(c, fn, inc=True):
    if inc:
        c.emit("vector", lambda eng, f=fn, s=c.sem("dv"): f(eng).then_inc(s, 1))
        return c.inc("dv")
    c.emit("vector", fn)
    return None


def s_op(c, fn, inc=True):
    if inc:
        c.emit("scalar", lambda eng, f=fn, s=c.sem("ac"): f(eng).then_inc(s, 1))
        return c.inc("ac")
    c.emit("scalar", fn)
    return None


def g_op(c, fn, inc=True):
    if inc:
        c.emit("gpsimd", lambda eng, f=fn, s=c.sem("gp"): f(eng).then_inc(s, 1))
        return c.inc("gp")
    c.emit("gpsimd", fn)
    return None


def wb_cache(c, fam, item, src_ap, cast_dep, frees=()):
    c.wait_dep("sync", cast_dep)
    sem = "wb_" + (frees[0] if frees else fam)
    wb = dma(c, "sync", c.wcache[fam][item], src_ap, sem)
    c.wc_written[fam][item] = (sem, wb)
    for r in frees:
        c.release_add(r, sem, wb)
    return wb


def dma(c, queue, out_ap, in_ap, sem_name):
    c.emit(queue, lambda eng, o=out_ap, i=in_ap, s=c.sem(sem_name):
           eng.dma_start(out=o, in_=i).then_inc(s, 16))
    return c.inc(sem_name, 16)



def emit_ln(c, xT, xn, tok0, ntok, sc):
    ones_f = c.bufs["ones_f"]
    p_sum, p_sq = sc["p0"], sc["p1"]
    statB, statB2, sq = sc["statB"], sc["statB2"], sc["sq"]
    ntile = ntok // TOK
    sq_mm = {}
    stats_read = None
    prev_apply = None
    for it in range(ntile):
        t0 = tok0 + it * TOK
        c.require("tensor", "xT")
        c.require("vector", "xT")
        if stats_read is not None:
            c.wait("tensor", "dv", stats_read)
        if prev_apply is not None:
            c.wait("vector", "dv", prev_apply)
        for kc in range(KC):
            par = kc % 2
            gi = it * KC + kc
            if gi >= 2:
                c.wait("vector", "pe", sq_mm[gi - 2])
            vq = v_op(c, lambda eng, o=sq[:, par], i=xT[:, kc, t0:t0 + TOK]:
                      eng.tensor_tensor(o, i, i, op=ALU.mult))
            c.wait("tensor", "dv", vq)
            t_mm(c, p_sum[:], ones_f[:], xT[:, kc, t0:t0 + TOK],
                 start=(kc == 0), stop=(kc == KC - 1))
            sq_mm[gi] = t_mm(c, p_sq[:], ones_f[:], sq[:, par],
                             start=(kc == 0), stop=(kc == KC - 1), inc=True)
        pe_done = c.counts["pe"]
        sB = statB[:, it % 2]
        sB2 = statB2[:, it % 2]
        c.wait("vector", "pe", pe_done)
        if prev_apply is not None:
            c.wait("scalar", "dv", prev_apply)
        v_op(c, lambda eng, o=sB2, i=p_sum[:]:
             eng.tensor_single_scalar(o, i, 1.0 / D, op=ALU.mult))
        v1 = v_op(c, lambda eng, o=sB, i=p_sq[:]:
                  eng.tensor_single_scalar(o, i, 1.0 / D, op=ALU.mult))
        stats_read = v1
        c.wait("scalar", "dv", v1)
        a1 = s_op(c, lambda eng, o=sq[:, 0], i=sB2:
                  eng.activation(out=o, in_=i, func=AF.Square))
        c.wait("vector", "ac", a1)
        v2 = v_op(c, lambda eng, o=sB, a=sB, b=sq[:, 0]: eng.tensor_sub(o, a, b))
        c.wait("scalar", "dv", v2)
        a2 = s_op(c, lambda eng, o=sB, i=sB, b=c.const_cols["eps_ln"]:
                  eng.activation(out=o, in_=i, func=AF.Sqrt, bias=b))
        c.wait("vector", "ac", a2)
        v3 = v_op(c, lambda eng, o=sB, i=sB: eng.reciprocal(o, i))
        c.wait("vector", "dv", v3)
        v4 = v_op(c, lambda eng, o=sB2, i=sB2, r=sB:
                  eng.scalar_tensor_tensor(o, i, -1.0, r, op0=ALU.mult, op1=ALU.mult))
        c.wait("vector", "dv", v4)
        sq_w = {}
        for kc in range(KC):
            par = kc % 2
            if kc >= 2:
                c.wait("vector", "dv", sq_w[kc - 2])
            vv = v_op(c, lambda eng, o=sq[:, par], x=xT[:, kc, t0:t0 + TOK], r=sB:
                      eng.tensor_mul(o, x, r))
            c.wait("vector", "dv", vv)
            sq_w[kc] = v_op(c, lambda eng, o=xn[:, kc, it * TOK:(it + 1) * TOK],
                            a=sq[:, par], b=sB2: eng.tensor_add(o, a, b))
        prev_apply = c.counts["dv"]
    c.produce("xn", "dv", c.counts["dv"])



DBG = {"on": False, "maps": []}


def dbg_dump(c, tag, ap, shape, dtype):
    if not DBG["on"] or (DBG.get("only") and not any(t in tag for t in DBG["only"])):
        return
    nm = "dbg_%s" % tag
    d = c.nc.dram_tensor(nm, list(shape), dtype, kind="ExternalOutput").ap()
    bt = c.barrier()
    c.wait_dep("sync", bt)
    ld = dma(c, "sync", d, ap, "dbgs")
    c.wait("sync", "dbgs", ld)
    for e in ("tensor", "vector", "scalar", "gpsimd"):
        c.wait(e, "dbgs", ld)
    DBG["maps"].append(nm)


def emit_ffn0(c, xT, half):
    tok0 = half * HALF
    with ExitStack() as es:
        xn = asb(c, "f0xn", [128, KC, HALF], BF16, es)[:]
        acc = asb(c, "f0acc", [128, KC, HALF], F32, es)[:]
        combT = asb(c, "f0cT", [E0, HALF], BF16, es)[:]
        sel_sb = asb(c, "f0sel", [E0, 2, 128], BF16, es)[:]
        gps = [aps(c, "f0g%d" % i, [128, TOK], F32, es)[:] for i in range(2)]
        ups = [aps(c, "f0u%d" % i, [128, TOK], F32, es)[:] for i in range(2)]
        p_ed = aps(c, "f0ed", [128, 3, TOK], F32, es)[:]
        p_cb = aps(c, "f0pcb", [128, TOK], F32, es)[:]

        with ExitStack() as bs:
            gate_f = asb(c, "f0gtf", [128, KC, E0], F32, bs)[:]
            gate_b = asb(c, "f0gtb", [128, KC, E0], BF16, bs)[:]
            statB = asb(c, "f0statB", [128, 2, TOK], F32, bs)[:]
            statB2 = asb(c, "f0statB2", [128, 2, TOK], F32, bs)[:]
            sq = asb(c, "f0sq", [128, 2, TOK], F32, bs)[:]
            scoresT = asb(c, "f0scT", [E0, HALF], F32, bs)[:]
            scoresT2 = asb(c, "f0scT2", [E0, HALF], F32, bs)[:]
            stok = asb(c, "f0stok", [128, NTC, E0], F32, bs)[:]
            r8 = asb(c, "f0r8", [128, 4, 8], F32, bs)[:]
            r64 = asb(c, "f0r64", [128, 2, E0], F32, bs)[:]
            r1 = asb(c, "f0r1", [128, 4], F32, bs)[:]
            ctok = asb(c, "f0ctok", [128, E0], F32, bs)[:]

            ldg = dma(c, "sync", gate_f[:], c.dram["gateT"].rearrange(
                "(kc kp) e -> kp kc e", kp=128), "ldw_gate")
            c.wait("scalar", "ldw_gate", ldg)
            a_g = s_op(c, lambda eng, o=gate_b[:], i=gate_f[:]:
                       eng.activation(out=o, in_=i, func=AF.Copy))

            ln_sc = {"p0": gps[0], "p1": gps[1], "statB": statB,
                     "statB2": statB2, "sq": sq}
            emit_ln(c, xT, xn, tok0, HALF, ln_sc)
            dbg_dump(c, "xn_h%d" % half, xn, [128, KC, HALF], BF16)

            c.require("tensor", "xn")
            c.wait("tensor", "ac", a_g)
            lg_done = []
            for it in range(NT):
                last = None
                if lg_done:
                    c.wait("tensor", "ac", lg_done[-1])
                for kc in range(KC):
                    last = t_mm(c, p_cb[0:E0, :], gate_b[:, kc],
                                xn[:, kc, it * TOK:(it + 1) * TOK],
                                start=(kc == 0), stop=(kc == KC - 1),
                                inc=(kc == KC - 1))
                c.wait("scalar", "pe", last)
                a1 = s_op(c, lambda eng, o=scoresT[:, it * TOK:(it + 1) * TOK],
                          i=p_cb[0:E0, :]: eng.activation(out=o, in_=i, func=AF.Sigmoid))
                lg_done.append(a1)
            c.wait("vector", "ac", lg_done[-1])
            gb = c.bufs["gbias"]
            vsc = v_op(c, lambda eng, o=scoresT2[:], i=scoresT[:], s=gb[:]:
                       eng.tensor_scalar(o, i, s, None, op0=ALU.add))
            ident = c.bufs["ident"]
            c.wait("tensor", "dv", vsc)
            tr_done = {}
            for tc in range(NTC):
                if tc >= 2:
                    c.wait("tensor", "ac", tr_done[tc - 2])
                pe1 = t_transpose(c, p_ed[:, tc % 2, 0:E0],
                                  scoresT2[:, tc * 128:(tc + 1) * 128],
                                  ident[0:E0, 0:E0])
                c.wait("scalar", "pe", pe1)
                tr_done[tc] = s_op(c, lambda eng, o=stok[:, tc], i=p_ed[:, tc % 2, 0:E0]:
                                   eng.activation(out=o, in_=i, func=AF.Copy))
            last_cT = None
            last_pe2 = None
            for tc in range(NTC):
                c.wait("vector", "ac", tr_done[tc])
                st_ap = stok[:, tc]
                v1 = v_op(c, lambda eng, o=r8[:, 0],
                          i=st_ap.rearrange("p (g j) -> p g j", g=8):
                          eng.reduce_max(o, i, axis=AX.X))
                c.wait("vector", "dv", v1)
                v2 = v_op(c, lambda eng, o=r8[:, 1], i=r8[:, 0]: eng.max(out=o, in_=i))
                c.wait("vector", "dv", v2)
                v3 = v_op(c, lambda eng, o=r8[:, 2], i=r8[:, 0], s=r8[:, 1, 3:4]:
                          eng.tensor_scalar(o, i, s, None, op0=ALU.is_ge))
                c.wait("vector", "dv", v3)
                for g in range(8):
                    v_op(c, lambda eng, o=r64[:, 0, g * 8:(g + 1) * 8],
                         i=st_ap[:, g * 8:(g + 1) * 8], s=r8[:, 2, g:g + 1]:
                         eng.tensor_scalar(o, i, s, None, op0=ALU.mult),
                         inc=(g == 7))
                c.wait("vector", "dv", c.counts["dv"])
                v5 = v_op(c, lambda eng, o=r8[:, 3], i=r64[:, 0]: eng.max(out=o, in_=i))
                c.wait("vector", "dv", v5)
                v6 = v_op(c, lambda eng, o=r64[:, 1], i=r64[:, 0], s=r8[:, 3, 5:6]:
                          eng.tensor_scalar(o, i, s, None, op0=ALU.is_ge))
                c.wait("vector", "dv", v6)
                v7 = v_op(c, lambda eng, o=r64[:, 1], a=r64[:, 1], b=r64[:, 0]:
                          eng.tensor_mul(o, a, b))
                c.wait("vector", "dv", v7)
                v8 = v_op(c, lambda eng, o=r1[:, 0:1], i=r64[:, 1]:
                          eng.reduce_sum(o, i, axis=AX.X))
                c.wait("vector", "dv", v8)
                v9 = v_op(c, lambda eng, o=r1[:, 1:2], i=r1[:, 0:1]:
                          eng.tensor_single_scalar(o, i, 1e-8, op=ALU.add))
                c.wait("vector", "dv", v9)
                v10 = v_op(c, lambda eng, o=r1[:, 2:3], i=r1[:, 1:2]:
                           eng.reciprocal(o, i))
                c.wait("vector", "dv", v10)
                if last_pe2 is not None:
                    c.wait("vector", "pe", last_pe2)
                v11 = v_op(c, lambda eng, o=ctok[:], i=r64[:, 1], s=r1[:, 2:3]:
                           eng.tensor_scalar(o, i, s, None, op0=ALU.mult))
                c.wait("tensor", "dv", v11)
                if last_cT is not None:
                    c.wait("tensor", "ac", last_cT)
                pe2 = t_transpose(c, p_ed[0:E0, 2, 0:128], ctok, ident[:])
                last_pe2 = pe2
                c.wait("scalar", "pe", pe2)
                last_cT = s_op(c, lambda eng, o=combT[:, tc * 128:(tc + 1) * 128],
                               i=p_ed[0:E0, 2, 0:128]:
                               eng.activation(out=o, in_=i, func=AF.Copy))
            c.produce("combT", "ac", last_cT)
            dbg_dump(c, "scoresT_h%d" % half, scoresT2, [E0, HALF], F32)
            dbg_dump(c, "combT_h%d" % half, combT, [E0, HALF], BF16)
            rt_dv = c.counts["dv"]; rt_ac = c.counts["ac"]; rt_pe = c.counts["pe"]
            for r in ("wgu0", "wgu1", "wd0", "wd1", "stage0", "stage1"):
                c.release_add(r, "dv", rt_dv)
                c.release_add(r, "ac", rt_ac)
                c.release_add(r, "pe", rt_pe)
            for b in range(3):
                c.release_add("ped%d" % b, "ac", rt_ac)
                c.release_add("ped%d" % b, "dv", rt_dv)
            c.release_add("pcb", "ac", rt_ac)
            for p in range(2):
                c.release_add("gps%d" % p, "dv", rt_dv)
                c.release_add("ups%d" % p, "dv", rt_dv)

        wg = asb(c, "f0wg", [128, 2, KC * 512], BF16, es)[:]
        wu = asb(c, "f0wu", [128, 2, KC * 512], BF16, es)[:]
        wd = asb(c, "f0wd", [128, 2, IC0 * 1024], BF16, es)[:]
        stage = asb(c, "f0st", [128, 2, 1024], F32, es)[:]
        hst = asb(c, "f0h", [128, 8, TOK], BF16, es)[:]
        sg = asb(c, "f0sg", [128, 2, TOK], F32, es)[:]
        ht = asb(c, "f0ht", [128, 2, TOK], F32, es)[:]
        cbs = asb(c, "f0cb", [128, 2, TOK], BF16, es)[:]

        stream_ctr = [0]

        def stream_chunk(dst_ap, src_ap, key=None):
            sslot = stream_ctr[0] % 2
            stream_ctr[0] += 1
            c.acquire("sync", "stage%d" % sslot)
            dst_stage = stage[:, sslot, :]
            if len(src_ap.shape) == 3:
                dst_stage = dst_stage.rearrange("p (q m) -> p q m",
                                                q=src_ap.shape[1])
            ld = dma(c, "sync", dst_stage, src_ap, "ldw_st%d" % sslot)
            c.wait("gpsimd", "ldw_st%d" % sslot, ld)
            if key:
                c.acquire("gpsimd", key)
            av = g_op(c, lambda eng, o=dst_ap, i=stage[:, sslot, :]:
                      eng.tensor_copy(o, i))
            c.release("stage%d" % sslot, "gp", av)
            return av

        wg_f = wg.rearrange("p s m -> p (s m)")
        wu_f = wu.rearrange("p s m -> p (s m)")
        wd_f = wd.rearrange("p s m -> p (s m)")
        ld_sh = None
        for i, (dst, drn) in enumerate([(wg_f, "sgT"), (wu_f, "suT"),
                                        (wd_f, "sdT")]):
            for r in ("wgu0", "wgu1", "wd0", "wd1"):
                c.acquire("sync", r)
            if i in c.wc_written["sh"]:
                c.wait_dep("sync", c.wc_written["sh"][i])
                ldv = dma(c, "sync", dst, c.wcache["sh"][i], "ldshc")
                ld_sh = ("ldshc", ldv)
            else:
                last = None
                for ch in range(8):
                    last = stream_chunk(dst[:, ch * 1024:(ch + 1) * 1024],
                                        c.dram[drn][ch * 128:(ch + 1) * 128])
                wb_cache(c, "sh", i, dst, ("gp", last), frees=("wgu0", "wgu1", "wd0", "wd1"))
                ld_sh = ("gp", last)
        c.wait_dep("tensor", ld_sh)
        acc_last = {}
        h_done = {}
        for it in range(NT):
            ts = slice(it * TOK, (it + 1) * TOK)
            for ic in range(8):
                par = ic % 2
                c.acquire("tensor", "gps%d" % par)
                c.acquire("tensor", "ups%d" % par)
                for kc in range(KC):
                    t_mm(c, gps[par][:], wg_f[:, kc * 1024 + ic * 128: kc * 1024 + ic * 128 + 128],
                         xn[:, kc, ts], start=(kc == 0), stop=(kc == KC - 1))
                last_u = None
                for kc in range(KC):
                    last_u = t_mm(c, ups[par][:], wu_f[:, kc * 1024 + ic * 128: kc * 1024 + ic * 128 + 128],
                                  xn[:, kc, ts], start=(kc == 0), stop=(kc == KC - 1),
                                  inc=(kc == KC - 1))
                c.wait("scalar", "pe", last_u)
                c.acquire("scalar", "sg%d" % par)
                a_s = s_op(c, lambda eng, o=sg[:, par], i=gps[par][:]:
                           eng.activation(out=o, in_=i, func=AF.Silu))
                c.release("gps%d" % par, "ac", a_s)
                c.wait("vector", "ac", a_s)
                c.acquire("vector", "hst%d" % ic)
                v_h = v_op(c, lambda eng, o=hst[:, ic], a=sg[:, par], b=ups[par][:]:
                           eng.tensor_mul(o, a, b))
                c.release("ups%d" % par, "dv", v_h)
                c.release("sg%d" % par, "dv", v_h)
                h_done[ic] = v_h
            c.wait("tensor", "dv", h_done[7])
            for hc in range(KC):
                bank = hc % 3
                c.acquire("tensor", "ped%d" % bank)
                last = None
                for ic in range(8):
                    last = t_mm(c, p_ed[:, bank, :],
                                wd_f[:, ic * 1024 + hc * 128: ic * 1024 + hc * 128 + 128],
                                hst[:, ic], start=(ic == 0), stop=(ic == 7),
                                inc=(ic == 7))
                c.wait("scalar", "pe", last)
                a_i = s_op(c, lambda eng, o=acc[:, hc, ts], i=p_ed[:, bank, :]:
                           eng.activation(out=o, in_=i, func=AF.Copy))
                c.release("ped%d" % bank, "ac", a_i)
                acc_last[(hc, it)] = ("ac", a_i)
            pe_now = c.counts["pe"]
            for ic in range(8):
                c.release("hst%d" % ic, "pe", pe_now)

        dbg_dump(c, "accsh_h%d" % half, acc, [128, KC, HALF], F32)

        def stream_expert_gu(e):
            slot = e % 2
            key = "wgu%d" % slot
            if e in c.wc_written["eg"]:
                c.wait_dep("sync", c.wc_written["eg"][e])
                c.wait_dep("sync", c.wc_written["eu"][e])
                c.acquire("sync", key)
                dma(c, "sync", wg[:, slot], c.wcache["eg"][e], "ldgu%d" % slot)
                ldv = dma(c, "sync", wu[:, slot], c.wcache["eu"][e],
                          "ldgu%d" % slot)
                last = ("ldgu%d" % slot, ldv)
            else:
                lv = None
                for ch in range(IC0):
                    lv = stream_chunk(wg[:, slot, ch * 1024:(ch + 1) * 1024],
                                      c.dram["egT"][e, ch * 256:(ch + 1) * 256].rearrange(
                                          "(q kp) m -> kp q m", kp=128), key)
                wb_cache(c, "eg", e, wg[:, slot], ("gp", lv), frees=("wgu%d" % slot,))
                for ch in range(IC0):
                    lv = stream_chunk(wu[:, slot, ch * 1024:(ch + 1) * 1024],
                                      c.dram["euT"][e, ch * 256:(ch + 1) * 256].rearrange(
                                          "(q kp) m -> kp q m", kp=128), key)
                wb_cache(c, "eu", e, wu[:, slot], ("gp", lv), frees=("wgu%d" % slot,))
                last = ("gp", lv)
            c.acquire("sync", "sel%d" % slot)
            lds = dma(c, "sync", sel_sb[:, slot, :],
                      c.dram["sel64"][:, e * 128:(e + 1) * 128],
                      "ldw_sel%d" % slot)
            return last, lds

        def stream_expert_ed(e):
            slot = e % 2
            key = "wd%d" % slot
            if e in c.wc_written["ed"]:
                c.wait_dep("sync", c.wc_written["ed"][e])
                c.acquire("sync", key)
                ldv = dma(c, "sync", wd[:, slot], c.wcache["ed"][e],
                          "lded%d" % slot)
                return ("lded%d" % slot, ldv)
            lv = None
            for ch in range(IC0):
                lv = stream_chunk(wd[:, slot, ch * 1024:(ch + 1) * 1024],
                                  c.dram["edT"][e, ch * 128:(ch + 1) * 128], key)
            wb_cache(c, "ed", e, wd[:, slot], ("gp", lv), frees=("wd%d" % slot,))
            return ("gp", lv)

        pe_sh = c.counts["pe"]
        for s in (0, 1):
            c.release("wgu%d" % s, "pe", pe_sh)
            c.release("wd%d" % s, "pe", pe_sh)
        gu_done = {}
        ed_cast = {}
        sel_done = {}
        gu_done[0], sel_done[0] = stream_expert_gu(0)
        ed_cast[0] = stream_expert_ed(0)
        gu_done[1], sel_done[1] = stream_expert_gu(1)
        ed_cast[1] = stream_expert_ed(1)
        hprime = {}

        def emit_ed(e, it):
            ts = slice(it * TOK, (it + 1) * TOK)
            slot = e % 2
            jslot = (e * NT + it) % 2
            c.wait_dep("tensor", ed_cast[e])
            c.wait("tensor", "dv", hprime[(e, it)])
            for hc in range(KC):
                bank = hc % 3
                c.acquire("tensor", "ped%d" % bank)
                last = None
                for ic in range(IC0):
                    last = t_mm(c, p_ed[:, bank, :],
                                wd[:, slot, ic * 1024 + hc * 128: ic * 1024 + hc * 128 + 128],
                                hst[:, jslot * 4 + ic],
                                start=(ic == 0), stop=(ic == IC0 - 1),
                                inc=(ic == IC0 - 1))
                c.wait("vector", "pe", last)
                c.wait_dep("vector", acc_last.get((hc, it)))
                vadd = v_op(c, lambda eng, o=acc[:, hc, ts], a=acc[:, hc, ts],
                            b=p_ed[:, bank, :]: eng.tensor_add(o, a, b))
                c.release("ped%d" % bank, "dv", vadd)
                acc_last[(hc, it)] = ("dv", vadd)
            pe_now = c.counts["pe"]
            for ic in range(IC0):
                c.release("hst%d" % (jslot * 4 + ic), "pe", pe_now)
            if it == NT - 1:
                c.release("wd%d" % slot, "pe", pe_now)
                c.release("sel%d" % slot, "pe", pe_now)

        jobs = [(e, it) for e in range(E0) for it in range(NT)]
        for ji, (e, it) in enumerate(jobs):
            slot = e % 2
            jslot = ji % 2
            ts = slice(it * TOK, (it + 1) * TOK)
            if it == 0:
                c.wait_dep("tensor", gu_done[e])
                c.wait("tensor", "ldw_sel%d" % slot, sel_done[e])
            c.acquire("tensor", "pcb")
            c.require("tensor", "combT")
            cbm = t_mm(c, p_cb[:], sel_sb[:, slot, :], combT[:, ts],
                       start=True, stop=True, inc=True)
            c.wait("scalar", "pe", cbm)
            c.acquire("scalar", "cbs%d" % jslot)
            cbe = s_op(c, lambda eng, o=cbs[:, jslot], i=p_cb[:]:
                       eng.activation(out=o, in_=i, func=AF.Copy))
            c.release("pcb", "ac", cbe)
            for ic in range(IC0):
                par = ic % 2
                c.acquire("tensor", "gps%d" % par)
                c.acquire("tensor", "ups%d" % par)
                for kc in range(KC):
                    t_mm(c, gps[par][:], wg[:, slot, kc * 512 + ic * 128: kc * 512 + ic * 128 + 128],
                         xn[:, kc, ts], start=(kc == 0), stop=(kc == KC - 1))
                last_u = None
                for kc in range(KC):
                    last_u = t_mm(c, ups[par][:], wu[:, slot, kc * 512 + ic * 128: kc * 512 + ic * 128 + 128],
                                  xn[:, kc, ts], start=(kc == 0), stop=(kc == KC - 1),
                                  inc=(kc == KC - 1))
                c.wait("scalar", "pe", last_u)
                c.acquire("scalar", "sg%d" % par)
                a_s = s_op(c, lambda eng, o=sg[:, par], i=gps[par][:]:
                           eng.activation(out=o, in_=i, func=AF.Silu))
                c.release("gps%d" % par, "ac", a_s)
                c.wait("vector", "ac", a_s)
                c.acquire("vector", "ht%d" % par)
                v_h = v_op(c, lambda eng, o=ht[:, par], a=sg[:, par], b=ups[par][:]:
                           eng.tensor_mul(o, a, b))
                c.release("ups%d" % par, "dv", v_h)
                c.release("sg%d" % par, "dv", v_h)
                c.wait("vector", "dv", v_h)
                c.wait("vector", "ac", cbe)
                c.acquire("vector", "hst%d" % (jslot * 4 + ic))
                g_h = v_op(c, lambda eng, o=hst[:, jslot * 4 + ic], a=ht[:, par],
                           b=cbs[:, jslot]: eng.tensor_tensor(o, a, b, op=ALU.mult))
                c.release("ht%d" % par, "dv", g_h)
                if ic == IC0 - 1:
                    c.release("cbs%d" % jslot, "dv", g_h)
                    hprime[(e, it)] = g_h
            if it == NT - 1:
                pe_now = c.counts["pe"]
                c.release("wgu%d" % slot, "pe", pe_now)
                c.release("sel%d" % slot, "pe", pe_now)
                if e + 2 < E0:
                    gu_done[e + 2], sel_done[e + 2] = stream_expert_gu(e + 2)
            if ji >= 1:
                pe_, pit_ = jobs[ji - 1]
                emit_ed(pe_, pit_)
                if pit_ == NT - 1 and pe_ + 2 < E0:
                    ed_cast[pe_ + 2] = stream_expert_ed(pe_ + 2)
        emit_ed(*jobs[-1])

        c.wait("vector", "dv", c.counts["dv"])
        for kc in range(KC):
            v_op(c, lambda eng, o=xT[:, kc, tok0:tok0 + HALF],
                 a=acc[:, kc, :], x2=xT[:, kc, tok0:tok0 + HALF]:
                 eng.scalar_tensor_tensor(o, a, 0.5, x2, op0=ALU.mult, op1=ALU.add))
        dbg_dump(c, "acc_h%d" % half, acc, [128, KC, HALF], F32)
        c.produce("xT", "dv", c.counts["dv"])
        c.barrier()


NQC = S // 128
NJC = S // 128
QT4 = S // TOK
AVT = 128
NAV = S // AVT


_ATTN_UID = [0]


def emit_attn(c, xT, layer):
    _ATTN_UID[0] += 1
    uid = _ATTN_UID[0]
    ident = c.bufs["ident"]
    identb = c.bufs["identb"]
    ones_f = c.bufs["ones_f"]
    with ExitStack() as es:
        xn = asb(c, "atxn", [128, KC, S], BF16, es)[:]
        attnout = asb(c, "atao", [128, NHP, S], BF16, es)[:]
        qt = asb(c, "atqkv", [128, 3, S], BF16, es)[:]
        wqf = asb(c, "atwqf", [128, KC, 128], F32, es)[:]
        wqb = asb(c, "atwqb", [128, KC, 384], BF16, es)[:]
        if layer == 1:
            xab = asb(c, "atxab", [96, S], BF16, es)[:]
            aqb = asb(c, "ataqb", [128, KC, 96], BF16, es)[:]
            bqf = asb(c, "atbqf", [72, 128], F32, es)[:]
            bqb = asb(c, "atbqb", [72, 128], BF16, es)[:]
            bias_sb = asb(c, "atbias", [128, 3], F32, es)[:]
            aob = asb(c, "ataob", [128, KC, 8], BF16, es)[:]
            xaO = xab[0:8, :]
        PS = aps(c, "atPS", [128, 8, TOK], F32, es)[:]
        psA = PS[:, 0, :]
        psB = PS[:, 1, :]
        psE = PS[:, 2, :]

        with ExitStack() as bs:
            statB = asb(c, "atstatB", [128, 2, TOK], F32, bs)[:]
            statB2 = asb(c, "atstatB2", [128, 2, TOK], F32, bs)[:]
            sq = asb(c, "atsq", [128, 2, TOK], F32, bs)[:]
            ln_sc = {"p0": psA, "p1": psB, "statB": statB, "statB2": statB2,
                     "sq": sq}
            emit_ln(c, xT, xn, 0, S, ln_sc)

            if layer == 0:
                c.require("vector", "xn")
                gp1 = c.bufs["gp1"]
                sqr_mm = {}
                stats_read = None
                prev_apply = None
                for it in range(QT4):
                    t0 = it * TOK
                    if stats_read is not None:
                        c.wait("tensor", "dv", stats_read)
                    if prev_apply is not None:
                        c.wait("vector", "dv", prev_apply)
                    for kc in range(KC):
                        par = kc % 2
                        gi = it * KC + kc
                        if gi >= 2:
                            c.wait("vector", "pe", sqr_mm[gi - 2])
                        vq = v_op(c, lambda eng, o=sq[:, par], i=xn[:, kc, t0:t0 + TOK]:
                                  eng.tensor_tensor(o, i, i, op=ALU.mult))
                        c.wait("tensor", "dv", vq)
                        sqr_mm[gi] = t_mm(c, psA[:], c.bufs["ones_f"][:], sq[:, par],
                                          start=(kc == 0), stop=(kc == KC - 1),
                                          inc=True)
                    pe_done = c.counts["pe"]
                    sB = statB[:, it % 2]
                    c.wait("scalar", "pe", pe_done)
                    if prev_apply is not None:
                        c.wait("scalar", "dv", prev_apply)
                    a1 = s_op(c, lambda eng, o=sB, i=psA[:]:
                              eng.activation(out=o, in_=i, func=AF.Sqrt))
                    c.wait("vector", "ac", a1)
                    v1 = v_op(c, lambda eng, o=sB, i=sB:
                              eng.tensor_single_scalar(o, i, 1e-12, op=ALU.max))
                    c.wait("vector", "dv", v1)
                    v2 = v_op(c, lambda eng, o=sB, i=sB: eng.reciprocal(o, i))
                    stats_read = v2
                    c.wait("vector", "dv", v2)
                    for kc in range(KC):
                        par = kc % 2
                        vv = v_op(c, lambda eng, o=sq[:, par], x=xn[:, kc, t0:t0 + TOK],
                                  r=sB: eng.tensor_mul(o, x, r))
                        c.wait("vector", "dv", vv)
                        v_op(c, lambda eng, o=xn[:, kc, t0:t0 + TOK], i=sq[:, par],
                             s=gp1[:, kc:kc + 1]:
                             eng.tensor_scalar(o, i, s, None, op0=ALU.mult))
                    prev_apply = c.counts["dv"]
                c.produce("xn", "dv", c.counts["dv"])
            c.barrier()

        vtm = asb(c, "atvtm", [128, NJC, 130], BF16, es)[:]
        atile = asb(c, "atat", [128, 3, S], BF16, es)[:]
        sqb = asb(c, "atsqb", [128, S], BF16, es)[:]
        mstat = asb(c, "atmst", [65, 16], F32, es)[:]
        alCM = asb(c, "atalcm", [128, NJC], F32, es)[:]
        orec = asb(c, "atorec", [1, 4, TOK], F32, es)[:]
        nrm = asb(c, "atnrm", [64, 4, TOK], F32, es)[:]
        if layer == 1:
            Ss = asb(c, "atSs", [128, S], F32, es)[:]
        von = v_op(c, lambda eng, o=vtm[:, :, 64:65]: eng.memset(o, 1.0))
        von = v_op(c, lambda eng, o=vtm[:, :, 129:130]: eng.memset(o, 1.0))

        xn_dep = c.ready["xn"]
        wname = "wqkv0" if layer == 0 else "wqkv1"
        if layer == 1:
            aqf_v = wqf.rearrange("p a b -> p (a b)")[:, 0:96 * KC].rearrange(
                "p (a b) -> p a b", a=KC)
            lda = dma(c, "sync", aqf_v, c.dram["aqkv1"].rearrange(
                "kc kp m -> kp kc m"), "ldw_aq")
            c.wait("gpsimd", "ldw_aq", lda)
            ga = g_op(c, lambda eng, o=aqb[:], i=aqf_v: eng.tensor_copy(o, i))
            aof_v = wqf.rearrange("p a b -> p (a b)")[:, 0:8 * KC].rearrange(
                "p (a b) -> p a b", a=KC)
            c.wait("sync", "gp", ga)
            ldao = dma(c, "sync", aof_v, c.dram["AoT"].rearrange(
                "(kc kp) r -> kp kc r", kp=128), "ldw_ao")
            c.wait("gpsimd", "ldw_ao", ldao)
            ga2a = g_op(c, lambda eng, o=aob[:], i=aof_v: eng.tensor_copy(o, i))
            c.release("atwqf", "gp", ga2a)

        for hp in range(NHP):
            qfam = "qkv%d" % layer
            if hp in c.wc_written[qfam]:
                c.wait_dep("sync", c.wc_written[qfam][hp])
                c.acquire("sync", "atwqb")
                ldq = dma(c, "sync", wqb.rearrange("p a b -> p (a b)"),
                          c.wcache[qfam][hp], "ldqkvc")
                gw_dep = ("ldqkvc", ldq)
            else:
                c.acquire("gpsimd", "atwqb")
                gw = None
                for p in range(3):
                    c.acquire("sync", "atwqf")
                    ldw = dma(c, "sync", wqf[:], c.dram[wname][hp, :, :, p].rearrange(
                        "kc kp m -> kp kc m"), "ldw_qkv")
                    c.wait("gpsimd", "ldw_qkv", ldw)
                    gw = g_op(c, lambda eng, o=wqb[:, :, p * 128:(p + 1) * 128], i=wqf[:]:
                              eng.tensor_copy(o, i))
                    c.release("atwqf", "gp", gw)
                wb_cache(c, qfam, hp, wqb.rearrange("p a b -> p (a b)"),
                         ("gp", gw), frees=("atwqb",))
                gw_dep = ("gp", gw)
            if layer == 1:
                c.acquire("sync", "atbq")
                for p in range(3):
                    dma(c, "sync", bqf[32 * p:32 * p + 8, :],
                        c.dram["bqkv1"][hp, :, p, :], "ldw_bq")
                ldbias = dma(c, "sync", bias_sb[:], c.dram["biasqkv1"][hp], "ldw_bq")
                c.wait("gpsimd", "ldw_bq", ldbias)
                gb = None
                for p in range(3):
                    gb = g_op(c, lambda eng, o=bqb[32 * p:32 * p + 8, :],
                              i=bqf[32 * p:32 * p + 8, :]: eng.tensor_copy(o, i))
            c.wait_dep("tensor", xn_dep)
            c.wait_dep("tensor", gw_dep)
            if layer == 1:
                c.wait("tensor", "gp", gb)
                c.wait("tensor", "gp", ga)
                c.wait("scalar", "ldw_bq", ldbias)
                for it in range(QT4):
                    ts = slice(it * TOK, (it + 1) * TOK)
                    c.acquire("tensor", "atb2")
                    last = None
                    for kc in range(KC):
                        last = t_mm(c, psE[0:96, :], aqb[:, kc, :], xn[:, kc, ts],
                                    start=(kc == 0), stop=(kc == KC - 1),
                                    inc=(kc == KC - 1))
                    c.wait("scalar", "pe", last)
                    ax = s_op(c, lambda eng, o=xab[:, ts], i=psE[0:96, :]:
                              eng.activation(out=o, in_=i, func=AF.Copy, scale=0.125))
                    c.release("atb2", "ac", ax)
                xab_ready = c.counts["ac"]
                c.wait("tensor", "ac", xab_ready)
            for p in range(3):
                for it in range(QT4):
                    ts = slice(it * TOK, (it + 1) * TOK)
                    bank, bname = (psA, "atb0") if (p * QT4 + it) % 2 == 0 else (psB, "atb1")
                    c.acquire("tensor", bname)
                    nk = KC if layer == 1 else KC - 1
                    for kc in range(nk):
                        t_mm(c, bank[:], wqb[:, kc, p * 128:(p + 1) * 128],
                             xn[:, kc, ts], start=(kc == 0), stop=False)
                    if layer == 1:
                        last = t_mm(c, bank[:], bqb[32 * p:32 * p + 8, :],
                                    xab[32 * p:32 * p + 8, ts],
                                    start=False, stop=True, inc=True)
                    else:
                        last = t_mm(c, bank[:], wqb[:, KC - 1, p * 128:(p + 1) * 128],
                                    xn[:, KC - 1, ts], start=False, stop=True, inc=True)
                    if layer == 0:
                        c.wait("scalar", "pe", last)
                        scale = 0.125 if p == 0 else 1.0
                        ev = s_op(c, lambda eng, o=qt[:, p, ts], i=bank[:], sc=scale:
                                  eng.activation(out=o, in_=i, func=AF.Copy, scale=sc))
                    else:
                        c.wait("vector", "pe", last)
                        c.wait("vector", "ldw_bq", ldbias)
                        ev = v_op(c, lambda eng, o=qt[:, p, ts], i=bank[:],
                                  b=bias_sb[:, p:p + 1]:
                                  eng.tensor_scalar(o, i, b, None, op0=ALU.add))
                        c.release(bname, "dv", ev)
                        continue
                    c.release(bname, "ac", ev)
            c.release("atwqb", "pe", c.counts["pe"])
            if layer == 1:
                c.release("atbq", "ac", c.counts["ac"])
            c.wait("tensor", "ac", c.counts["ac"])
            c.wait("tensor", "dv", c.counts["dv"])
            for jc in range(NJC):
                bname = "atb%d" % (2 + jc % 2)
                c.acquire("tensor", bname)
                pv = t_mm(c, PS[:, 2 + jc % 2, 0:128],
                          qt[:, 2, jc * 128:(jc + 1) * 128], identb[:],
                          start=True, stop=True, inc=True)
                c.wait("vector", "pe", pv)
                c.wait("vector", "dv", von)
                v_op(c, lambda eng, o=vtm[:, jc, 0:64], i=PS[:, 2 + jc % 2, 0:64]:
                     eng.tensor_copy(o, i))
                v2 = v_op(c, lambda eng, o=vtm[:, jc, 65:129],
                          i=PS[:, 2 + jc % 2, 64:128]: eng.tensor_copy(o, i))
                c.release(bname, "dv", v2)
            vtm_ready = c.counts["dv"]
            blk2 = c.bufs["blk2"]
            c.wait("vector", "ac", c.counts["ac"])
            c.wait("vector", "dv", c.counts["dv"])
            sq_done = None
            for side in range(2):
                if sq_done is not None:
                    c.wait("vector", "pe", sq_done)
                vq = v_op(c, lambda eng, o=sqb[:], i=qt[:, side, :]:
                          eng.tensor_tensor(o, i, i, op=ALU.mult))
                c.wait("tensor", "dv", vq)
                for t in range(4):
                    c.acquire("tensor", "atb%d" % (4 + t))
                    sq_done = t_mm(c, PS[0:65, 4 + t, :], blk2[:, 0:65],
                                   sqb[:, t * TOK:(t + 1) * TOK],
                                   start=True, stop=True, inc=True)
                c.wait("vector", "pe", sq_done)
                for t in range(4):
                    vr = v_op(c, lambda eng,
                              o=mstat[:, side * 5 + t:side * 5 + t + 1],
                              i=PS[0:65, 4 + t, :]: eng.reduce_max(o, i, axis=AX.X))
                    c.release("atb%d" % (4 + t), "dv", vr)
                c.wait("vector", "dv", c.counts["dv"])
                v_op(c, lambda eng, o=mstat[:, side * 5 + 4:side * 5 + 5],
                     i=mstat[:, side * 5:side * 5 + 4]:
                     eng.reduce_max(o, i, axis=AX.X))
            c.wait("vector", "dv", c.counts["dv"])
            vmul = v_op(c, lambda eng, o=mstat[:, 10:11], a=mstat[:, 4:5],
                        b=mstat[:, 9:10]:
                        eng.tensor_scalar(o, a, b, None, op0=ALU.mult))
            c.wait("scalar", "dv", vmul)
            asq = s_op(c, lambda eng, o=mstat[:, 11:12], i=mstat[:, 10:11]:
                       eng.activation(out=o, in_=i, func=AF.Sqrt))
            c.wait("vector", "ac", asq)
            slp = c.bufs["slp2"]
            vB = v_op(c, lambda eng, o=mstat[:, 12:13], i=mstat[:, 11:12],
                      s=slp[:, layer, hp:hp + 1]:
                      eng.tensor_scalar(o, i, s, None, op0=ALU.add))
            if hp == 0:
                dbg_dump(c, "qt_u%d" % uid, qt, [128, 3, S], BF16)
                dbg_dump(c, "vtm_u%d" % uid, vtm, [128, NJC, 130], BF16)
                dbg_dump(c, "mst_u%d" % uid, mstat, [65, 16], F32)
            for hh in range(2):
                h = hp * 2 + hh
                r0 = 64 * hh
                c.wait("tensor", "dv", vB)
                c.acquire("tensor", "atb2")
                pb = t_mm(c, PS[:, 2, 0:1], ones_f[r0:r0 + 1, 0:128],
                          mstat[r0:r0 + 1, 12:13], start=True, stop=True, inc=True)
                c.wait("vector", "pe", pb)
                if layer == 1:
                    alC = c.bufs["alibiC"]
                    vbias = v_op(c, lambda eng, o=alCM[:, 0:NJC], i=alC[:, h, :],
                                 s=PS[:, 2, 0:1]:
                                 eng.tensor_scalar(o, i, s, None, op0=ALU.subtract))
                else:
                    vbias = v_op(c, lambda eng, o=alCM[:, 0:1], i=PS[:, 2, 0:1]:
                                 eng.tensor_single_scalar(o, i, -1.0, op=ALU.mult))
                c.release("atb2", "dv", vbias)
                c.wait("scalar", "dv", vbias)
                if hp == 0 and hh == 0:
                    dbg_dump(c, "alcm_u%d" % uid, alCM, [128, NJC], F32)
                c.wait("scalar", "dv", vtm_ready)
                c.wait("tensor", "dv", vtm_ready)
                pe_st = {}
                pe_av = {}
                av15 = {}
                exa = {}
                exb = {}
                for jc in range(NJC + 1):
                    if jc > 0:
                        pj = jc - 1
                        sl = pj % 3
                        c.wait("tensor", "ac", exa[pj])
                        for t4 in (0, 1):
                            c.acquire("tensor", "atb%d" % (4 + t4))
                            v = t_mm(c, PS[0:65, 4 + t4, :],
                                     vtm[:, pj, hh * 65:hh * 65 + 65],
                                     atile[:, sl, t4 * TOK:(t4 + 1) * TOK],
                                     start=(pj == 0), stop=(pj == NJC - 1),
                                     inc=(t4 == 1 or pj == NJC - 1))
                            if pj == NJC - 1:
                                av15[t4] = v
                        c.wait("tensor", "ac", exb[pj])
                        for t4 in (2, 3):
                            c.acquire("tensor", "atb%d" % (4 + t4))
                            v = t_mm(c, PS[0:65, 4 + t4, :],
                                     vtm[:, pj, hh * 65:hh * 65 + 65],
                                     atile[:, sl, t4 * TOK:(t4 + 1) * TOK],
                                     start=(pj == 0), stop=(pj == NJC - 1),
                                     inc=(t4 == 3 or pj == NJC - 1))
                            if pj == NJC - 1:
                                av15[t4] = v
                        pe_av[pj] = v
                    if jc == NJC:
                        break
                    js = slice(jc * 128, (jc + 1) * 128)
                    if jc > 0:
                        c.wait("tensor", "ac", exa[jc - 1])
                    for t4 in (0, 1):
                        c.acquire("tensor", "atb%d" % t4)
                        v = t_mm(c, PS[:, t4, :], qt[r0:r0 + 64, 1, js],
                                 qt[r0:r0 + 64, 0, t4 * TOK:(t4 + 1) * TOK],
                                 start=True, stop=True, inc=(t4 == 1))
                    pe_st[(jc, 0)] = v
                    if jc > 0:
                        c.wait("tensor", "ac", exb[jc - 1])
                    for t4 in (2, 3):
                        c.acquire("tensor", "atb%d" % t4)
                        v = t_mm(c, PS[:, t4, :], qt[r0:r0 + 64, 1, js],
                                 qt[r0:r0 + 64, 0, t4 * TOK:(t4 + 1) * TOK],
                                 start=True, stop=True, inc=(t4 == 3))
                    pe_st[(jc, 1)] = v
                    sl = jc % 3
                    bias_ap = alCM[:, jc:jc + 1] if layer == 1 else alCM[:, 0:1]
                    c.wait("scalar", "pe", pe_st[(jc, 0)])
                    if jc >= 3:
                        c.wait("scalar", "pe", pe_av[jc - 3])
                    exa[jc] = s_op(c, lambda eng, o=atile[:, sl, 0:2 * TOK],
                                   i=PS[:, 0:2, :].rearrange("p a b -> p (a b)"),
                                   b=bias_ap:
                                   eng.activation(out=o, in_=i, func=AF.Exp, bias=b))
                    c.wait("scalar", "pe", pe_st[(jc, 1)])
                    exb[jc] = s_op(c, lambda eng, o=atile[:, sl, 2 * TOK:4 * TOK],
                                   i=PS[:, 2:4, :].rearrange("p a b -> p (a b)"),
                                   b=bias_ap:
                                   eng.activation(out=o, in_=i, func=AF.Exp, bias=b))
                vrs = {}
                pb2s = {}
                vcs = {}
                for t4 in range(4):
                    c.wait("vector", "pe", av15[t4])
                    vrs[t4] = v_op(c, lambda eng, o=orec[:, t4, :],
                                   i=PS[64:65, 4 + t4, :]: eng.reciprocal(o, i))
                c.wait("tensor", "ac", exb[NJC - 1])
                c.wait("tensor", "ac", exa[NJC - 1])
                for t4 in range(4):
                    c.wait("tensor", "dv", vrs[t4])
                    c.acquire("tensor", "atb%d" % t4)
                    pb2s[t4] = t_mm(c, PS[0:64, t4, :], ones_f[0:1, 0:64],
                                    orec[0:1, t4, :], start=True, stop=True,
                                    inc=True)
                for t4 in range(4):
                    c.wait("vector", "pe", pb2s[t4])
                    vcs[t4] = v_op(c, lambda eng, o=nrm[:, t4, :],
                                   i=PS[0:64, t4, :]: eng.tensor_copy(o, i))
                    c.release("atb%d" % t4, "dv", vcs[t4])
                for t4 in range(4):
                    c.wait("vector", "dv", vcs[t4])
                    vn = v_op(c, lambda eng,
                              o=attnout[r0:r0 + 64, hp, t4 * TOK:(t4 + 1) * TOK],
                              a=PS[0:64, 4 + t4, :], b=nrm[:, t4, :]:
                              eng.tensor_tensor(o, a, b, op=ALU.mult))
                    c.release("atb%d" % (4 + t4), "dv", vn)
        ao_ready = c.counts["dv"]
        dbg_dump(c, "ao_u%d" % uid, attnout, [128, NHP, S], BF16)

        if layer == 1:
            c.wait("tensor", "dv", ao_ready)
            for it in range(QT4):
                ts = slice(it * TOK, (it + 1) * TOK)
                c.acquire("tensor", "atb2")
                last = None
                for kc in range(KC):
                    last = t_mm(c, psE[0:8, :], aob[:, kc, :], attnout[:, kc, ts],
                                start=(kc == 0), stop=(kc == KC - 1),
                                inc=(kc == KC - 1))
                c.wait("scalar", "pe", last)
                axo = s_op(c, lambda eng, o=xaO[:, ts], i=psE[0:8, :]:
                           eng.activation(out=o, in_=i, func=AF.Copy, scale=0.125))
                c.release("atb2", "ac", axo)
            xaO_ready = c.counts["ac"]
            c.require("vector", "xT")
            c.require("vector", "xn")
            for kc in range(KC):
                v_op(c, lambda eng, o=xT[:, kc, :], a=xT[:, kc, :], b=xn[:, kc, :]:
                     eng.tensor_add(o, a, b))
            xplus = c.counts["dv"]
        c.wait("tensor", "dv", ao_ready)
        woname = "wo0" if layer == 0 else "WoT"
        bo1 = c.bufs["bo1"]
        wqb_flat = wqb.rearrange("p a b -> p (a b)")
        wo_b = wqb_flat[:, 0:1024].rearrange("p (a b) -> p a b", a=KC)
        if layer == 1:
            bob = wqb_flat[0:8, 2048:3072].rearrange("p (a b) -> p a b", a=KC)
            bof_v = wqf[0:8, :, :]
            c.acquire("sync", "atwqf")
            ldbo = dma(c, "sync", bof_v, c.dram["BoTr"].rearrange(
                "r (kc m) -> r kc m", kc=KC), "ldw_bo")
            c.wait("gpsimd", "ldw_bo", ldbo)
            c.acquire("gpsimd", "atwqb")
            ga2 = g_op(c, lambda eng, o=bob, i=bof_v: eng.tensor_copy(o, i))
            c.release("atwqf", "gp", ga2)
            c.wait("tensor", "gp", ga2)
        wofam = "wo%d" % layer
        for mc in range(KC):
            if mc in c.wc_written[wofam]:
                c.wait_dep("sync", c.wc_written[wofam][mc])
                c.acquire("sync", "atwqb")
                ldq = dma(c, "sync", wo_b,
                          c.wcache[wofam][mc].rearrange("p (a b) -> p a b", a=KC),
                          "ldqkvc")
                gw_dep = ("ldqkvc", ldq)
            else:
                c.acquire("sync", "atwqf")
                ldw = dma(c, "sync", wqf[:, 0:KC, 0:128],
                          c.dram[woname][:, mc * 128:(mc + 1) * 128].rearrange(
                              "(kc kp) m -> kp kc m", kp=128), "ldw_qkv")
                c.wait("gpsimd", "ldw_qkv", ldw)
                c.acquire("gpsimd", "atwqb")
                gw = g_op(c, lambda eng, o=wo_b, i=wqf[:, 0:KC, 0:128]:
                          eng.tensor_copy(o, i))
                c.release("atwqf", "gp", gw)
                wb_cache(c, wofam, mc, wo_b, ("gp", gw), frees=("atwqb",))
                gw_dep = ("gp", gw)
            c.wait_dep("tensor", gw_dep)
            if layer == 1:
                c.wait("tensor", "ac", xaO_ready)
            for it in range(QT4):
                ts = slice(it * TOK, (it + 1) * TOK)
                bank, bname = (psA, "atb0") if (mc * QT4 + it) % 2 == 0 else (psB, "atb1")
                c.acquire("tensor", bname)
                nk2 = KC if layer == 1 else KC - 1
                for kc in range(nk2):
                    t_mm(c, bank[:], wo_b[:, kc, :],
                         attnout[:, kc, ts], start=(kc == 0), stop=False)
                if layer == 1:
                    last = t_mm(c, bank[:], bob[0:8, mc, :],
                                xaO[0:8, ts], start=False, stop=True, inc=True)
                    c.wait("vector", "pe", last)
                    c.acquire("vector", "atSs_wo")
                    evs = v_op(c, lambda eng, o=Ss[:, ts], i=bank[:],
                               b=bo1[:, mc:mc + 1]:
                               eng.tensor_scalar(o, i, b, None, op0=ALU.add))
                    c.release(bname, "dv", evs)
                    c.wait("vector", "dv", evs)
                    c.wait("vector", "dv", xplus)
                    rz = c.bufs["rz"]
                    rv = v_op(c, lambda eng, o=xT[:, mc, ts], a=Ss[:, ts],
                              x2=xT[:, mc, ts], s=rz[:]:
                              eng.scalar_tensor_tensor(o, a, s, x2, op0=ALU.mult,
                                                       op1=ALU.add))
                    c.release("atSs_wo", "dv", rv)
                else:
                    last = t_mm(c, bank[:], wo_b[:, KC - 1, :],
                                attnout[:, KC - 1, ts], start=False, stop=True,
                                inc=True)
                    c.wait("vector", "pe", last)
                    rv = v_op(c, lambda eng, o=xT[:, mc, ts], a=xT[:, mc, ts],
                              b=bank[:]: eng.tensor_add(o, a, b))
                    c.release(bname, "dv", rv)
            c.release("atwqb", "pe", c.counts["pe"])
        c.produce("xT", "dv", c.counts["dv"])
        c.barrier()


def emit_ffn1(c, xT, half):
    tok0 = half * HALF
    ident = c.bufs["ident"]
    with ExitStack() as es:
        xn = asb(c, "f1xn", [128, KC, HALF], BF16, es)[:]
        acc = asb(c, "f1acc", [128, KC, HALF], F32, es)[:]
        cT = asb(c, "f1cT", [E1, HALF], BF16, es)[:]
        beta_row = asb(c, "f1brow", [1, HALF], F32, es)[:]
        web = asb(c, "f1web", [128, 2, KC, D], BF16, es)[:]
        stage = asb(c, "f1st", [128, 2, 1024], F32, es)[:]
        xnp = asb(c, "f1xnp", [128, 2, KC, TOK], BF16, es)[:]
        cbs = asb(c, "f1cb", [128, 2, TOK], BF16, es)[:]
        sel_sb = asb(c, "f1sel", [E1, 2, 128], BF16, es)[:]
        rt_f = asb(c, "f1rtf", [128, KC, E1], F32, es)[:]
        rt_b = asb(c, "f1rtb", [128, KC, E1], BF16, es)[:]
        p_ed = aps(c, "f1ed", [128, 3, TOK], F32, es)[:]
        p_cb = aps(c, "f1pcb", [128, TOK], F32, es)[:]
        p_a = aps(c, "f1pa", [128, TOK], F32, es)[:]
        p_b = aps(c, "f1pb", [128, TOK], F32, es)[:]

        with ExitStack() as bs:
            statB = asb(c, "f1statB", [128, 2, TOK], F32, bs)[:]
            statB2 = asb(c, "f1statB2", [128, 2, TOK], F32, bs)[:]
            sq = asb(c, "f1sq", [128, 2, TOK], F32, bs)[:]
            lgT = asb(c, "f1lgT", [E1, HALF], F32, bs)[:]
            ltok = asb(c, "f1ltok", [128, NTC, E1], F32, bs)[:]
            r8 = asb(c, "f1r8", [128, 4, E1], F32, bs)[:]
            r1 = asb(c, "f1r1", [128, 6], F32, bs)[:]
            ctokb = asb(c, "f1ctok", [128, E1], F32, bs)[:]
            bcol = asb(c, "f1bcol", [128, NTC], F32, bs)[:]
            brt = asb(c, "f1brt", [16, 128], F32, bs)[:]

            ldr = dma(c, "sync", rt_f[:], c.dram["routerT"].rearrange(
                "(kc kp) e -> kp kc e", kp=128), "ldw_gate")
            c.wait("scalar", "ldw_gate", ldr)
            a_r = s_op(c, lambda eng, o=rt_b[:], i=rt_f[:]:
                       eng.activation(out=o, in_=i, func=AF.Copy))

            ln_sc = {"p0": p_a, "p1": p_b, "statB": statB, "statB2": statB2,
                     "sq": sq}
            emit_ln(c, xT, xn, tok0, HALF, ln_sc)

            c.require("tensor", "xn")
            c.wait("tensor", "ac", a_r)
            lg_done = []
            for it in range(NT):
                if lg_done:
                    c.wait("tensor", "ac", lg_done[-1])
                last = None
                for kc in range(KC):
                    last = t_mm(c, p_cb[0:E1, :], rt_b[:, kc],
                                xn[:, kc, it * TOK:(it + 1) * TOK],
                                start=(kc == 0), stop=(kc == KC - 1),
                                inc=(kc == KC - 1))
                c.wait("scalar", "pe", last)
                a1 = s_op(c, lambda eng, o=lgT[:, it * TOK:(it + 1) * TOK],
                          i=p_cb[0:E1, :]: eng.activation(out=o, in_=i, func=AF.Copy))
                lg_done.append(a1)
            c.wait("tensor", "ac", lg_done[-1])
            tr_done = {}
            for tc in range(NTC):
                if tc >= 2:
                    c.wait("tensor", "ac", tr_done[tc - 2])
                pe1 = t_transpose(c, p_ed[:, tc % 2, 0:E1],
                                  lgT[:, tc * 128:(tc + 1) * 128],
                                  ident[0:E1, 0:E1])
                c.wait("scalar", "pe", pe1)
                tr_done[tc] = s_op(c, lambda eng, o=ltok[:, tc], i=p_ed[:, tc % 2, 0:E1]:
                                   eng.activation(out=o, in_=i, func=AF.Copy))
            last_cT = None
            last_pe2 = None
            for tc in range(NTC):
                c.wait("vector", "ac", tr_done[tc])
                lt = ltok[:, tc]
                v1 = v_op(c, lambda eng, o=r1[:, 0:1], i=lt:
                          eng.reduce_max(o, i, axis=AX.X, negate=True))
                c.wait("scalar", "dv", v1)
                a1 = s_op(c, lambda eng, o=r8[:, 0], i=lt, b=r1[:, 0:1],
                          acc=r1[:, 1:2]:
                          eng.activation(out=o, in_=i, func=AF.Exp, bias=b,
                                         accum_out=acc))
                c.wait("vector", "ac", a1)
                v2 = v_op(c, lambda eng, o=r1[:, 2:3], i=r1[:, 1:2]:
                          eng.reciprocal(o, i))
                c.wait("vector", "dv", v2)
                v3 = v_op(c, lambda eng, o=r8[:, 1], i=r8[:, 0], s=r1[:, 2:3]:
                          eng.tensor_scalar(o, i, s, None, op0=ALU.mult))
                c.wait("vector", "dv", v3)
                v4 = v_op(c, lambda eng, o=r8[:, 2], i=r8[:, 1]: eng.max(out=o, in_=i))
                c.wait("vector", "dv", v4)
                v5 = v_op(c, lambda eng, o=r1[:, 3:4], a=r8[:, 2, 0:1], b=r8[:, 2, 1:2]:
                          eng.tensor_add(o, a, b))
                c.wait("vector", "dv", v5)
                v6 = v_op(c, lambda eng, o=r1[:, 3:4], i=r1[:, 3:4]:
                          eng.tensor_single_scalar(o, i, 1e-6, op=ALU.add))
                c.wait("vector", "dv", v6)
                v7 = v_op(c, lambda eng, o=r1[:, 4:5], i=r1[:, 3:4]:
                          eng.reciprocal(o, i))
                c.wait("vector", "dv", v7)
                v8 = v_op(c, lambda eng, o=r8[:, 3], i=r8[:, 1], s=r8[:, 2, 1:2]:
                          eng.tensor_scalar(o, i, s, None, op0=ALU.is_ge))
                c.wait("vector", "dv", v8)
                v9 = v_op(c, lambda eng, o=r8[:, 3], a=r8[:, 3], b=r8[:, 1]:
                          eng.tensor_mul(o, a, b))
                c.wait("vector", "dv", v9)
                if last_pe2 is not None:
                    c.wait("vector", "pe", last_pe2)
                v10 = v_op(c, lambda eng, o=ctokb[:], i=r8[:, 3], s=r1[:, 4:5]:
                           eng.tensor_scalar(o, i, s, None, op0=ALU.mult))
                c.wait("vector", "dv", v10)
                v11 = v_op(c, lambda eng, o=r8[:, 3], a=ctokb[:], b=r8[:, 1]:
                           eng.tensor_mul(o, a, b))
                c.wait("vector", "dv", v11)
                v12 = v_op(c, lambda eng, o=bcol[:, tc:tc + 1], i=r8[:, 3]:
                           eng.reduce_sum(o, i, axis=AX.X))
                c.wait("tensor", "dv", v12)
                if last_cT is not None:
                    c.wait("tensor", "ac", last_cT)
                pe2 = t_transpose(c, p_ed[0:E1, 2, 0:128], ctokb, ident[:])
                last_pe2 = pe2
                c.wait("scalar", "pe", pe2)
                last_cT = s_op(c, lambda eng, o=cT[:, tc * 128:(tc + 1) * 128],
                               i=p_ed[0:E1, 2, 0:128]:
                               eng.activation(out=o, in_=i, func=AF.Copy))
            c.wait("tensor", "dv", c.counts["dv"])
            c.wait("tensor", "ac", last_cT)
            pe3 = t_transpose(c, p_ed[0:NTC, 2, 0:128], bcol, ident[:])
            c.wait("scalar", "pe", pe3)
            ab = s_op(c, lambda eng, o=brt[0:NTC, :], i=p_ed[0:NTC, 2, 0:128]:
                      eng.activation(out=o, in_=i, func=AF.Copy))
            c.wait("sync", "ac", ab)
            b1 = dma(c, "sync", c.dram["beta_bounce"][:],
                     brt[0:NTC, :], "ldw_mls")
            c.wait("sync", "ldw_mls", b1)
            b2 = dma(c, "sync", beta_row[:],
                     c.dram["beta_bounce"].rearrange("a b -> (a b)").unsqueeze(0),
                     "ldw_mls")
            c.wait("tensor", "ldw_mls", b2)
            c.barrier()

        stream_ctr = [0]

        def stream_chunk(dst_ap, src_ap, key=None):
            sslot = stream_ctr[0] % 2
            stream_ctr[0] += 1
            c.acquire("sync", "f1stage%d" % sslot)
            ld = dma(c, "sync", stage[:, sslot, :], src_ap, "ldw_st%d" % sslot)
            c.wait("gpsimd", "ldw_st%d" % sslot, ld)
            if key:
                c.acquire("gpsimd", key)
            av = g_op(c, lambda eng, o=dst_ap, i=stage[:, sslot, :]:
                      eng.tensor_copy(o, i))
            c.release("f1stage%d" % sslot, "gp", av)
            return av

        def stream_expert(e):
            slot = e % 2
            key = "f1w%d" % slot
            if e in c.wc_written["expw"]:
                c.wait_dep("sync", c.wc_written["expw"][e])
                c.acquire("sync", key)
                ldv = dma(c, "sync", web[:, slot].rearrange("p a b -> p (a b)"),
                          c.wcache["expw"][e], "ldex%d" % slot)
                last = ("ldex%d" % slot, ldv)
            else:
                lv = None
                for ch in range(KC):
                    lv = stream_chunk(web[:, slot, ch],
                                      c.dram["expW"][e, ch * 128:(ch + 1) * 128],
                                      key)
                wb_cache(c, "expw", e,
                         web[:, slot].rearrange("p a b -> p (a b)"), ("gp", lv),
                         frees=("f1w%d" % slot,))
                last = ("gp", lv)
            c.acquire("sync", "f1sel%d" % slot)
            lds = dma(c, "sync", sel_sb[:, slot, :],
                      c.dram["sel8"][:, e * 128:(e + 1) * 128],
                      "ldw_sel%d" % slot)
            return last, lds

        w_done = {}
        sel_done = {}
        w_done[0], sel_done[0] = stream_expert(0)
        w_done[1], sel_done[1] = stream_expert(1)
        acc_last = {}
        for e in range(E1):
            slot = e % 2
            c.wait_dep("tensor", w_done[e])
            c.wait("tensor", "ldw_sel%d" % slot, sel_done[e])
            for it in range(NT):
                ts = slice(it * TOK, (it + 1) * TOK)
                ji = e * NT + it
                jslot = ji % 2
                c.acquire("tensor", "f1pcb")
                cbm = t_mm(c, p_cb[:], sel_sb[:, slot, :], cT[:, ts],
                           start=True, stop=True, inc=True)
                c.wait("scalar", "pe", cbm)
                c.acquire("scalar", "f1cbs%d" % jslot)
                cbe = s_op(c, lambda eng, o=cbs[:, jslot], i=p_cb[:]:
                           eng.activation(out=o, in_=i, func=AF.Copy))
                c.release("f1pcb", "ac", cbe)
                c.wait("vector", "ac", cbe)
                c.require("vector", "xn")
                c.acquire("vector", "f1xnp%d" % jslot)
                g_h = None
                for kc in range(KC):
                    g_h = v_op(c, lambda eng, o=xnp[:, jslot, kc], a=xn[:, kc, ts],
                               b=cbs[:, jslot]: eng.tensor_tensor(o, a, b, op=ALU.mult))
                c.release("f1cbs%d" % jslot, "dv", g_h)
                c.wait("tensor", "dv", g_h)
                for mc in range(KC):
                    bank = mc % 3
                    c.acquire("tensor", "f1ped%d" % bank)
                    first = True
                    if e == 0:
                        t_mm(c, p_ed[:, bank, :], c.bufs["onerow_f"][0:1, :],
                             beta_row[0:1, ts], start=True, stop=False)
                        first = False
                    last = None
                    for kc in range(KC):
                        last = t_mm(c, p_ed[:, bank, :],
                                    web[:, slot, kc, mc * 128:(mc + 1) * 128],
                                    xnp[:, jslot, kc], start=first and kc == 0,
                                    stop=(kc == KC - 1), inc=(kc == KC - 1))
                        first = False
                    if e == 0:
                        c.wait("scalar", "pe", last)
                        a_i = s_op(c, lambda eng, o=acc[:, mc, ts], i=p_ed[:, bank, :]:
                                   eng.activation(out=o, in_=i, func=AF.Copy))
                        c.release("f1ped%d" % bank, "ac", a_i)
                        acc_last[(mc, it)] = ("ac", a_i)
                    else:
                        c.wait("vector", "pe", last)
                        c.wait_dep("vector", acc_last[(mc, it)])
                        vadd = v_op(c, lambda eng, o=acc[:, mc, ts],
                                    a=acc[:, mc, ts], b=p_ed[:, bank, :]:
                                    eng.tensor_add(o, a, b))
                        c.release("f1ped%d" % bank, "dv", vadd)
                        acc_last[(mc, it)] = ("dv", vadd)
                c.release("f1xnp%d" % jslot, "pe", c.counts["pe"])
            c.release("f1w%d" % slot, "pe", c.counts["pe"])
            c.release("f1sel%d" % slot, "pe", c.counts["pe"])
            if e + 2 < E1:
                w_done[e + 2], sel_done[e + 2] = stream_expert(e + 2)

        c.wait("vector", "dv", c.counts["dv"])
        c.wait("vector", "ac", c.counts["ac"])
        for kc in range(KC):
            v_op(c, lambda eng, o=xT[:, kc, tok0:tok0 + HALF],
                 a=acc[:, kc, :], x2=xT[:, kc, tok0:tok0 + HALF]:
                 eng.scalar_tensor_tensor(o, a, 0.5, x2, op0=ALU.mult, op1=ALU.add))
        c.produce("xT", "dv", c.counts["dv"])
        c.barrier()



DSHAPES_F32 = {
    "sgT": (D, ISH), "suT": (D, ISH), "sdT": (ISH, D),
    "gateT": (D, E0), "gbias": (E0, 1),
    "egT": (E0, D, INTER0), "euT": (E0, D, INTER0), "edT": (E0, INTER0, D),
    "gamma": (128, KC), "wqkv0": (NHP, KC, 128, 3, 128), "wo0": (D, D),
    "WoT": (D, D), "AoT": (D, 8), "BoTr": (8, D),
    "AqT": (D, 8), "AkT": (D, 8), "AvT": (D, 8),
    "BqTr": (8, D), "BkTr": (8, D), "BvTr": (8, D),
    "wqkv1": (NHP, KC, 128, 3, 128), "aqkv1": (KC, 128, 96),
    "bqkv1": (NHP, 8, 3, 128), "biasqkv1": (NHP, 128, 3),
    "bo1": (128, KC), "rz": (128, 1),
    "routerT": (D, E1), "expW": (E1, D, D),
    "ident": (128, 128), "ones_f": (128, 128), "onerow_f": (1, 128),
    "ones33": (33, 128),
    "alibiC": (128, HEADS, S // 128), "slp2": (65, 2, NHP),
}
DSHAPES_BF16 = {"identb": (128, 128), "sel64": (E0, E0 * 128),
                "blk2": (128, 65),
                "sel8": (E1, E1 * 128),
                "ones_b": (128, 128), "onerow_b": (1, 128)}


def build_program(parts):
    nc = bass.Bass("TRN2", target_bir_lowering=False)
    es = ExitStack()
    c = Ctx(nc, es)
    for k, shp in DSHAPES_F32.items():
        c.dram[k] = nc.dram_tensor(k, list(shp), F32, kind="ExternalInput").ap()
    for k, shp in DSHAPES_BF16.items():
        c.dram[k] = nc.dram_tensor(k, list(shp), BF16, kind="ExternalInput").ap()
    c.dram["xT"] = nc.dram_tensor("xT", [SPC, KC, 128, S], F32,
                                  kind="ExternalInput").ap()
    c.dram["outT"] = nc.dram_tensor("outT", [SPC, KC, 128, S], F32,
                                    kind="ExternalOutput").ap()

    xT = asb(c, "xT_sb", [128, KC, S], F32)[:]
    c.bufs["dummy"] = asb(c, "dummy", [128, 4], F32)[:]
    for nm, shp, dt in [("ident", (128, 128), F32), ("identb", (128, 128), BF16),
                        ("ones_f", (128, 128), F32), ("ones_b", (128, 128), BF16),
                        ("onerow_b", (1, 128), BF16), ("gamma", (128, KC), F32),
                        ("gbias", (E0, 1), F32), ("rz", (128, 1), F32),
                        ("bo1", (128, KC), F32),
                        ("alibiC", (128, HEADS, S // 128), F32),
                        ("slp2", (65, 2, NHP), F32), ("blk2", (128, 65), BF16)]:
        c.bufs[nm] = asb(c, "c_" + nm, list(shp), dt)[:]
        dma(c, "sync", c.bufs[nm], c.dram[nm], "ld")
    c.bufs["onerow_f"] = asb(c, "c_onerowf", [1, 128], F32)[:]
    dma(c, "sync", c.bufs["onerow_f"], c.dram["onerow_f"], "ld")
    c.bufs["ones33"] = asb(c, "c_ones33", [33, 128], F32)[:]
    dma(c, "sync", c.bufs["ones33"], c.dram["ones33"], "ld")
    c.dram["mls_bounce"] = nc.dram_tensor("mls_bounce", [16, 128], F32,
                                          kind="Internal").ap()
    c.wcache = {}
    c.wc_written = {}
    for fam, n, cols in [("eg", E0, 4096), ("eu", E0, 4096), ("ed", E0, 4096),
                         ("sh", 3, 8192), ("expw", E1, 8192),
                         ("qkv0", NHP, 3072), ("qkv1", NHP, 3072),
                         ("wo0", KC, 1024), ("wo1", KC, 1024)]:
        c.wcache[fam] = nc.dram_tensor("wc_" + fam, [n, 128, cols], BF16,
                                       kind="Internal").ap()
        c.wc_written[fam] = {}
    c.dram["beta_bounce"] = nc.dram_tensor("beta_bounce", [NTC, 128], F32,
                                           kind="Internal").ap()
    cb = asb(c, "c_eps", [128, 8], F32)[:]
    c.bufs["gp1"] = asb(c, "c_gp1", [128, KC], F32)[:]
    vals = {"eps_ln": (0, 1e-5), "eps_rms": (1, 1e-12), "one": (4, 1.0),
            "neg_big": (5, NEG_BIG), "zero": (6, 0.0), "c32": (7, 32.0)}
    c.const_cols = {k: cb[:, i:i + 1] for k, (i, v) in vals.items()}
    for k, (i, v) in vals.items():
        c.emit("vector", lambda eng, ap=cb[:, i:i + 1], vv=v: eng.memset(ap, vv))
    c.emit("vector", lambda eng, s=c.sem("dv"): eng.engine_nop().then_inc(s, 1))
    c.inc("dv")
    ld_consts = c.counts["ld"]
    dv0 = c.counts["dv"]
    for e in ENGS[:4]:
        c.wait(e, "ld", ld_consts)
        c.wait(e, "dv", dv0)
    a_gp = v_op(c, lambda eng, o=c.bufs["gp1"], i=c.bufs["gamma"]:
                eng.tensor_scalar(o, i, 32.0, 32.0, op0=ALU.mult, op1=ALU.add))
    for e in ENGS[:4]:
        c.wait(e, "dv", a_gp)

    for seq in range(SPC):
        lx = None
        c.acquire("sync", "xTbuf")
        for kc in range(KC):
            lx = dma(c, "sync", xT[:, kc, :], c.dram["xT"][seq, kc], "ldx")
        for e in ("tensor", "vector", "scalar"):
            c.wait(e, "ldx", lx)
        c.produce("xT", "ldx", lx)
        for part in parts:
            if part == "ffn0":
                for half in range(S // HALF):
                    emit_ffn0(c, xT, half)
            elif part == "attn0":
                emit_attn(c, xT, layer=0)
            elif part == "ffn1":
                for half in range(S // HALF):
                    emit_ffn1(c, xT, half)
            elif part == "attn1":
                emit_attn(c, xT, layer=1)
            else:
                raise ValueError(part)
        c.wait_dep("sync", c.ready.get("xT"))
        sx = None
        for kc in range(KC):
            sx = dma(c, "sync", c.dram["outT"][seq, kc], xT[:, kc, :], "sto")
        c.release("xTbuf", "sto", sx)
    c.wait("sync", "sto", c.counts["sto"])

    with nc.Block() as block:
        @block.tensor
        def _(eng):
            for fn in c.prog["tensor"]:
                fn(eng)

        @block.vector
        def _(eng):
            for fn in c.prog["vector"]:
                fn(eng)

        @block.scalar
        def _(eng):
            for fn in c.prog["scalar"]:
                fn(eng)

        @block.gpsimd
        def _(eng):
            for fn in c.prog["gpsimd"]:
                fn(eng)

        @block.sync
        def _(eng):
            for fn in c.prog["sync"]:
                fn(eng)
    es.close()
    return nc



_PARTS_ALL = ["ffn0", "attn0", "ffn0", "ffn1", "attn1", "ffn1"]


def run_parts(inputs, parts):
    h, percore = prep_host(inputs)
    nc = build_program(parts)
    names = set(DSHAPES_F32) | set(DSHAPES_BF16) | {"xT"}
    in_maps = []
    for cid in range(NCORE):
        m = {k: v for k, v in h.items() if k in names}
        m["xT"] = percore[cid]["xT"]
        in_maps.append(m)
    res = run_bass_kernel_spmd(nc, in_maps, core_ids=list(range(NCORE)))
    out = np.empty((BATCH, S, D), dtype=np.float32)
    for cid in range(NCORE):
        ot = res.results[cid]["outT"].reshape(SPC, D, S)
        out[cid * SPC:(cid + 1) * SPC] = ot.transpose(0, 2, 1)
    return out


def kernel(**inputs):
    return run_parts(inputs, _PARTS_ALL)
